# version 34
# baseline (speedup 1.0000x reference)
import contextlib
import numpy as np
import ml_dtypes
import concourse.bass as bass
import concourse.mybir as mybir
from concourse.bass_utils import run_bass_kernel_spmd

F32 = mybir.dt.float32
BF16 = mybir.dt.bfloat16
U32 = mybir.dt.uint32
AF = mybir.ActivationFunctionType
ALU = mybir.AluOpType
AX = mybir.AxisListType

P = 128
T = 2048
D = 1024
NB = D // P



class Sem:
    def __init__(self, nc, es, name):
        self.h = es.enter_context(nc.semaphore(name))
        self.name = name


class Seq:

    def __init__(self, sem, autosync):
        self.raw = None
        self.sem = sem
        self.cnt = 0
        self.autosync = autosync
        self.waited = {}

    def wait(self, sem, val):
        if val <= 0:
            return
        if self.waited.get(sem.name, -1) >= val:
            return
        self.raw.wait_ge(sem.h, val)
        self.waited[sem.name] = val

    def dep(self, deps):
        for d in deps:
            if isinstance(d, Tok):
                i = d.sem._pool_i
                self.wait(d.sem, d.val)
                d.sem._pool_q.consumers[i].append((self, self.cnt + 1))
            else:
                s, v = d
                self.wait(s.sem if isinstance(s, (Seq, Q)) else s, v)

    def op(self, name, *a, deps=(), **kw):
        self.dep(deps)
        if self.autosync and self.cnt:
            self.wait(self.sem, self.cnt)
        inst = getattr(self.raw, name)(*a, **kw)
        inst.then_inc(self.sem.h, 1)
        self.cnt += 1
        return self.cnt

    def mm(self, out, lhsT, rhs, start, stop, deps=(), inc=False, **kw):
        self.dep(deps)
        inst = self.raw.matmul(out, lhsT, rhs, start=start, stop=stop, **kw)
        if inc:
            inst.then_inc(self.sem.h, 1)
            self.cnt += 1
        return self.cnt


class Tok:

    def __init__(self, sem, val):
        self.sem = sem
        self.val = val


class Q:

    def __init__(self, nc, es, name, raw_get, npool, naux=0, npers=0):
        self.raw = None
        self._raw_get = raw_get
        n = npool + naux + npers
        self.sems = [Sem(nc, es, f"{name}{i}") for i in range(n)]
        self.tot = [0] * n
        self.consumers = [[] for _ in range(n)]
        self.idx = 0
        self.npool = npool
        self.aux_lo = npool
        self.aux_n = naux
        self.aux_idx = 0
        self.pers_lo = npool + naux
        self.pers_n = npers
        self.pers_idx = 0
        self.waited = {}
        self.cur_batch = None

    def wait(self, sem, val):
        if val <= 0:
            return
        if self.waited.get(sem.name, -1) >= val:
            return
        self.raw.wait_ge(sem.h, val)
        self.waited[sem.name] = val

    def dep(self, deps):
        for d in deps:
            if isinstance(d, Tok):
                i = d.sem._pool_i
                self.wait(d.sem, d.val)
                d.sem._pool_q.consumers[i].append((self, None))
            else:
                s, v = d
                self.wait(s.sem if isinstance(s, (Seq, Q)) else s, v)

    def open_batch(self, pool="main"):
        self.cur_batch = self._next_tok(pool)
        return self.cur_batch

    def close_batch(self):
        t = self.cur_batch
        self.cur_batch = None
        return t

    def _next_tok(self, pool="main"):
        if pool == "main":
            i = self.idx % self.npool
            self.idx += 1
        elif pool == "aux":
            i = self.aux_lo + self.aux_idx % self.aux_n
            self.aux_idx += 1
        else:
            assert self.pers_idx < self.pers_n, "persist sems exhausted"
            i = self.pers_lo + self.pers_idx
            self.pers_idx += 1
        sem = self.sems[i]
        sem._pool_i = i
        sem._pool_q = self
        gate = self.consumers[i]
        self.consumers[i] = []
        for c in gate:
            if isinstance(c[0], Seq):
                self.wait(c[0].sem, c[1])
            elif isinstance(c[0], Q):
                pass
        return Tok(sem, self.tot[i])

    def dma(self, out, in_, deps=(), batch=None, pool="main", **kw):
        self.dep(deps)
        if batch is None and self.cur_batch is not None:
            batch = self.cur_batch
        tok = batch if batch is not None else self._next_tok(pool)
        i = tok.sem._pool_i
        self.raw.dma_start(out=out, in_=in_, **kw).then_inc(tok.sem.h, 16)
        self.tot[i] += 16
        tok.val = self.tot[i]
        return tok


class Ctx:
    def __init__(self, nc, es, taps=()):
        self.nc = nc
        self.es = es
        self.taps = set(taps)
        self.tap_list = []
        self.s_te = Sem(nc, es, "s_te")
        self.s_ve = Sem(nc, es, "s_ve")
        self.s_se = Sem(nc, es, "s_se")
        self.s_gc = Sem(nc, es, "s_gc")
        self.te = Seq(self.s_te, autosync=False)
        self.ve = Seq(self.s_ve, autosync=True)
        self.se = Seq(self.s_se, autosync=True)
        self.gc = Seq(self.s_gc, autosync=True)
        self.qs = Q(nc, es, "qs", None, 6, naux=0, npers=6)
        self.qg = Q(nc, es, "qg", None, 10, naux=20, npers=12)

    def bind(self):
        nc = self.nc
        self.te.raw = nc.tensor
        self.ve.raw = nc.vector
        self.se.raw = nc.scalar
        self.gc.raw = nc.gpsimd
        self.qs.raw = nc.sync
        self.qg.raw = nc.gpsimd

    def sb(self, name, shape, dtype):
        return self.es.enter_context(self.nc.sbuf_tensor(name, shape, dtype))

    def tap(self, name, src_ap, deps, shape, rearr=None):
        if name not in self.taps:
            return
        t = self.nc.dram_tensor("tap_" + name, shape, F32,
                                kind="ExternalOutput").ap()
        dst = t if rearr is None else t.rearrange(*rearr[0], **rearr[1])
        self.qg.dma(dst, src_ap, deps=deps)
        self.tap_list.append(name)


class Arena:

    def __init__(self, ctx, name, cols):
        self.t = ctx.sb(name, [P, cols], BF16)
        self.cols = cols
        self.off = 0
        self.marks = []

    def push(self):
        self.marks.append(self.off)

    def pop(self):
        self.off = self.marks.pop()

    def alloc(self, cols):
        cols = (cols + 31) // 32 * 32
        assert self.off + cols <= self.cols, \
            f"arena overflow: {self.off}+{cols} > {self.cols}"
        v = self.t[:, self.off:self.off + cols]
        self.off += cols
        return v

    def alloc3(self, nb, t):
        return self.alloc(nb * t).rearrange("p (b t) -> p b t", b=nb)


def barrier(ctx):
    engs = [ctx.te, ctx.ve, ctx.se, ctx.gc]
    qs = [ctx.qs, ctx.qg]
    cnts = [(e.sem, e.cnt) for e in engs]
    for q in qs:
        for sem, tot in zip(q.sems, q.tot):
            if tot > 0:
                cnts.append((sem, tot))
    for e in engs + qs:
        for sem, v in cnts:
            if getattr(e, "sem", None) is not sem:
                e.wait(sem, v)



def emit_consts(ctx):
    c = {}
    c["ones1b"] = ctx.sb("c_ones1b", [1, P], BF16)
    c["ones1f"] = ctx.sb("c_ones1f", [1, P], F32)
    c["onescolb"] = ctx.sb("c_onescolb", [P, 1], BF16)
    c["identb"] = ctx.sb("c_identb", [P, P], BF16)
    c["cpool"] = ctx.sb("c_cpool", [P, 8], F32)
    ve, gc = ctx.ve, ctx.gc
    ve.op("memset", c["ones1b"][:], 1.0)
    ve.op("memset", c["ones1f"][:], 1.0)
    ve.op("memset", c["onescolb"][:], 1.0)
    ve.op("memset", c["identb"][:], 1.0)
    ve.op("memset", c["cpool"][:, 0:1], 1e-5)
    ve.op("memset", c["cpool"][:, 1:2], 1e-6)
    ve.op("memset", c["cpool"][:, 2:3], 0.125)
    ve.op("memset", c["cpool"][:, 3:4], 1.0)
    ve.op("memset", c["cpool"][:, 4:5], 0.0)
    ve.op("memset", c["cpool"][:, 5:6], -1.0)
    mv = ve.cnt
    gc.op("affine_select", c["identb"][:], c["identb"][:], [[1, P]],
          ALU.is_equal, 0.0, base=0, channel_multiplier=-1,
          deps=[(ctx.ve, mv)])
    c["ready_gc"] = gc.cnt
    c["ready_ve"] = ve.cnt
    ctx.consts = c
    return c



def emit_ln(ctx, src3d, src_is_f32, dst3d, scr, gamma=None, beta=None,
            src_dep=()):
    te, ve, se = ctx.te, ctx.ve, ctx.se
    c = ctx.consts
    PS = ctx.PS
    src_dep = list(src_dep) + [(ctx.se, ctx.se.cnt), (ctx.ve, ctx.ve.cnt)]

    if src_is_f32:
        cast = scr["cast3d"]
        for b in range(NB):
            ve.op("tensor_copy", cast[:, b, :], src3d[:, b, :], deps=src_dep)
        xb3 = cast
    else:
        xb3 = src3d
    vcast = ve.cnt

    for tc in range(4):
        for b in range(NB):
            te.mm(PS[0:1, 4 + tc, :], c["onescolb"][:],
                  xb3[:, b, 512 * tc:512 * (tc + 1)],
                  start=(b == 0), stop=(b == NB - 1),
                  deps=([(ve, vcast)] + src_dep + [(ve, c["ready_ve"])]),
                  inc=(b == NB - 1))
    sq = scr["sq"]
    for b in range(NB):
        se.op("activation", sq[:, :], xb3[:, b, :], AF.Square,
              deps=[(ve, vcast), (te, te.cnt)] + src_dep)
        scnt = se.cnt
        for tc in range(4):
            te.mm(PS[32:33, 4 + tc, :], c["onescolb"][:],
                  sq[:, 512 * tc:512 * (tc + 1)],
                  start=(b == 0), stop=(b == NB - 1),
                  deps=[(se, scnt)], inc=(tc == 3))
    tsums = te.cnt

    rows = scr["rows"]
    r0 = rows[0:1, 0:T]
    r1 = rows[0:1, T:2 * T]
    r2 = rows[32:33, 0:T]
    r3 = rows[32:33, T:2 * T]
    sum_row = PS[0:1, 4:8, :].rearrange("p a b -> p (a b)")
    sq_row = PS[32:33, 4:8, :].rearrange("p a b -> p (a b)")
    ve.op("tensor_scalar_mul", r0, sum_row, 1.0 / D, deps=[(te, tsums)])
    ve.op("tensor_scalar_mul", r2, sq_row, 1.0 / D, deps=[(te, tsums)])
    ve.op("tensor_mul", r3, r0, r0)
    ve.op("tensor_sub", r2, r2, r3)
    vvar = ve.cnt
    se.op("activation", r3, r2, AF.Sqrt, bias=c["cpool"][0:1, 0:1],
          deps=[(ve, vvar)])
    ssq = se.cnt
    ve.op("reciprocal", r1, r3, deps=[(se, ssq)])
    vrows = ve.cnt

    mub, rsb = scr["mub"], scr["rsb"]
    for what, dstb in ((r0, scr["mub"]), (r1, scr["rsb"])):
        dep = [(ve, ve.cnt)]
        for tc in range(4):
            te.mm(PS[:, tc, :], c["ones1b"][:],
                  what[0:1, 512 * tc:512 * (tc + 1)],
                  start=True, stop=True, deps=dep + [(ve, vrows)], inc=True)
        tb = te.cnt
        for tc in range(4):
            ve.op("tensor_copy", dstb[:, 512 * tc:512 * (tc + 1)],
                  PS[:, tc, :], deps=[(te, tb)])
    vbc = ve.cnt

    for b in range(NB):
        ve.op("tensor_sub", dst3d[:, b, :], xb3[:, b, :], mub[:, :],
              deps=[(ve, vbc)] + src_dep)
        ve.op("tensor_mul", dst3d[:, b, :], dst3d[:, b, :], rsb[:, :])
        if gamma is not None:
            ve.op("tensor_scalar", dst3d[:, b, :], dst3d[:, b, :],
                  gamma[:, b:b + 1], beta[:, b:b + 1],
                  ALU.mult, ALU.add)
    return ve.cnt



def make_ln_scratch(ctx):
    return {
        "sq": ctx.sb("ln_sq", [P, T], BF16),
        "rows": ctx.sb("ln_rows", [33, 2 * T], BF16),
        "mub": ctx.sb("ln_mub", [P, T], BF16),
        "rsb": ctx.sb("ln_rsb", [P, T], BF16),
    }


DR_SHAPES = {
    "xT": [D, T],
    "w_qT": [D, D], "w_kT": [D, D], "w_vT": [D, D],
    "bq_a": [P, NB], "bk_a": [P, NB], "bv_a": [D, 1],
    "w_oT": [D, D], "b_o": [P, NB],
    "ln1_g": [P, NB], "ln1_b": [P, NB], "ln2_g": [P, NB], "ln2_b": [P, NB],
    "wq0": [D, D], "bq0": [P, NB], "wk0": [D, D], "bk0": [P, NB],
    "wv0": [D, D], "bv0": [D, 1], "wp0": [D, D], "bp0": [P, NB],
    "gate_w0": [D, 8], "gate_b0": [1, 8],
    "exp_w1": [8, D, 4 * D], "exp_b1": [8, P, 32],
    "exp_w2": [8, 4 * D, D], "exp_b2": [8, D],
    "wq1": [D, D], "wk1": [D, D], "wv1": [D, D], "wb1": [D, 4],
    "conv_q": [P, NB, 4], "conv_k": [P, NB, 4], "conv_v": [P, NB, 4],
    "fir_long": [P, NB, 64], "fir_short": [P, NB, 5],
    "g_w1": [D + 16, D], "g_b1": [P, NB], "g_w2": [D, 4], "g_b2": [4, 1],
    "log_temp": [1, 4], "base_bias": [4, 4], "crl": [1, 4], "rms_w": [P, 2],
    "wo1": [D, D], "gate_w1": [D, 8],
    "m_w1": [8, D, D], "m_b1": [8, P, 8], "m_w2": [8, D, D], "m_b2": [8, D],
}

L0_NAMES = ["w_qT", "w_kT", "w_vT", "bq_a", "bk_a", "bv_a",
            "w_oT", "b_o", "ln1_g", "ln1_b", "ln2_g",
            "ln2_b", "wq0", "bq0", "wk0", "bk0", "wv0", "bv0", "wp0", "bp0",
            "gate_w0", "gate_b0", "exp_w1", "exp_b1", "exp_w2", "exp_b2"]
L1_NAMES = ["wq1", "wk1", "wv1", "wb1", "conv_q", "conv_k", "conv_v",
            "fir_long", "fir_short", "g_w1", "g_b1", "g_w2", "g_b2",
            "log_temp", "base_bias", "crl", "rms_w", "wo1", "gate_w1",
            "m_w1", "m_b1", "m_w2", "m_b2"]


def build_program(layers=(0, 1), taps=()):
    nc = bass.Bass("TRN2", target_bir_lowering=False, debug=False)
    dr = {"xT": nc.dram_tensor("xT", DR_SHAPES["xT"], F32,
                               kind="ExternalInput").ap()}
    use = (L0_NAMES if 0 in layers else []) + (L1_NAMES if 1 in layers else [])
    for nm in use:
        dr[nm] = nc.dram_tensor(nm, DR_SHAPES[nm], F32,
                                kind="ExternalInput").ap()
    out = nc.dram_tensor("outT", [D, T], F32, kind="ExternalOutput").ap()

    es = contextlib.ExitStack()
    with es:
        block = es.enter_context(nc.Block())
        ctx = Ctx(nc, es, taps)
        ctx.dr = dr
        ctx.PS = es.enter_context(nc.psum_tensor("PS", [P, 8, 512], F32))

        RES = ctx.sb("RES", [P, NB, T], F32)
        SCR_A = ctx.sb("SCR_A", [P, NB, T], BF16)
        ctx.RES, ctx.SCR_A = RES, SCR_A
        ctx.arena = Arena(ctx, "ARENA", 41088)
        assert 41088 >= 41056 + 16
        ctx.fcol = ctx.sb("fcol", [P, 16], F32)
        ctx.rowf = ctx.sb("rowf", [1, 512], F32)
        ctx.S0 = ctx.sb("S0", [P, 2, 256], F32)
        ctx.rowbuf = ctx.sb("rowbuf", [1, D], BF16)
        ctx.cmcolb = ctx.sb("cmcolb", [P, 2], BF16)

        @block.tensor
        def _(tensor):
            ctx.bind()
            lp = ctx.es.enter_context(nc.allow_low_precision(
                reason="bf16 stat rows within stated tolerance"))
            emit_consts(ctx)
            ln_scr = make_ln_scratch(ctx)
            ctx.ln_scr = ln_scr

            xld = ctx.qs.dma(RES[:, :, :],
                             dr["xT"].rearrange("(b p) t -> p b t", p=P),
                             pool="persist")
            ctx.xld = xld

            if 0 in layers:
                ctx.arena.push()
                ln_scr["cast3d"] = ctx.arena.alloc3(NB, T)
                v0 = emit_ln(ctx, RES, True, SCR_A, ln_scr, src_dep=[xld])
                ctx.arena.pop()
                ctx.tap("xn0", SCR_A[:, :, :], [(ctx.ve, v0)], [D, T],
                        rearr=(("(b p) t -> p b t",), {"p": P}))
                emit_layer0(ctx, ln_scr, v0)

            if 1 in layers:
                emit_layer1(ctx, ln_scr)

            barrier(ctx)
            st_tok = ctx.qs.dma(out.rearrange("(b p) t -> p b t", p=P),
                                RES[:, :, :])
            ctx.qs.wait(st_tok.sem, st_tok.val)
            ctx.gc.dep([st_tok])

    return nc, dr


def emit_layer0(ctx, ln_scr, v_xn0):
    te, ve, se, gc, qs, qg = ctx.te, ctx.ve, ctx.se, ctx.gc, ctx.qs, ctx.qg
    c = ctx.consts
    dr = ctx.dr
    ar = ctx.arena
    xn0, RES = ctx.SCR_A, ctx.RES

    SM = ctx.sb("SM0", [P, 96], F32)
    smap = {}
    qs.open_batch(pool="persist")
    for i, nm in enumerate(["ln1_g", "ln1_b", "ln2_g", "ln2_b", "bq0", "bk0",
                            "bp0", "b_o", "bq_a", "bk_a"]):
        smap[nm] = SM[:, 8 * i:8 * (i + 1)]
        sm_tok = qs.dma(smap[nm], dr[nm])
    qs.close_batch()
    rowbuf = ctx.rowbuf
    rb_tok = qg.dma(rowbuf[:, :], dr["bv0"].rearrange("d o -> o d"),
                    pool="persist")

    ar.push()

    h2 = ar.alloc3(NB, T)
    v_h2 = emit_ln(ctx, xn0, False, h2, ln_scr, gamma=smap["ln1_g"],
                   beta=smap["ln1_b"],
                   src_dep=[(ve, v_xn0), sm_tok])
    ctx.tap("h2", h2[:, :, :], [(ve, v_h2)], [D, T],
            rearr=(("(b p) t -> p b t",), {"p": P}))

    ACC = h2

    def evac_inner(ps_ap, mb, tc, tecnt):
        ve.op("scalar_tensor_tensor", ACC[:, mb, 512 * tc:512 * (tc + 1)],
              ps_ap, smap["bp0"][:, mb:mb + 1],
              xn0[:, mb, 512 * tc:512 * (tc + 1)],
              ALU.add, ALU.add, deps=[(te, tecnt), (ve, v_xn0)])

    w_in = {"wq": dr["wq0"], "wk": dr["wk0"], "wv": dr["wv0"],
            "wo": dr["wp0"], "bqf": smap["bq0"], "bkf": smap["bk0"],
            "bvr": rowbuf[:, :]}
    r = emit_mha(ctx, h2, [(ve, v_h2), rb_tok, sm_tok], w_in,
                 causal=True, evac_out=evac_inner, name="att0i",
                 spill_o=True)
    v_acc = r["ve"]
    ctx.tap("acc", ACC[:, :, :], [(ve, v_acc)], [D, T],
            rearr=(("(b p) t -> p b t",), {"p": P}))

    for b in range(NB):
        ve.op("tensor_add", RES[:, b, :], RES[:, b, :], ACC[:, b, :],
              deps=[(ve, v_acc)])
    v_res1 = ve.cnt

    moein = ACC
    v_moein = emit_ln(ctx, ACC, False, moein, ln_scr, gamma=smap["ln2_g"],
                      beta=smap["ln2_b"], src_dep=[(ve, v_res1)])
    ctx.tap("moein", moein[:, :, :], [(ve, v_moein)], [D, T],
            rearr=(("(b p) t -> p b t",), {"p": P}))

    rb_tok = qg.dma(rowbuf[:, :], dr["bv_a"].rearrange("d o -> o d"),
                    deps=[(ctx.te, r["te"])], pool="persist")
    def evac_outer(ps_ap, mb, tc, tecnt):
        ve.op("scalar_tensor_tensor", RES[:, mb, 512 * tc:512 * (tc + 1)],
              ps_ap, smap["b_o"][:, mb:mb + 1],
              RES[:, mb, 512 * tc:512 * (tc + 1)],
              ALU.add, ALU.add, deps=[(te, tecnt), (ve, v_res1)])

    w_out = {"wq": dr["w_qT"], "wk": dr["w_kT"], "wv": dr["w_vT"],
             "wo": dr["w_oT"], "bqf": smap["bq_a"], "bkf": smap["bk_a"],
             "bvr": rowbuf[:, :]}
    r = emit_mha(ctx, xn0, [(ve, v_xn0), rb_tok, sm_tok], w_out,
                 causal=False, evac_out=evac_outer, name="att0o",
                 spill_o=True)
    v_res2 = r["ve"]
    ctx.tap("res_preMoE", RES[:, :, :], [(ve, v_res2)], [D, T],
            rearr=(("(b p) t -> p b t",), {"p": P}))

    w1bufs = [ctx.SCR_A[:, :, 0:1024], ctx.SCR_A[:, :, 1024:2048]]
    emit_moe(ctx, moein, [(ve, v_moein)], dr["exp_w1"], dr["exp_b1"],
             dr["exp_w2"], dr["exp_b2"], dr["gate_w0"], dr["gate_b0"],
             H=4 * D, softmax_then_topk=False, w1bufs=w1bufs,
             res_dep=v_res2, name="moe0")
    ar.pop()


def emit_layer1(ctx, ln_scr):
    te, ve, se, gc, qs, qg = ctx.te, ctx.ve, ctx.se, ctx.gc, ctx.qs, ctx.qg
    c = ctx.consts
    dr = ctx.dr
    PS = ctx.PS
    ar = ctx.arena
    xn1, RES = ctx.SCR_A, ctx.RES
    rows = ln_scr["rows"]
    sq = ln_scr["sq"]
    mub = ln_scr["mub"]
    rsb = ln_scr["rsb"]

    ar.push()
    mskA = ar.alloc(2 * P)
    mskL = ar.alloc(P)
    self_j = ar.alloc(4 * P)
    beta_r = ar.alloc(T)
    wsl_flat = ar.alloc(2048)
    wsl = wsl_flat.rearrange("p (b f) -> p b f", b=NB)
    FIRL = ar.alloc(NB * 64).rearrange("p (b k) -> p b k", b=NB)
    stm = ar.alloc(64)
    st16 = ar.alloc(P)
    idt4 = ar.alloc(32)
    W2S = ar.alloc(NB * 4).rearrange("p (b k) -> p b k", b=NB)
    ve.op("memset", mskA[:, 0:P], -1.0)
    ve.op("memset", mskA[:, P:2 * P], -1.0)
    ve.op("memset", mskL[:, :], 1.0)
    ve.op("memset", self_j[:, :], 1.0)
    mv = ve.cnt
    gc.op("affine_select", mskA[:, 0:P], mskA[:, 0:P], [[-1, P]], ALU.is_gt,
          0.0, base=0, channel_multiplier=1, deps=[(ve, mv)])
    gc.op("affine_select", mskA[:, P:2 * P], mskA[:, P:2 * P], [[1, P]],
          ALU.is_gt, 0.0, base=0, channel_multiplier=-1)
    gc.op("affine_select", mskL[:, :], mskL[:, :], [[1, P]], ALU.is_ge,
          0.0, base=0, channel_multiplier=-1)
    for j in range(4):
        gc.op("affine_select", self_j[0:4, P * j:P * (j + 1)],
              self_j[0:4, P * j:P * (j + 1)], [[0, P]], ALU.is_equal, 0.0,
              base=-j, channel_multiplier=1)
    g_msk = gc.cnt
    w2s_tok = qg.dma(W2S[:, :, :],
                     dr["g_w2"].rearrange("(b p) f -> p b f", p=P),
                     pool="persist")

    firl_tok = qg.dma(FIRL[:, :, :], dr["fir_long"], pool="persist")

    SM = ctx.sb("SM1", [P, 16], F32)
    sm_tok = qs.open_batch(pool="persist")
    qs.dma(SM[:, 0:8], dr["g_b1"])
    qs.dma(SM[:, 8:10], dr["rms_w"])
    CW = ctx.sb("l1_cw", [P, NB, 12], F32)
    qs.dma(CW[:, :, 0:4], dr["conv_q"])
    qs.dma(CW[:, :, 4:8], dr["conv_k"])
    qs.dma(CW[:, :, 8:12], dr["conv_v"])
    CWB = ctx.sb("l1_cwb", [P, NB, 12], BF16)
    qg.open_batch(pool="persist")
    qg.dma(CWB[:, :, 0:4], dr["conv_q"])
    qg.dma(CWB[:, :, 4:8], dr["conv_k"])
    cwb_tok = qg.dma(CWB[:, :, 8:12], dr["conv_v"])
    qg.close_batch()
    FIRS = ctx.sb("l1_firs", [P, NB, 5], F32)
    qs.dma(FIRS[:, :, :], dr["fir_short"])
    SR = ctx.sb("l1_sr", [4, 24], F32)
    qs.dma(SR[0:1, 0:4], dr["log_temp"])
    qs.dma(SR[0:1, 4:8], dr["crl"])
    BB = ctx.sb("l1_bb", [4, 8], F32)
    qs.dma(BB[0:4, 0:4], dr["base_bias"])
    qs.dma(BB[0:4, 4:5], dr["g_b2"])
    sm_tok = qs.close_batch()

    ar.push()
    ln_scr["cast3d"] = ar.alloc3(NB, T)
    v_res = ve.cnt
    v_xn1 = emit_ln(ctx, RES, True, xn1, ln_scr,
                    src_dep=[(ve, v_res), ctx.xld])
    ar.pop()
    ctx.tap("xn1", xn1[:, :, :], [(ve, v_xn1)], [D, T],
            rearr=(("(b p) t -> p b t",), {"p": P}))

    wbs = ar.alloc3(NB, 4)
    wb_ld = qg.dma(wbs, dr["wb1"].rearrange("(b p) f -> p b f", p=P),
                   pool="persist", deps=[(ctx.ve, v_xn1), (ctx.se, se.cnt)])

    emit_l1_heads(ctx, xn1, v_xn1, beta_r, (wbs, wb_ld), wsl,
                  dict(sm=sm_tok, SM=SM, CW=CW, FIRS=FIRS, FIRL=FIRL, SR=SR,
                       BB=BB, mskA=mskA, mskL=mskL, self_j=self_j,
                       g_msk=g_msk, wsl_flat=wsl_flat, firl=firl_tok,
                       idt4=idt4, W2S=W2S, w2s_tok=w2s_tok, stm=stm,
                       st16=st16, CWB=CWB, cwb_tok=cwb_tok))
    v_res2 = ctx.ve.cnt
    ar.pop()
    ctx.tap("res_preMoE1", RES[:, :, :], [(ve, v_res2)], [D, T],
            rearr=(("(b p) t -> p b t",), {"p": P}))

    ar.push()
    w1bufs = [ar.alloc3(NB, 1024), ar.alloc3(NB, 1024)]
    emit_moe(ctx, xn1, [(ve, v_xn1)], dr["m_w1"], dr["m_b1"],
             dr["m_w2"], dr["m_b2"], dr["gate_w1"], None,
             H=D, softmax_then_topk=True, w1bufs=w1bufs,
             res_dep=v_res2, name="moe1")
    ar.pop()


def emit_l1_heads(ctx, xn1, v_xn1, beta_r, wb, wsl, aux):
    wbs, wb_ld = wb
    te, ve, se, gc, qs, qg = ctx.te, ctx.ve, ctx.se, ctx.gc, ctx.qs, ctx.qg
    RES = ctx.RES
    c = ctx.consts
    dr = ctx.dr
    PS = ctx.PS
    ar = ctx.arena
    rows = ctx.ln_scr["rows"]
    sq = ctx.ln_scr["sq"]
    mub = ctx.ln_scr["mub"]
    rsb = ctx.ln_scr["rsb"]
    SM, CW, FIRS, FIRL = aux["SM"], aux["CW"], aux["FIRS"], aux["FIRL"]
    SR, BB = aux["SR"], aux["BB"]
    mskA, mskL, self_j = aux["mskA"], aux["mskL"], aux["self_j"]
    sm_tok = aux["sm"]
    NCH = 16

    S0 = ctx.S0

    idt4 = aux["idt4"][0:4, 0:4]
    W2S = aux["W2S"]
    w2s_tok = aux["w2s_tok"]
    FLOOR = ctx.sb("l1_floor", [P, 16], F32)
    stm = aux["stm"]
    st16 = aux["st16"][0:16, :]
    CWB = aux["CWB"]
    cwb_tok = aux["cwb_tok"]
    st16_war = []
    for jj in range(4):
        ve.op("memset", FLOOR[:, 4 * jj:4 * jj + 1], 0.05)
        ve.op("memset", FLOOR[:, 4 * jj + 1:4 * jj + 2], 0.05)
        ve.op("memset", FLOOR[:, 4 * jj + 2:4 * jj + 4], 0.0)
    v_floor = ve.cnt

    se.op("activation", SR[0:1, 8:12], SR[0:1, 0:4], AF.Abs, deps=[sm_tok])
    se.op("activation", SR[0:1, 8:12], SR[0:1, 8:12], AF.Exp,
          scale=ctx.consts["cpool"][0:1, 5:6])
    se.op("activation", SR[0:1, 8:12], SR[0:1, 8:12], AF.Ln,
          bias=c["cpool"][0:1, 3:4])
    s_sp = se.cnt
    ve.op("tensor_scalar_max", SR[0:1, 16:20], SR[0:1, 0:4], 0.0,
          deps=[sm_tok, (se, s_sp)])
    ve.op("tensor_add", SR[0:1, 16:20], SR[0:1, 16:20], SR[0:1, 8:12])
    ve.op("tensor_scalar_add", SR[0:1, 16:20], SR[0:1, 16:20], 1e-4)
    ve.op("reciprocal", SR[0:1, 8:12], SR[0:1, 16:20])
    v_itemp = ve.cnt
    se.op("activation", SR[0:1, 12:16], SR[0:1, 4:8], AF.Sigmoid,
          deps=[sm_tok, (ve, v_itemp)])
    s_crl = se.cnt
    ve.op("tensor_scalar_mul", SR[0:1, 12:16], SR[0:1, 12:16], 0.5,
          deps=[(se, s_crl)])
    v_crl = ve.cnt

    bank_ve = {b: 0 for b in range(8)}
    bank_se = {b: 0 for b in range(8)}

    for h in range(4):
        ar.push()
        Bv = ar.alloc(2 * (64 + T)).rearrange("p (b t) -> p b t", b=2)
        delta = ar.alloc3(2, T)
        Bq = ar.alloc3(2, T)
        Bk = ar.alloc3(2, T)
        o_h = ar.alloc3(2, T)

        ar.push()
        A_pad = ar.alloc(2 * (64 + T)).rearrange("p (b t) -> p b t", b=2)
        ve.op("memset", A_pad[:, :, 0:64], 0.0)
        ve.op("memset", Bv[:, :, 0:64], 0.0)
        v_pad = ve.cnt
        conv_done = {}
        for ti, (wnm, cw0, Bdst, off) in enumerate(
                (("wq1", 0, Bq, 0), ("wk1", 4, Bk, 0), ("wv1", 8, Bv, 64))):
            w_ld = wload3(ctx, wsl[:, :, 0:256],
                          dr[wnm][:, 256 * h:256 * (h + 1)],
                          deps=[(te, te.cnt)])
            for mb in range(2):
                for kb in range(NB):
                    for tc in range(4):
                        te.mm(PS[:, 4 + tc, :],
                              wsl[:, kb, 128 * mb:128 * (mb + 1)],
                              xn1[:, kb, 512 * tc:512 * (tc + 1)],
                              start=(kb == 0), stop=(kb == NB - 1),
                              deps=[w_ld, (ve, v_xn1), (ve, bank_ve[4 + tc]),
                                    (se, bank_se[4 + tc])],
                              inc=(kb == NB - 1 and tc == 3))
                tcnt = te.cnt
                for tc in range(4):
                    ve.op("tensor_copy",
                          A_pad[:, mb, 64 + 512 * tc:64 + 512 * (tc + 1)],
                          PS[:, 4 + tc, :], deps=[(te, tcnt), (ve, v_pad)])
                    bank_ve[4 + tc] = ve.cnt
            dt4 = [sq[:, 128 * jj:128 * (jj + 1)] for jj in range(8)]
            gdeps = [(se, se.cnt), (ve, ve.cnt), cwb_tok, (te, te.cnt)]
            for mb in range(2):
                gb = 2 * h + mb
                for j in range(4):
                    gc.op("affine_select", dt4[4 * mb + j],
                          CWB[:, gb, cw0 + j:cw0 + j + 1].broadcast_to([P, P]),
                          [[1, P]], ALU.is_equal, 0.0, base=0,
                          channel_multiplier=-1, deps=gdeps)
            gcv = gc.cnt
            for mb in range(2):
                for j in range(4):
                    for tc in range(4):
                        te.mm(PS[:, 4 + tc, :], dt4[4 * mb + j],
                              A_pad[:, mb, 61 + j + 512 * tc:
                                    61 + j + 512 * (tc + 1)],
                              start=(j == 0), stop=(j == 3),
                              deps=[(gc, gcv), (ve, bank_ve[4 + tc]),
                                    (se, bank_se[4 + tc])],
                              inc=(j == 3 and tc == 3))
                tcnt = te.cnt
                for tc in range(4):
                    se.op("activation",
                          Bdst[:, mb, off + 512 * tc:off + 512 * (tc + 1)],
                          PS[:, 4 + tc, :], AF.Silu, deps=[(te, tcnt)])
                    bank_se[4 + tc] = se.cnt
            conv_done[ti] = se.cnt
            conv_done[ti] = se.cnt
        ar.pop()

        for Bsrc, sdep in ((Bq, conv_done[0]), (Bk, conv_done[1])):
            for mb in range(2):
                se.op("activation", sq[:, :], Bsrc[:, mb, :], AF.Square,
                      deps=[(se, sdep), (te, te.cnt)])
                scnt = se.cnt
                for tc in range(4):
                    te.mm(PS[0:1, 4 + tc, :], c["onescolb"][:, :],
                          sq[:, 512 * tc:512 * (tc + 1)],
                          start=(mb == 0), stop=(mb == 1),
                          deps=[(se, scnt), (ve, bank_ve[4 + tc]),
                                (se, bank_se[4 + tc])],
                          inc=(tc == 3))
            tsum = te.cnt
            se.op("activation", rows[0:1, 0:T],
                  PS[0:1, 4:8, :].rearrange("p a b -> p (a b)"), AF.Sqrt,
                  bias=c["cpool"][0:1, 1:2], deps=[(te, tsum)])
            ssq_ = se.cnt
            for tc in range(4):
                bank_se[4 + tc] = ssq_
            ve.op("reciprocal", rows[0:1, 0:T], rows[0:1, 0:T],
                  deps=[(se, ssq_)])
            vr = ve.cnt
            for tc in range(4):
                te.mm(PS[:, tc, :], c["ones1b"][:, :],
                      rows[0:1, 512 * tc:512 * (tc + 1)],
                      start=True, stop=True,
                      deps=[(ve, vr), (ve, bank_ve[tc])], inc=True)
            tbc = te.cnt
            for tc in range(4):
                ve.op("tensor_copy", mub[:, 512 * tc:512 * (tc + 1)],
                      PS[:, tc, :], deps=[(te, tbc)])
                bank_ve[tc] = ve.cnt
            for mb in range(2):
                ve.op("tensor_mul", Bsrc[:, mb, :], Bsrc[:, mb, :], mub[:, :])
            for tc in range(4):
                bank_ve[4 + tc] = ve.cnt
        v_norm = ve.cnt

        ar.push()
        kb_fm = ar.alloc3(2, T)
        vb_fm = ar.alloc3(2, T)
        Xb = ar.alloc(16 * P).rearrange("p (i q) -> p i q", i=16)
        XTb = ar.alloc(16 * P).rearrange("p (i q) -> p i q", i=16)
        for tc in range(4):
            for kb in range(NB):
                te.mm(PS[0:1, 4 + tc, :], wbs[:, kb, h:h + 1],
                      xn1[:, kb, 512 * tc:512 * (tc + 1)],
                      start=(kb == 0), stop=(kb == NB - 1),
                      deps=[wb_ld, (ve, v_xn1), (se, bank_se[4 + tc]),
                            (ve, bank_ve[4 + tc])],
                      inc=(kb == NB - 1))
        tcnt = te.cnt
        se.op("activation", beta_r[0:1, 0:T],
              PS[0:1, 4:8, :].rearrange("p a b -> p (a b)"), AF.Sigmoid,
              deps=[(te, tcnt)])
        s_beta = se.cnt
        for tc in range(4):
            bank_se[4 + tc] = s_beta
        for tc in range(4):
            te.mm(PS[:, tc, :], c["ones1b"][:, :],
                  beta_r[0:1, 512 * tc:512 * (tc + 1)],
                  start=True, stop=True,
                  deps=[(se, s_beta), (ve, v_norm), (ve, bank_ve[tc])],
                  inc=True)
        tbc = te.cnt
        for tc in range(4):
            ve.op("tensor_copy", mub[:, 512 * tc:512 * (tc + 1)],
                  PS[:, tc, :], deps=[(te, tbc)])
            bank_ve[tc] = ve.cnt
        for mb in range(2):
            ve.op("tensor_mul", kb_fm[:, mb, :], Bk[:, mb, :], mub[:, :],
                  deps=[(se, conv_done[1])])
            ve.op("tensor_mul", vb_fm[:, mb, :], Bv[:, mb, 64:64 + T],
                  mub[:, :], deps=[(se, conv_done[2])])
        v_kbvb = ve.cnt

        ve.op("memset", S0[:, :, :], 0.0, deps=[(te, te.cnt)])
        ve.op("memset", rsb[:, 1536:2048], 0.0)
        v_s = ve.cnt
        bank_ve = {b: max(bank_ve[b], v_s) for b in range(8)}

        Nc = sq[:, 0:512].rearrange("p (i q) -> p i q", i=4)
        NTc = sq[:, 512:1024].rearrange("p (i q) -> p i q", i=4)
        for grp in range(4):
            g0 = 4 * grp
            for ii in range(4):
                i = g0 + ii
                csl = slice(P * i, P * (i + 1))
                for mb in range(2):
                    te.mm(PS[:, 0, P * ii:P * (ii + 1)],
                          kb_fm[:, mb, csl], Bk[:, mb, csl],
                          start=(mb == 0), stop=(mb == 1),
                          deps=[(ve, v_kbvb), (ve, bank_ve[0])],
                          inc=(mb == 1))
                    te.mm(PS[:, 1, P * ii:P * (ii + 1)],
                          Bk[:, mb, csl], kb_fm[:, mb, csl],
                          start=(mb == 0), stop=(mb == 1),
                          deps=[(ve, bank_ve[1])], inc=(mb == 1))
            tA = te.cnt
            ve.op("tensor_mul", Nc[:, :, :], PS[:, 0, :].rearrange(
                "p (i q) -> p i q", i=4),
                mskA[:, 0:P].rearrange("p (o q) -> p o q", o=1
                                       ).broadcast_to([P, 4, P]),
                deps=[(te, tA), (gc, aux["g_msk"])])
            bank_ve[0] = ve.cnt
            ve.op("tensor_mul", NTc[:, :, :], PS[:, 1, :].rearrange(
                "p (i q) -> p i q", i=4),
                mskA[:, P:2 * P].rearrange("p (o q) -> p o q", o=1
                                           ).broadcast_to([P, 4, P]))
            bank_ve[1] = ve.cnt
            ve.op("tensor_add", Xb[:, g0:g0 + 4, :], Nc[:, :, :],
                  c["identb"][:, :].rearrange("p (o q) -> p o q", o=1
                                              ).broadcast_to([P, 4, P]),
                  deps=[(te, te.cnt)])
            ve.op("tensor_add", XTb[:, g0:g0 + 4, :], NTc[:, :, :],
                  c["identb"][:, :].rearrange("p (o q) -> p o q", o=1
                                              ).broadcast_to([P, 4, P]))
            vX = ve.cnt
            for lv in range(6):
                for ii in range(4):
                    te.mm(PS[:, 2, P * ii:P * (ii + 1)], NTc[:, ii, :],
                          Nc[:, ii, :], start=True, stop=True,
                          deps=[(ve, ve.cnt), (se, se.cnt), (ve, bank_ve[2]),
                                (se, bank_se[2])],
                          inc=(ii == 3))
                    te.mm(PS[:, 3, P * ii:P * (ii + 1)], Nc[:, ii, :],
                          NTc[:, ii, :], start=True, stop=True,
                          deps=[(ve, bank_ve[3]), (se, bank_se[3])],
                          inc=(ii == 3))
                tSq = te.cnt
                ve.op("tensor_copy", Nc[:, :, :], PS[:, 2, :].rearrange(
                    "p (i q) -> p i q", i=4), deps=[(te, tSq)])
                bank_ve[2] = ve.cnt
                ve.op("tensor_copy", NTc[:, :, :], PS[:, 3, :].rearrange(
                    "p (i q) -> p i q", i=4), deps=[(te, tSq)])
                bank_ve[3] = ve.cnt
                vsq = ve.cnt
                for ii in range(4):
                    te.mm(PS[:, 4, P * ii:P * (ii + 1)], NTc[:, ii, :],
                          Xb[:, g0 + ii, :], start=True, stop=False,
                          deps=[(ve, vsq), (se, se.cnt), (ve, bank_ve[4]),
                                (se, bank_se[4])])
                    te.mm(PS[:, 4, P * ii:P * (ii + 1)], c["identb"][:, :],
                          Xb[:, g0 + ii, :], start=False, stop=True,
                          inc=(ii == 3))
                    te.mm(PS[:, 5, P * ii:P * (ii + 1)], Xb[:, g0 + ii, :],
                          NTc[:, ii, :], start=True, stop=False,
                          deps=[(ve, vsq), (se, se.cnt), (ve, bank_ve[5]),
                                (se, bank_se[5])])
                    te.mm(PS[:, 5, P * ii:P * (ii + 1)], c["identb"][:, :],
                          XTb[:, g0 + ii, :], start=False, stop=True,
                          inc=(ii == 3))
                tL = te.cnt
                ve.op("tensor_copy", Xb[:, g0:g0 + 4, :],
                      PS[:, 4, :].rearrange("p (i q) -> p i q", i=4),
                      deps=[(te, tL)])
                bank_ve[4] = ve.cnt
                ve.op("tensor_copy", XTb[:, g0:g0 + 4, :],
                      PS[:, 5, :].rearrange("p (i q) -> p i q", i=4))
                bank_ve[5] = ve.cnt
        vXdone = ve.cnt

        kn2 = [rsb[:, 0:256], rsb[:, 256:512]]
        u2 = [rsb[:, 512:768], rsb[:, 768:1024]]
        wneg2 = [rsb[:, 1024:1280], rsb[:, 1280:1536]]
        S_bf2 = [rsb[:, 1536:2048].rearrange("p (b d) -> p b d", b=2),
                 mub[:, 0:512].rearrange("p (b d) -> p b d", b=2)]
        kb_cm2 = mub[:, 512:768]
        vb_cm2 = mub[:, 768:1024]
        ALT2 = [mub[:, 1024:1152], mub[:, 1152:1280]]
        ve.op("memset", S_bf2[1][:, :, :], 0.0, deps=[(te, te.cnt)])
        v_s2 = ve.cnt
        v_sbf = [v_s, v_s2]
        v_upre = [0, 0]
        s_wneg = [0, 0]
        s_kn = [0, 0]
        v_alt = [0, 0]
        v_u = [0, 0]

        def scan_pre(jj):
            p = jj % 2
            csl = slice(P * jj, P * (jj + 1))
            for mb in range(2):
                te.mm(PS[:, 3, 256:384], Bk[:, mb, csl], Bq[:, mb, csl],
                      start=(mb == 0), stop=(mb == 1),
                      deps=[(ve, bank_ve[3]), (se, bank_se[3]),
                            (ve, v_kbvb)])
            for mb in range(2):
                te.mm(PS[:, 2, P * mb:P * (mb + 1)],
                      Bk[:, mb, csl], c["identb"][:, :],
                      start=True, stop=True,
                      deps=[(ve, bank_ve[2]), (se, bank_se[2])])
                te.mm(PS[:, 2, 256 + P * mb:256 + P * (mb + 1)],
                      kb_fm[:, mb, csl], c["identb"][:, :],
                      start=True, stop=True, inc=(mb == 1))
                te.mm(PS[:, 3, P * mb:P * (mb + 1)],
                      vb_fm[:, mb, csl], c["identb"][:, :],
                      start=True, stop=True, inc=(mb == 1))
            t_t = te.cnt
            se.op("activation", kn2[p][:, :], PS[:, 2, 0:256], AF.Copy,
                  deps=[(te, t_t)])
            se.op("activation", kb_cm2[:, :], PS[:, 2, 256:512], AF.Copy)
            bank_se[2] = se.cnt
            se.op("activation", vb_cm2[:, :], PS[:, 3, 0:256], AF.Copy)
            bank_se[3] = se.cnt
            s_evac1 = se.cnt
            ve.op("tensor_mul", ALT2[p][:, :], PS[:, 3, 256:384],
                  mskL[:, :], deps=[(te, t_t), (gc, aux["g_msk"])])
            bank_ve[3] = ve.cnt
            v_alt[p] = ve.cnt
            for mb in range(2):
                te.mm(PS[:, 4, P * mb:P * (mb + 1)],
                      kb_cm2[:, P * mb:P * (mb + 1)], XTb[:, jj, :],
                      start=True, stop=True,
                      deps=[(se, s_evac1), (ve, bank_ve[4]),
                            (se, bank_se[4]), (ve, vXdone)],
                      inc=(mb == 1))
            t_w = te.cnt
            se.op("activation", wneg2[p][:, :], PS[:, 4, 0:256], AF.Copy,
                  scale=c["cpool"][:, 5:6], deps=[(te, t_w)])
            bank_se[4] = se.cnt
            s_wneg[p] = se.cnt
            s_kn[p] = s_evac1
            te.mm(PS[:, 6 + p, 0:256], XTb[:, jj, :], vb_cm2[:, :],
                  start=True, stop=False,
                  deps=[(se, s_evac1), (ve, bank_ve[6 + p]), (ve, vXdone),
                        (se, bank_se[6 + p])])

        def scan_chain(jj):
            p = jj % 2
            for mb in range(2):
                te.mm(PS[:, 6 + p, 0:256],
                      wneg2[p][:, P * mb:P * (mb + 1)],
                      S_bf2[1 - p][:, mb, :], start=False, stop=(mb == 1),
                      deps=[(se, s_wneg[p]), (ve, v_sbf[1 - p])],
                      inc=(mb == 1))
            t_u = te.cnt
            ve.op("tensor_copy", u2[p][:, :], PS[:, 6 + p, 0:256],
                  deps=[(te, t_u)])
            bank_ve[6 + p] = ve.cnt
            v_u[p] = ve.cnt
            for mb in range(2):
                te.mm(PS[:, 1, 256 * mb:256 * (mb + 1)],
                      kn2[p][:, P * mb:P * (mb + 1)], u2[p][:, :],
                      start=True, stop=True,
                      deps=[(ve, v_u[p]), (se, s_kn[p]), (ve, bank_ve[1])],
                      inc=(mb == 1))
            t_s = te.cnt
            ve.op("tensor_add", S_bf2[p][:, :, :], S0[:, :, :],
                  PS[:, 1, :].rearrange("p (b d) -> p b d", b=2),
                  deps=[(te, t_s)])
            v_sbf[p] = ve.cnt
            ve.op("tensor_add", S0[:, :, :], S0[:, :, :],
                  PS[:, 1, :].rearrange("p (b d) -> p b d", b=2))
            bank_ve[1] = ve.cnt

        def scan_o(jj):
            p = jj % 2
            csl = slice(P * jj, P * (jj + 1))
            for dvb in range(2):
                for mb in range(2):
                    te.mm(PS[:, 0, P * dvb:P * (dvb + 1)],
                          S_bf2[1 - p][:, mb, P * dvb:P * (dvb + 1)],
                          Bq[:, mb, csl], start=(mb == 0), stop=False,
                          deps=[(ve, bank_ve[0]), (ve, v_sbf[1 - p]),
                                (ve, v_u[p]), (ve, v_alt[p])])
                te.mm(PS[:, 0, P * dvb:P * (dvb + 1)],
                      u2[p][:, P * dvb:P * (dvb + 1)], ALT2[p][:, :],
                      start=False, stop=True, inc=(dvb == 1))
            t_o = te.cnt
            ve.op("tensor_copy", delta[:, :, csl],
                  PS[:, 0, 0:256].rearrange("p (b q) -> p b q", b=2),
                  deps=[(te, t_o)])
            bank_ve[0] = ve.cnt

        scan_pre(0)
        scan_pre(1)
        for jj in range(16):
            scan_chain(jj)
            scan_o(jj)
            if jj + 2 < 16:
                scan_pre(jj + 2)
        v_delta = ve.cnt
        ar.pop()

        ar.push()
        W1q = ar.alloc3(NB, 256)
        stats16 = ar.alloc(T)
        p_tm = ar.alloc(64).rearrange("p (tt j) -> p tt j", tt=16)
        h1t = ar.alloc(512)
        W1B = aux["wsl_flat"][0:16, 0:1024]
        w1b_tok = ctx.qg.dma(W1B, dr["g_w1"][D:D + 16, :],
                             deps=[(te, te.cnt)])

        for mb in range(2):
            gb = 2 * h + mb
            ve.op("tensor_scalar_mul", Bq[:, mb, :], Bv[:, mb, 60:60 + T],
                  FIRS[:, gb, 0:1], deps=[(ve, v_delta), sm_tok])
            for j in range(1, 5):
                ve.op("scalar_tensor_tensor", Bq[:, mb, :],
                      Bv[:, mb, 60 + j:60 + j + T], FIRS[:, gb, j:j + 1],
                      Bq[:, mb, :], ALU.mult, ALU.add)
        v_fs = ve.cnt
        half = [[sq[:, 128 * jj:128 * (jj + 1)] for jj in range(8)],
                [sq[:, 1024 + 128 * jj:1024 + 128 * (jj + 1)]
                 for jj in range(8)]]
        for mb in range(2):
            gb = 2 * h + mb
            half_war = [te.cnt, te.cnt]
            for jg in range(8):
                par = jg % 2
                gdeps = [sm_tok, aux["firl"], (te, half_war[par]),
                         (ve, v_delta)]
                for jj in range(8):
                    j = 8 * jg + jj
                    gc.op("affine_select", half[par][jj],
                          FIRL[:, gb, j:j + 1].broadcast_to([P, P]),
                          [[1, P]], ALU.is_equal, 0.0, base=0,
                          channel_multiplier=-1, deps=gdeps)
                gcd = gc.cnt
                for jj in range(8):
                    j = 8 * jg + jj
                    for tc in range(4):
                        bank = 4 + tc
                        te.mm(PS[:, bank, :], half[par][jj],
                              Bv[:, mb, 1 + j + 512 * tc:
                                 1 + j + 512 * (tc + 1)],
                              start=(j == 0), stop=(j == 63),
                              deps=[(gc, gcd), (ve, bank_ve[bank]),
                                    (se, bank_se[bank])],
                              inc=(jj == 7 and tc == 3))
                half_war[par] = te.cnt
            tfl = te.cnt
            for tc in range(4):
                bank = 4 + tc
                ve.op("tensor_copy", Bk[:, mb, 512 * tc:512 * (tc + 1)],
                      PS[:, bank, :], deps=[(te, tfl)])
                bank_ve[bank] = ve.cnt
        v_fl = ve.cnt

        st_toks = []
        r_mean = rows[0:1, 0:T]
        r_var = rows[0:1, T:2 * T]
        r_abs = rows[32:33, 0:T]
        r_l2 = rows[32:33, T:2 * T]
        sqb = [sq, mub]
        st_row_war = []
        t3_prev = te.cnt
        for si, (src2, off) in enumerate(
                ((Bq, 0), (Bk, 0), (delta, 0), (Bv, 64))):
            sdep = [(ve, v_fl), (ve, v_fs)]
            for tc in range(4):
                for mb in range(2):
                    te.mm(PS[0:1, 4 + tc, :], c["onescolb"][:, :],
                          src2[:, mb, off + 512 * tc:off + 512 * (tc + 1)],
                          start=(mb == 0), stop=(mb == 1),
                          deps=sdep + [(ve, bank_ve[4 + tc]),
                                       (se, bank_se[4 + tc])],
                          inc=(mb == 1 and tc == 3))
            t1 = te.cnt
            for mb in range(2):
                se.op("activation", sqb[mb][:, :],
                      src2[:, mb, off:off + T], AF.Square,
                      deps=sdep + [(te, t3_prev)])
            s_sq = se.cnt
            for tc in range(4):
                for mb in range(2):
                    te.mm(PS[32:33, 4 + tc, :], c["onescolb"][:, :],
                          sqb[mb][:, 512 * tc:512 * (tc + 1)],
                          start=(mb == 0), stop=(mb == 1),
                          deps=[(se, s_sq)], inc=(mb == 1 and tc == 3))
            t2 = te.cnt
            for mb in range(2):
                se.op("activation", sqb[mb][:, :],
                      src2[:, mb, off:off + T], AF.Abs, deps=[(te, t2)])
            s_ab = se.cnt
            for tc in range(4):
                for mb in range(2):
                    te.mm(PS[64:65, 4 + tc, :], c["onescolb"][:, :],
                          sqb[mb][:, 512 * tc:512 * (tc + 1)],
                          start=(mb == 0), stop=(mb == 1),
                          deps=[(se, s_ab)], inc=(mb == 1 and tc == 3))
            t3 = te.cnt
            t3_prev = t3
            s1 = PS[0:1, 4:8, :].rearrange("p a b -> p (a b)")
            s2 = PS[32:33, 4:8, :].rearrange("p a b -> p (a b)")
            s3 = PS[64:65, 4:8, :].rearrange("p a b -> p (a b)")
            ve.op("tensor_scalar_mul", r_mean, s1, 1.0 / 256,
                  deps=[(te, t1)] + st_row_war)
            ve.op("tensor_mul", r_var, r_mean, r_mean, deps=[(te, t2)])
            ve.op("scalar_tensor_tensor", r_var, s2, 1.0 / 256, r_var,
                  ALU.mult, ALU.subtract)
            ve.op("tensor_scalar_mul", r_abs, s3, 1.0 / 256,
                  deps=[(te, t3)])
            v_rows = ve.cnt
            se.op("activation", r_l2, s2, AF.Sqrt,
                  deps=[(te, t2), (ve, v_rows)] + st_row_war)
            s_rows = se.cnt
            for tc in range(4):
                bank_ve[4 + tc] = v_rows
                bank_se[4 + tc] = s_rows
            for j, (rsrc, dep) in enumerate(((r_mean, (ve, v_rows)),
                                             (r_var, (ve, v_rows)),
                                             (r_abs, (ve, v_rows)),
                                             (r_l2, (se, s_rows)))):
                tok = qg.dma(stats16[4 * si + j:4 * si + j + 1, :],
                             rsrc, deps=[dep], pool="aux")
                st_toks.append(tok)
            st_row_war = st_toks[-4:]

        ve.op("tensor_add", BB[0:4, 5:6], BB[0:4, 4:5], BB[0:4, h:h + 1],
              deps=[sm_tok])
        v_bb = ve.cnt
        lg_fm = rows[0:4, T:2 * T]
        lg_war = [se.cnt, se.cnt, se.cnt, se.cnt]
        for qd in range(4):
            w1_ld = wload3(ctx, W1q, dr["g_w1"][0:D, 256 * qd:256 * (qd + 1)],
                           deps=[(te, te.cnt)])
            for fb2 in range(2):
                fblk = 2 * qd + fb2
                for kb in range(NB):
                    for tc in range(4):
                        te.mm(PS[:, tc, :],
                              W1q[:, kb, 128 * fb2:128 * (fb2 + 1)],
                              xn1[:, kb, 512 * tc:512 * (tc + 1)],
                              start=(kb == 0), stop=False,
                              deps=[w1_ld, w1b_tok, (se, bank_se[tc]),
                                    (ve, bank_ve[tc]), (ve, v_xn1)] + st_toks,
                              inc=False)
                for tc in range(4):
                    te.mm(PS[:, tc, :],
                          W1B[0:16, 128 * fblk:128 * (fblk + 1)],
                          stats16[0:16, 512 * tc:512 * (tc + 1)],
                          start=False, stop=True, inc=True)
                tcnt = te.cnt
                for tc in range(4):
                    se.op("activation", h1t[:, :], PS[:, tc, :], AF.Gelu,
                          bias=SM[:, fblk:fblk + 1],
                          deps=[(te, tcnt), (te, te.cnt)])
                    scnt = se.cnt
                    bank_se[tc] = scnt
                    te.mm(PS[0:4, 4 + tc, :], W2S[:, fblk, :], h1t[:, :],
                          start=(fblk == 0), stop=(fblk == 7),
                          deps=[(se, scnt), w2s_tok, (se, lg_war[tc]),
                                (ve, bank_ve[4 + tc])],
                          inc=True)
                    if fblk == 7:
                        se.op("activation", lg_fm[:, 512 * tc:512 * (tc + 1)],
                              PS[0:4, 4 + tc, :], AF.Copy,
                              deps=[(te, te.cnt)])
                        lg_war[tc] = se.cnt
                        bank_se[4 + tc] = se.cnt
        v_lg_se = se.cnt
        ve.op("tensor_scalar_add", lg_fm[:, :], lg_fm[:, :], BB[0:4, 5:6],
              deps=[(se, v_lg_se), (ve, v_bb)])
        v_lg = ve.cnt

        te.mm(PS[0:4, 2, 0:1], c["ones1f"][0:1, 0:4], SR[0:1, 8 + h:9 + h],
              start=True, stop=True,
              deps=[(ve, v_itemp), (ve, bank_ve[2])], inc=True)
        tid = te.cnt
        ve.op("tensor_scalar_mul", BB[0:4, 6:7], PS[0:4, 2, 0:1], 1.0,
              deps=[(te, tid)])
        ve.op("tensor_scalar_mul", idt4[:, :], c["identb"][0:4, 0:4],
              BB[0:4, 6:7])
        v_idt = ve.cnt
        bank_ve[2] = v_idt

        for tg in range(4):
            for k in range(4):
                tt = 4 * tg + k
                te.mm(PS[:, 3, 4 * k:4 * k + 4],
                      lg_fm[:, P * tt:P * (tt + 1)], idt4[:, :],
                      start=True, stop=True,
                      deps=[(ve, v_lg), (ve, v_idt), (ve, bank_ve[3])],
                      inc=(k == 3))
            tcnt = te.cnt
            g16 = PS[:, 3, 0:16].rearrange("p (a b) -> p a b", b=4)
            ve.op("tensor_reduce", stm[:, 0:4], g16, AX.X, ALU.max,
                  deps=[(te, tcnt)])
            ve.op("tensor_sub", stm[:, 16:32].rearrange(
                "p (a b) -> p a b", b=4), g16,
                stm[:, 0:4].rearrange("p (a b) -> p a b", b=1
                                      ).broadcast_to([P, 4, 4]))
            vsm = ve.cnt
            se.op("activation", stm[:, 16:32], stm[:, 16:32], AF.Exp,
                  deps=[(ve, vsm)])
            ssm = se.cnt
            bank_ve[3] = ve.cnt
            e16 = stm[:, 16:32].rearrange("p (a b) -> p a b", b=4)
            ve.op("tensor_reduce", stm[:, 0:4], e16, AX.X, ALU.add,
                  deps=[(se, ssm)])
            ve.op("reciprocal", stm[:, 4:8], stm[:, 0:4])
            ve.op("tensor_mul", e16, e16,
                  stm[:, 4:8].rearrange("p (a b) -> p a b", b=1
                                        ).broadcast_to([P, 4, 4]))
            ve.op("tensor_max", stm[:, 16:32], stm[:, 16:32], FLOOR[:, :],
                  deps=[(ve, v_floor)])
            ve.op("tensor_reduce", stm[:, 0:4], e16, AX.X, ALU.add)
            ve.op("reciprocal", stm[:, 4:8], stm[:, 0:4])
            ve.op("tensor_mul", p_tm[:, 4 * tg:4 * tg + 4, :], e16,
                  stm[:, 4:8].rearrange("p (a b) -> p a b", b=1
                                        ).broadcast_to([P, 4, 4]))
        v_ptm = ve.cnt

        prow4 = rows[0:4, 0:T]
        for tt in range(16):
            te.mm(PS[0:4, 2, P * (tt % 4):P * (tt % 4) + P]
                  if False else PS[0:4, 2, 0:P],
                  p_tm[:, tt, :], c["identb"][:, :],
                  start=True, stop=True,
                  deps=[(ve, v_ptm), (se, se.cnt), (ve, bank_ve[2])],
                  inc=True)
            tcnt = te.cnt
            se.op("activation", prow4[:, P * tt:P * (tt + 1)],
                  PS[0:4, 2, 0:P], AF.Copy, deps=[(te, tcnt)])
            bank_ve[2] = ve.cnt
        s_prow = se.cnt
        te.mm(PS[:, 2, 0:1], c["ones1f"][:, :], SR[0:1, 12 + h:13 + h],
              start=True, stop=True, deps=[(ve, v_crl), (se, s_prow)],
              inc=True)
        tcnt = te.cnt
        ve.op("tensor_scalar_mul", ctx.fcol[:, 15:16], PS[:, 2, 0:1], 1.0,
              deps=[(te, tcnt)])
        v_crlc = ve.cnt
        bank_ve[2] = v_crlc

        tmp = h1t
        srcs = [(Bq, 0), (Bk, 0), (delta, 0), (Bv, 64)]
        comb_war = ve.cnt
        for j in range(4):
            for tc in range(4):
                bank = j % 2
                te.mm(PS[:, bank, :], self_j[0:4, P * j:P * (j + 1)],
                      prow4[:, 512 * tc:512 * (tc + 1)],
                      start=True, stop=True,
                      deps=[(se, s_prow), (ve, bank_ve[bank]),
                            (gc, aux["g_msk"])], inc=True)
                tcnt = te.cnt
                src2, off = srcs[j]
                for mb in range(2):
                    if j == 0:
                        ve.op("tensor_mul",
                              o_h[:, mb, 512 * tc:512 * (tc + 1)],
                              src2[:, mb, off + 512 * tc:off + 512 * (tc + 1)],
                              PS[:, bank, :], deps=[(te, tcnt), (ve, comb_war)])
                    else:
                        ve.op("tensor_mul", tmp[:, :],
                              src2[:, mb, off + 512 * tc:off + 512 * (tc + 1)],
                              PS[:, bank, :], deps=[(te, tcnt)])
                        ve.op("tensor_add",
                              o_h[:, mb, 512 * tc:512 * (tc + 1)],
                              o_h[:, mb, 512 * tc:512 * (tc + 1)], tmp[:, :])
                bank_ve[bank] = ve.cnt
        for mb in range(2):
            ve.op("scalar_tensor_tensor", o_h[:, mb, :], Bq[:, mb, :],
                  ctx.fcol[:, 15:16], o_h[:, mb, :], ALU.mult, ALU.add,
                  deps=[(ve, v_crlc)])
        v_comb = ve.cnt

        sqb2 = [sq, mub]
        for mb in range(2):
            se.op("activation", sqb2[mb][:, :], o_h[:, mb, :],
                  AF.Square, deps=[(ve, v_comb), (te, te.cnt)])
        s_rs = se.cnt
        for tt in range(16):
            for mb in range(2):
                te.mm(PS[:, 6, tt:tt + 1], sqb2[mb][:, P * tt:P * (tt + 1)],
                      c["onescolb"][:, :], start=(mb == 0), stop=(mb == 1),
                      deps=[(se, s_rs), (ve, bank_ve[6])],
                      inc=(tt == 15 and mb == 1))
        trs = te.cnt
        ve.op("tensor_scalar_mul", stm[:, 0:16], PS[:, 6, 0:16], 1.0 / 256,
              deps=[(te, trs)])
        bank_ve[6] = ve.cnt
        vv = ve.cnt
        se.op("activation", stm[:, 16:32], stm[:, 0:16], AF.Sqrt,
              bias=c["cpool"][:, 0:1], deps=[(ve, vv)])
        srs = se.cnt
        ve.op("reciprocal", stm[:, 0:16], stm[:, 16:32], deps=[(se, srs)])
        v_rstd = ve.cnt
        te.mm(PS[0:16, 2, 0:P], stm[:, 0:16], c["identb"][:, :],
              start=True, stop=True,
              deps=[(ve, v_rstd), (ve, bank_ve[2])], inc=True)
        tcnt = te.cnt
        ve.op("tensor_copy", st16[:, :], PS[0:16, 2, 0:P],
              deps=[(te, tcnt)] + st16_war)
        bank_ve[2] = ve.cnt
        v16 = ve.cnt
        rrow = rows[0:1, 0:T]
        rs_tok = qg.dma(rrow.rearrange("o (a b) -> o a b", a=16), st16[:, :],
                        deps=[(ve, v16)], pool="aux")
        st16_war = [rs_tok]
        for tc in range(4):
            te.mm(PS[:, tc, :], c["ones1b"][:, :],
                  rrow[0:1, 512 * tc:512 * (tc + 1)],
                  start=True, stop=True,
                  deps=[rs_tok, (ve, bank_ve[tc])], inc=True)
        tbc = te.cnt
        for tc in range(4):
            ve.op("tensor_copy", mub[:, 512 * tc:512 * (tc + 1)],
                  PS[:, tc, :], deps=[(te, tbc)])
            bank_ve[tc] = ve.cnt
        for mb in range(2):
            ve.op("scalar_tensor_tensor", o_h[:, mb, :], o_h[:, mb, :],
                  SM[:, 8 + mb:9 + mb], mub[:, :], ALU.mult, ALU.mult)
        v_on = ve.cnt
        s_on = se.cnt

        wo_war = te.cnt
        for quarter in range(8):
            wo_ld = wload3(ctx, wsl[:, 0:2, 0:P],
                           dr["wo1"][256 * h:256 * (h + 1),
                                     P * quarter:P * (quarter + 1)],
                           deps=[(te, wo_war)])
            mbk = quarter
            for tc in range(4):
                bank = tc % 2
                for kb in range(2):
                    te.mm(PS[:, bank, :], wsl[:, kb, 0:P],
                          o_h[:, kb, 512 * tc:512 * (tc + 1)],
                          start=(kb == 0), stop=(kb == 1),
                          deps=[wo_ld, (ve, v_on), (se, s_on),
                                (ve, bank_ve[bank]), (se, bank_se[bank])],
                          inc=(kb == 1))
                tcnt = te.cnt
                ve.op("tensor_add", RES[:, mbk, 512 * tc:512 * (tc + 1)],
                      RES[:, mbk, 512 * tc:512 * (tc + 1)], PS[:, bank, :],
                      deps=[(te, tcnt)])
                bank_ve[bank] = ve.cnt
            wo_war = te.cnt
        ar.pop()
        ar.pop()



def host_layout(inputs, core):
    g = lambda k: np.asarray(inputs[k])
    fmv = lambda v: v.reshape(NB, P).T
    m = {}
    m["xT"] = g("L0_attn_x")[core].T
    wqkvT = g("L0_attn_W_qkv").T.reshape(D, 16, 3, 64)
    m["w_qT"] = wqkvT[:, :, 0, :].reshape(D, D)
    m["w_kT"] = wqkvT[:, :, 1, :].reshape(D, D)
    m["w_vT"] = wqkvT[:, :, 2, :].reshape(D, D)
    bqkv = g("L0_attn_b_qkv").reshape(16, 3, 64)
    m["bq_a"] = fmv(bqkv[:, 0, :].reshape(-1))
    m["bk_a"] = fmv(bqkv[:, 1, :].reshape(-1))
    m["bv_a"] = bqkv[:, 2, :].reshape(-1, 1)
    m["w_oT"] = g("L0_attn_W_o").T
    m["b_o"] = fmv(g("L0_attn_b_o"))
    m["ln1_g"] = fmv(g("L0_ffn_ln1_g"))
    m["ln1_b"] = fmv(g("L0_ffn_ln1_b"))
    m["ln2_g"] = fmv(g("L0_ffn_ln2_g"))
    m["ln2_b"] = fmv(g("L0_ffn_ln2_b"))
    m["wq0"] = g("L0_ffn_Wq"); m["bq0"] = fmv(g("L0_ffn_bq"))
    m["wk0"] = g("L0_ffn_Wk"); m["bk0"] = fmv(g("L0_ffn_bk"))
    m["wv0"] = g("L0_ffn_Wv"); m["bv0"] = g("L0_ffn_bv").reshape(-1, 1)
    m["wp0"] = g("L0_ffn_Wp"); m["bp0"] = fmv(g("L0_ffn_bp"))
    m["gate_w0"] = g("L0_ffn_gate_W")
    m["gate_b0"] = g("L0_ffn_gate_b").reshape(1, -1)
    m["exp_w1"] = g("L0_ffn_exp_W1")
    m["exp_b1"] = g("L0_ffn_exp_b1").reshape(8, 32, P).transpose(0, 2, 1)
    m["exp_w2"] = g("L0_ffn_exp_W2"); m["exp_b2"] = g("L0_ffn_exp_b2")
    m["wq1"] = g("L1_attn_Wq"); m["wk1"] = g("L1_attn_Wk")
    m["wv1"] = g("L1_attn_Wv"); m["wb1"] = g("L1_attn_Wb")
    sw = lambda v, k: v.reshape(NB, P, k).transpose(1, 0, 2)
    m["conv_q"] = sw(g("L1_attn_conv_q"), 4)
    m["conv_k"] = sw(g("L1_attn_conv_k"), 4)
    m["conv_v"] = sw(g("L1_attn_conv_v"), 4)
    m["fir_long"] = sw(g("L1_attn_fir_long").reshape(D, 64), 64)
    m["fir_short"] = sw(g("L1_attn_fir_short").reshape(D, 5), 5)
    m["g_w1"] = g("L1_attn_W1"); m["g_b1"] = fmv(g("L1_attn_b1"))
    m["g_w2"] = g("L1_attn_W2"); m["g_b2"] = g("L1_attn_b2").reshape(-1, 1)
    m["log_temp"] = g("L1_attn_log_temp").reshape(1, -1)
    m["base_bias"] = g("L1_attn_base_bias").T
    m["crl"] = g("L1_attn_conv_residual_logit").reshape(1, -1)
    m["rms_w"] = g("L1_attn_rms_w").reshape(2, P).T
    m["wo1"] = g("L1_attn_Wo")
    m["gate_w1"] = g("L1_ffn_gate_w").T
    m["m_w1"] = g("L1_ffn_w1")
    m["m_b1"] = g("L1_ffn_b1").reshape(8, 8, P).transpose(0, 2, 1)
    m["m_w2"] = g("L1_ffn_w2"); m["m_b2"] = g("L1_ffn_b2")
    return {k: np.ascontiguousarray(v, dtype=np.float32) for k, v in m.items()}


def kernel(**inputs):
    nc, dr = build_program(layers=(0, 1))
    in_names = set(dr.keys())
    maps = []
    for core in range(8):
        full = host_layout(inputs, core)
        maps.append({k: v for k, v in full.items() if k in in_names})
    res = run_bass_kernel_spmd(nc, maps, core_ids=list(range(8)))
    out = np.stack([np.ascontiguousarray(r["outT"].T) for r in res.results])
    return out.astype(np.float32)



def wload3(ctx, dst3, dram_ap, deps=()):
    return ctx.qg.dma(dst3, dram_ap.rearrange("(b p) f -> p b f", p=P), deps=deps)


def emit_mha(ctx, x3d, x_dep, w, causal, evac_out, name, NG=2,
             spill_o=False):
    te, ve, se, gc, qg = ctx.te, ctx.ve, ctx.se, ctx.gc, ctx.qg
    c = ctx.consts
    PS = ctx.PS
    rows = ctx.ln_scr["rows"]
    fcol = ctx.fcol

    ar = ctx.arena
    ar.push()
    x_dep = list(x_dep) + [(ctx.se, ctx.se.cnt), (ctx.ve, ctx.ve.cnt),
                           (ctx.gc, ctx.gc.cnt)]
    nbg = 1
    if spill_o:
        o_scr = ctx.nc.dram_tensor(name + "_oscr", [D, T], BF16,
                                   kind="Internal").ap()
        ostg = [ar.alloc(512) for _ in range(2)]
        ostg_tok = [None, None]
        ostg_all = {}
        O3 = None
    else:
        O3 = ar.alloc3(NB, T)
    qf = ar.alloc3(nbg, T)
    kf = ar.alloc3(nbg, T)
    vtm = ar.alloc(16 * NG * 65).rearrange("p (tt h e) -> p tt h e", tt=16, h=NG)
    sq1 = ctx.ln_scr["sq"]
    sq2 = ctx.ln_scr["mub"]
    rb = ctx.ln_scr["rsb"][:, 0:512]
    cmcol = ctx.cmcolb
    wqs = ar.alloc3(NB, NG * 64)
    wks = ar.alloc3(NB, NG * 64)
    wvs = ar.alloc3(NB, NG * 64)
    wos = wvs

    ve.op("memset", vtm[:, :, :, 64:65], 1.0)

    assert NG == 2
    n_groups = 16 // NG
    pt3 = [ctx.ln_scr["rsb"][:, 512:1536], ctx.ln_scr["mub"][:, 0:1024],
           sq1[:, 0:1024]]
    pt_war = [0, 0, 0]
    st_war = [0, 0]
    se0 = se.cnt
    bank_war = {b: 0 for b in range(8)}
    bank_wars = {b: 0 for b in range(8)}
    wbuf_war = {"q": 0, "k": 0, "v": 0}

    for g in range(n_groups):
        cs = slice(g * NG * 64, (g + 1) * NG * 64)
        wq_ld = wload3(ctx, wqs, w["wq"][:, cs], deps=[(te, wbuf_war["q"])])
        wk_ld = wload3(ctx, wks, w["wk"][:, cs], deps=[(te, wbuf_war["k"])])
        wv_ld = wload3(ctx, wvs, w["wv"][:, cs], deps=[(te, wbuf_war["v"])])

        for which, wt, dstf, ld, bfm in (("q", wqs, qf, wq_ld, w["bqf"]),
                                         ("k", wks, kf, wk_ld, w["bkf"])):
            for kb in range(NB):
                for tc in range(4):
                    te.mm(PS[:, 4 + tc, :], wt[:, kb, 0:P],
                          x3d[:, kb, 512 * tc:512 * (tc + 1)],
                          start=(kb == 0), stop=(kb == NB - 1),
                          deps=[ld, (ve, bank_war[4 + tc]),
                                (se, bank_wars[4 + tc])] + x_dep,
                          inc=(kb == NB - 1 and tc == 3))
            tcnt = te.cnt
            for tc in range(4):
                ve.op("tensor_scalar_add",
                      dstf[:, 0, 512 * tc:512 * (tc + 1)],
                      PS[:, 4 + tc, :], bfm[:, g:g + 1],
                      deps=[(te, tcnt)])
                bank_war[4 + tc] = ve.cnt
            wbuf_war[which] = te.cnt
        vqk = ve.cnt

        for tt in range(16):
            bank = 2 + (tt % 2)
            te.mm(PS[:, bank, 0:NG * 64], c["ones1b"][0:1, :],
                  w["bvr"][0:1, cs], start=True, stop=False,
                  deps=[wv_ld, (ve, bank_war[bank]), (se, st_war[1]),
                        (ve, c["ready_ve"])] + x_dep)
            for kb in range(NB):
                te.mm(PS[:, bank, 0:NG * 64], x3d[:, kb, P * tt:P * (tt + 1)],
                      wvs[:, kb, :], start=False, stop=(kb == NB - 1),
                      inc=(kb == NB - 1))
            tcnt = te.cnt
            for hh in range(NG):
                ve.op("tensor_copy", vtm[:, tt, hh, 0:64],
                      PS[:, bank, hh * 64:hh * 64 + 64], deps=[(te, tcnt)])
            bank_war[bank] = ve.cnt
        wbuf_war["v"] = te.cnt

        sqs = {}
        for j, (srcf, dsts) in enumerate(((qf, sq1), (kf, sq2))):
            se.op("activation", dsts[:, :], srcf[:, 0, :], AF.Square,
                  deps=[(ve, vqk), (te, pt_war[1]), (te, pt_war[2])])
            sqs[j] = se.cnt
        for j, dsts in ((0, sq1), (1, sq2)):
            for tmb in range(16):
                te.mm(PS[:, 6 + j, tmb:tmb + 1],
                      dsts[:, P * tmb:P * (tmb + 1)],
                      c["onescolb"][:, :], start=True, stop=True,
                      deps=[(se, sqs[j]), (ve, bank_war[6 + j])],
                      inc=(tmb == 15))
        tq = te.cnt
        for j in range(2):
            ve.op("reduce_max", cmcol[:, j:j + 1], PS[:, 6 + j, 0:16], AX.X,
                  deps=[(te, tq)])
            bank_war[6 + j] = ve.cnt
        vc = ve.cnt
        for j in range(2):
            te.mm(PS[0:1, 6 + j, 0:128], cmcol[:, j:j + 1], c["identb"][:],
                  start=True, stop=True,
                  deps=[(ve, vc), (ve, bank_war[6 + j]), (gc, c["ready_gc"])],
                  inc=True)
        tcnt = te.cnt
        for j in range(2):
            ve.op("reduce_max", rows[0:1, 1500 + j:1501 + j],
                  PS[0:1, 6 + j, 0:128], AX.X, deps=[(te, tcnt)])
            bank_war[6 + j] = ve.cnt
        vmx = ve.cnt
        se.op("activation", rows[0:1, 1500:1502], rows[0:1, 1500:1502],
              AF.Sqrt, deps=[(ve, vmx)])
        ssq = se.cnt
        ve.op("tensor_mul", rows[0:1, 1502:1503], rows[0:1, 1500:1501],
              rows[0:1, 1501:1502], deps=[(se, ssq)])
        vprod = ve.cnt
        te.mm(PS[:, 7, 0:1], c["ones1b"][:], rows[0:1, 1502:1503],
              start=True, stop=True,
              deps=[(ve, vprod), (ve, bank_war[7])], inc=True)
        tcnt = te.cnt
        ve.op("tensor_scalar_mul", fcol[:, 2 * g:2 * g + 1], PS[:, 7, 0:1],
              -0.125, deps=[(te, tcnt), (se, se0)])
        bank_war[7] = ve.cnt
        vcb = ve.cnt

        for qc in range(4):
            kt_hi = 16 if not causal else 4 * (qc + 1)
            pv_depq = {}
            for kt in range(kt_hi + 2):
                if kt < kt_hi:
                    par = kt % 3
                    sb = kt % 2
                    for hh in range(NG):
                        base = 64 * hh
                        bank = 2 * sb + hh
                        te.mm(PS[:, bank, :],
                              kf[base:base + 64, 0, P * kt:P * (kt + 1)],
                              qf[base:base + 64, 0, 512 * qc:512 * (qc + 1)],
                              start=True, stop=True,
                              deps=[(ve, vcb), (se, st_war[sb]),
                                    (ve, bank_war[bank])],
                              inc=True)
                    tcnt = te.cnt
                    se.op("activation", pt3[par][:, :],
                          PS[:, 2 * sb:2 * sb + 2, :].rearrange(
                              "p a b -> p (a b)"),
                          AF.Exp, bias=fcol[:, 2 * g:2 * g + 1],
                          scale=c["cpool"][:, 2:3],
                          deps=[(te, tcnt), (te, pt_war[par])])
                    scnt = se.cnt
                    st_war[sb] = scnt
                    for hh in range(NG):
                        dep_pv = (se, scnt)
                        if causal and kt >= 4 * qc:
                            gc.op("affine_select",
                                  pt3[par][:, 512 * hh:512 * (hh + 1)],
                                  pt3[par][:, 512 * hh:512 * (hh + 1)],
                                  [[1, 512]], ALU.is_ge,
                                  0.0, base=qc * 512 - kt * P,
                                  channel_multiplier=-1, deps=[(se, scnt)])
                            dep_pv = (gc, gc.cnt)
                        pv_depq.setdefault(kt, {})[hh] = dep_pv
                if kt >= 2:
                    kp = kt - 2
                    parp = kp % 3
                    for hh in range(NG):
                        te.mm(PS[0:65, 6 + hh, 0:512], vtm[:, kp, hh, :],
                              pt3[parp][:, 512 * hh:512 * (hh + 1)],
                              start=(kp == 0), stop=(kp == kt_hi - 1),
                              deps=[pv_depq[kp][hh],
                                    (ve, bank_war[6 + hh])],
                              inc=(kp == kt_hi - 1))
                        pt_war[parp] = te.cnt
            for hh in range(NG):
                h = g * NG + hh
                oblk, obase = h // 2, 64 * (h % 2)
                tpv = te.cnt
                ve.op("tensor_copy", rows[0:1, 512 * hh:512 * (hh + 1)],
                      PS[64:65, 6 + hh, :], deps=[(te, tpv)])
                vr_ = ve.cnt
                te.mm(PS[0:64, 4 + hh, :], c["ones1b"][0:1, 0:64],
                      rows[0:1, 512 * hh:512 * (hh + 1)],
                      start=True, stop=True,
                      deps=[(ve, vr_), (ve, bank_war[4 + hh])], inc=True)
                tcnt = te.cnt
                ve.op("reciprocal", rb[0:64, :], PS[0:64, 4 + hh, :],
                      deps=[(te, tcnt)])
                bank_war[4 + hh] = ve.cnt
                if spill_o:
                    par = (4 * h + qc) % 2
                    dep_st = [ostg_tok[par]] if ostg_tok[par] else []
                    ve.op("tensor_mul", ostg[par][0:64, :],
                          PS[0:64, 6 + hh, 0:512], rb[0:64, :], deps=dep_st)
                    ostg_tok[par] = ctx.qs.dma(
                        o_scr[64 * h:64 * h + 64, 512 * qc:512 * (qc + 1)],
                        ostg[par][0:64, :], deps=[(ve, ve.cnt)])
                    ostg_all[ostg_tok[par].sem.name] = ostg_tok[par]
                else:
                    ve.op("tensor_mul",
                          O3[obase:obase + 64, oblk, 512 * qc:512 * (qc + 1)],
                          PS[0:64, 6 + hh, 0:512], rb[0:64, :])
                bank_war[6 + hh] = ve.cnt
    vO = ve.cnt

    assert spill_o
    ord_toks = list(ostg_all.values())
    wo_buf = ar.alloc3(NB, 512)
    Ost = [ar.alloc3(NB, 512) for _ in range(2)]
    ost_tok = [None, None]
    ost_war = [0, 0]
    wo_war = wbuf_war["v"]
    for half in range(2):
        wo_ld = wload3(ctx, wo_buf,
                       w["wo"][:, 512 * half:512 * (half + 1)],
                       deps=[(te, wo_war)])
        for tc in range(4):
            par = tc % 2
            ost_tok[par] = ctx.qs.dma(
                Ost[par][:, :, :],
                o_scr[:, 512 * tc:512 * (tc + 1)].rearrange(
                    "(b p) t -> p b t", p=P),
                deps=[(te, ost_war[par])] + ord_toks)
            for q2 in range(4):
                quarter = 4 * half + q2
                bank = 4 + q2
                for kb in range(NB):
                    te.mm(PS[:, bank, :], wo_buf[:, kb, P * q2:P * (q2 + 1)],
                          Ost[par][:, kb, :],
                          start=(kb == 0), stop=(kb == NB - 1),
                          deps=[wo_ld, ost_tok[par], (ve, bank_war[bank]),
                                (se, bank_wars[bank])],
                          inc=(kb == NB - 1))
                evac_out(PS[:, bank, :], quarter, tc, te.cnt)
                bank_war[bank] = ve.cnt
            ost_war[par] = te.cnt
        wo_war = te.cnt
    ar.pop()
    return {"ve": ve.cnt, "te": te.cnt}



def emit_moe(ctx, x3d, x_dep, w1_d, b1_d, w2_d, b2_d, gw_d, gb_d, H,
             softmax_then_topk, w1bufs, res_dep, name):
    te, ve, se, gc, qs, qg = ctx.te, ctx.ve, ctx.se, ctx.gc, ctx.qs, ctx.qg
    c = ctx.consts
    PS = ctx.PS
    RES = ctx.RES
    x_dep = list(x_dep) + [(ctx.se, ctx.se.cnt), (ctx.ve, ctx.ve.cnt),
                           (ctx.gc, ctx.gc.cnt), (ctx.ve, res_dep)]
    NFC = H // 1024
    ar = ctx.arena
    ar.push()
    macc = ar.alloc(16 * 1024).rearrange("p (tt d) -> p tt d", tt=16)
    sq3 = ctx.ln_scr["sq"].rearrange("p (b t) -> p b t", b=4)
    mub3 = ctx.ln_scr["mub"].rearrange("p (b t) -> p b t", b=4)
    midv = lambda fb: (sq3 if fb % 2 == 0 else mub3)[:, fb // 2, :]
    w2buf = ar.alloc3(NB, 1024)
    gws = ar.alloc3(NB, 8)
    gbr = ar.alloc(32)
    b2r = ctx.rowbuf
    if not hasattr(ctx, "moe_sb"):
        ctx.moe_sb = dict(
            iota8=ctx.sb("moe_iota", [P, 8], F32),
            wf=ctx.sb("moe_wf", [P, 16, 8], F32),
            gsc=ctx.sb("moe_gsc", [P, 20], F32),
            i8=ctx.sb("moe_i8", [P, 8], U32),
            b1f=ctx.sb("moe_b1f", [P, 32], F32))
    iota8 = ctx.moe_sb["iota8"]
    wf = ctx.moe_sb["wf"]
    gsc = ctx.moe_sb["gsc"]
    i8 = ctx.moe_sb["i8"]
    b1f = ctx.moe_sb["b1f"]

    gc.op("iota", iota8[:, :], [[1, 8]], channel_multiplier=0,
          allow_small_or_imprecise_dtypes=True)
    g_iota = gc.cnt

    gw_ld = ctx.qg.dma(gws, gw_d.rearrange("(b p) f -> p b f", p=P),
                       pool="persist")
    gb_tok = gw_ld
    if gb_d is not None:
        gb_tok = ctx.qg.dma(gbr[0:1, 0:8], gb_d, pool="persist")

    gate_war = []
    for tt in range(16):
        if gb_d is not None:
            te.mm(PS[:, 6, 0:8], c["ones1b"][0:1, :], gbr[0:1, 0:8],
                  start=True, stop=False,
                  deps=[gw_ld, gb_tok, (ve, c["ready_ve"])] + x_dep + gate_war)
            first = False
        else:
            first = True
        for kb in range(NB):
            te.mm(PS[:, 6, 0:8], x3d[:, kb, P * tt:P * (tt + 1)],
                  gws[:, kb, :], start=(first and kb == 0),
                  stop=(kb == NB - 1),
                  deps=[gw_ld, gb_tok, (ve, c["ready_ve"])] + x_dep + gate_war,
                  inc=(kb == NB - 1))
        tcnt = te.cnt
        lg = gsc[:, 0:8]
        if softmax_then_topk:
            ve.op("reduce_max", gsc[:, 8:9], PS[:, 6, 0:8], AX.X,
                  negate=True, deps=[(te, tcnt)])
            se.op("activation", lg, PS[:, 6, 0:8], AF.Exp,
                  bias=gsc[:, 8:9], accum_out=gsc[:, 9:10],
                  deps=[(ve, ve.cnt), (te, tcnt)])
            ve.op("reciprocal", gsc[:, 10:11], gsc[:, 9:10],
                  deps=[(se, se.cnt)])
            ve.op("tensor_scalar_mul", lg, lg, gsc[:, 10:11])
        else:
            ve.op("tensor_copy", lg, PS[:, 6, 0:8], deps=[(te, tcnt)])
        vlg = ve.cnt
        gate_war = [(ve, ve.cnt), (se, se.cnt)]
        ve.op("max", gsc[:, 12:20], lg)
        ve.op("max_index", i8[:, :], gsc[:, 12:20], lg)
        ve.op("tensor_copy", gsc[:, 8:10], i8[:, 0:2])
        if softmax_then_topk:
            ve.op("tensor_copy", gsc[:, 10:11], gsc[:, 12:13])
            ve.op("tensor_copy", gsc[:, 11:12], gsc[:, 13:14])
            vw = ve.cnt
        else:
            ve.op("tensor_sub", gsc[:, 10:11], gsc[:, 12:13], gsc[:, 13:14])
            se.op("activation", gsc[:, 10:11], gsc[:, 10:11], AF.Sigmoid,
                  deps=[(ve, ve.cnt)])
            ve.op("tensor_scalar", gsc[:, 11:12], gsc[:, 10:11], -1.0, 1.0,
                  ALU.mult, ALU.add, deps=[(se, se.cnt)])
            vw = ve.cnt
        ve.op("scalar_tensor_tensor", wf[:, tt, :], iota8[:, :],
              gsc[:, 8:9], gsc[:, 10:11].broadcast_to([P, 8]),
              ALU.is_equal, ALU.mult, deps=[(gc, g_iota), (ve, vw)])
        ve.op("scalar_tensor_tensor", gsc[:, 12:20], iota8[:, :],
              gsc[:, 9:10], gsc[:, 11:12].broadcast_to([P, 8]),
              ALU.is_equal, ALU.mult)
        ve.op("tensor_add", wf[:, tt, :], wf[:, tt, :], gsc[:, 12:20])
    v_wf = ve.cnt
    bank_war = {b: 0 for b in range(8)}
    bank_war_se = {b: 0 for b in range(8)}
    bank_war[6] = v_wf

    w1_war = [te.cnt, te.cnt]
    w2_war = te.cnt
    b_war = se.cnt
    mid_se = {fb: 0 for fb in range(NB)}
    for e in range(8):
        b1_ld = qs.dma(b1f[:, 0:H // P], b1_d[e], deps=[(se, b_war)])
        b2_ld = qg.dma(b2r[0:1, 0:D], b2_d[e:e + 1, :], deps=[(te, w2_war)])
        for fc in range(NFC):
            wbi = (e * NFC + fc) % 2
            wb = w1bufs[wbi]
            w1_ld = wload3(ctx, wb, w1_d[e, :, 1024 * fc:1024 * (fc + 1)],
                           deps=[(te, w1_war[wbi])])
            w2_ld = wload3(ctx, w2buf,
                           w2_d[e, 1024 * fc:1024 * (fc + 1), :],
                           deps=[(te, w2_war)])
            for ch in range(4):
                t0 = 512 * ch
                for fb in range(NB):
                    bank = fb % 4
                    for kb in range(NB):
                        te.mm(PS[:, bank, :],
                              wb[:, kb, P * fb:P * (fb + 1)],
                              x3d[:, kb, t0:t0 + 512],
                              start=(kb == 0), stop=(kb == NB - 1),
                              deps=[w1_ld, (se, bank_war_se[bank])] + x_dep,
                              inc=(kb == NB - 1))
                    tcnt = te.cnt
                    se.op("activation", midv(fb)[:, :],
                          PS[:, bank, :], AF.Gelu,
                          bias=b1f[:, 8 * fc + fb:8 * fc + fb + 1],
                          deps=[(te, tcnt), b1_ld])
                    bank_war_se[bank] = se.cnt
                    mid_se[fb] = se.cnt
                for ts in range(4):
                    tt = 4 * ch + ts
                    for n in range(2):
                        ybank = 4 + 2 * (ts % 2) + n
                        if fc == 0:
                            te.mm(PS[:, ybank, :], c["ones1b"][0:1, :],
                                  b2r[0:1, 512 * n:512 * (n + 1)],
                                  start=True, stop=False,
                                  deps=[b2_ld, (ve, bank_war[ybank])])
                    for fb in range(NB):
                        for n in range(2):
                            ybank = 4 + 2 * (ts % 2) + n
                            te.mm(PS[:, ybank, :],
                                  midv(fb)[:, P * ts:P * (ts + 1)],
                                  w2buf[:, fb, 512 * n:512 * (n + 1)],
                                  start=(fc != 0 and fb == 0),
                                  stop=(fb == NB - 1),
                                  deps=[w2_ld, (se, mid_se[fb]),
                                        (ve, bank_war[ybank])],
                                  inc=(fb == NB - 1 and n == 1))
                    tcnt = te.cnt
                    for n in range(2):
                        ybank = 4 + 2 * (ts % 2) + n
                        dst = macc[:, tt, 512 * n:512 * (n + 1)]
                        if e == 0 and fc == 0:
                            ve.op("tensor_scalar_mul", dst,
                                  PS[:, ybank, :], wf[:, tt, e:e + 1],
                                  deps=[(te, tcnt), (ve, v_wf)])
                        else:
                            ve.op("scalar_tensor_tensor", dst,
                                  PS[:, ybank, :], wf[:, tt, e:e + 1],
                                  dst, ALU.mult, ALU.add,
                                  deps=[(te, tcnt)])
                        bank_war[ybank] = ve.cnt
            w1_war[wbi] = te.cnt
            w2_war = te.cnt
        b_war = se.cnt
    v_macc = ve.cnt

    for tt in range(16):
        for db in range(NB):
            bank = (tt * NB + db) % 2
            te.mm(PS[:, bank, 0:128], macc[:, tt, P * db:P * (db + 1)],
                  c["identb"][:], start=True, stop=True,
                  deps=[(ve, v_macc), (se, bank_war_se[bank]),
                        (ve, bank_war[bank]), (gc, c["ready_gc"])],
                  inc=True)
            tcnt = te.cnt
            ve.op("tensor_add", RES[:, db, P * tt:P * (tt + 1)],
                  RES[:, db, P * tt:P * (tt + 1)], PS[:, bank, 0:128],
                  deps=[(te, tcnt), (ve, res_dep)])
            bank_war[bank] = ve.cnt
    ar.pop()
    return ve.cnt


def run_layer1_build(taps=()):
    return build_program(layers=(1,), taps=taps)


# version 35
# speedup vs baseline: 1.0163x; 1.0163x over previous
import contextlib
import numpy as np
import ml_dtypes
import concourse.bass as bass
import concourse.mybir as mybir
from concourse.bass_utils import run_bass_kernel_spmd

F32 = mybir.dt.float32
BF16 = mybir.dt.bfloat16
U32 = mybir.dt.uint32
AF = mybir.ActivationFunctionType
ALU = mybir.AluOpType
AX = mybir.AxisListType

P = 128
T = 2048
D = 1024
NB = D // P



class Sem:
    def __init__(self, nc, es, name):
        self.h = es.enter_context(nc.semaphore(name))
        self.name = name


class Seq:

    def __init__(self, sem, autosync):
        self.raw = None
        self.sem = sem
        self.cnt = 0
        self.autosync = autosync
        self.waited = {}

    def wait(self, sem, val):
        if val <= 0:
            return
        if self.waited.get(sem.name, -1) >= val:
            return
        self.raw.wait_ge(sem.h, val)
        self.waited[sem.name] = val

    def dep(self, deps):
        for d in deps:
            if isinstance(d, Tok):
                i = d.sem._pool_i
                self.wait(d.sem, d.val)
                d.sem._pool_q.consumers[i].append((self, self.cnt + 1))
            else:
                s, v = d
                self.wait(s.sem if isinstance(s, (Seq, Q)) else s, v)

    def op(self, name, *a, deps=(), **kw):
        self.dep(deps)
        if self.autosync and self.cnt:
            self.wait(self.sem, self.cnt)
        inst = getattr(self.raw, name)(*a, **kw)
        inst.then_inc(self.sem.h, 1)
        self.cnt += 1
        return self.cnt

    def mm(self, out, lhsT, rhs, start, stop, deps=(), inc=False, **kw):
        self.dep(deps)
        inst = self.raw.matmul(out, lhsT, rhs, start=start, stop=stop, **kw)
        if inc:
            inst.then_inc(self.sem.h, 1)
            self.cnt += 1
        return self.cnt


class Tok:

    def __init__(self, sem, val):
        self.sem = sem
        self.val = val


class Q:

    def __init__(self, nc, es, name, raw_get, npool, naux=0, npers=0):
        self.raw = None
        self._raw_get = raw_get
        n = npool + naux + npers
        self.sems = [Sem(nc, es, f"{name}{i}") for i in range(n)]
        self.tot = [0] * n
        self.consumers = [[] for _ in range(n)]
        self.idx = 0
        self.npool = npool
        self.aux_lo = npool
        self.aux_n = naux
        self.aux_idx = 0
        self.pers_lo = npool + naux
        self.pers_n = npers
        self.pers_idx = 0
        self.waited = {}
        self.cur_batch = None

    def wait(self, sem, val):
        if val <= 0:
            return
        if self.waited.get(sem.name, -1) >= val:
            return
        self.raw.wait_ge(sem.h, val)
        self.waited[sem.name] = val

    def dep(self, deps):
        for d in deps:
            if isinstance(d, Tok):
                i = d.sem._pool_i
                self.wait(d.sem, d.val)
                d.sem._pool_q.consumers[i].append((self, None))
            else:
                s, v = d
                self.wait(s.sem if isinstance(s, (Seq, Q)) else s, v)

    def open_batch(self, pool="main"):
        self.cur_batch = self._next_tok(pool)
        return self.cur_batch

    def close_batch(self):
        t = self.cur_batch
        self.cur_batch = None
        return t

    def _next_tok(self, pool="main"):
        if pool == "main":
            i = self.idx % self.npool
            self.idx += 1
        elif pool == "aux":
            i = self.aux_lo + self.aux_idx % self.aux_n
            self.aux_idx += 1
        else:
            assert self.pers_idx < self.pers_n, "persist sems exhausted"
            i = self.pers_lo + self.pers_idx
            self.pers_idx += 1
        sem = self.sems[i]
        sem._pool_i = i
        sem._pool_q = self
        gate = self.consumers[i]
        self.consumers[i] = []
        for c in gate:
            if isinstance(c[0], Seq):
                self.wait(c[0].sem, c[1])
            elif isinstance(c[0], Q):
                pass
        return Tok(sem, self.tot[i])

    def dma(self, out, in_, deps=(), batch=None, pool="main", **kw):
        self.dep(deps)
        if batch is None and self.cur_batch is not None:
            batch = self.cur_batch
        tok = batch if batch is not None else self._next_tok(pool)
        i = tok.sem._pool_i
        self.raw.dma_start(out=out, in_=in_, **kw).then_inc(tok.sem.h, 16)
        self.tot[i] += 16
        tok.val = self.tot[i]
        return tok


class Ctx:
    def __init__(self, nc, es, taps=()):
        self.nc = nc
        self.es = es
        self.taps = set(taps)
        self.tap_list = []
        self.s_te = Sem(nc, es, "s_te")
        self.s_ve = Sem(nc, es, "s_ve")
        self.s_se = Sem(nc, es, "s_se")
        self.s_gc = Sem(nc, es, "s_gc")
        self.te = Seq(self.s_te, autosync=False)
        self.ve = Seq(self.s_ve, autosync=True)
        self.se = Seq(self.s_se, autosync=True)
        self.gc = Seq(self.s_gc, autosync=True)
        self.qs = Q(nc, es, "qs", None, 6, naux=0, npers=6)
        self.qg = Q(nc, es, "qg", None, 10, naux=20, npers=12)

    def bind(self):
        nc = self.nc
        self.te.raw = nc.tensor
        self.ve.raw = nc.vector
        self.se.raw = nc.scalar
        self.gc.raw = nc.gpsimd
        self.qs.raw = nc.sync
        self.qg.raw = nc.gpsimd

    def sb(self, name, shape, dtype):
        return self.es.enter_context(self.nc.sbuf_tensor(name, shape, dtype))

    def tap(self, name, src_ap, deps, shape, rearr=None):
        if name not in self.taps:
            return
        t = self.nc.dram_tensor("tap_" + name, shape, F32,
                                kind="ExternalOutput").ap()
        dst = t if rearr is None else t.rearrange(*rearr[0], **rearr[1])
        self.qg.dma(dst, src_ap, deps=deps)
        self.tap_list.append(name)


class Arena:

    def __init__(self, ctx, name, cols):
        self.t = ctx.sb(name, [P, cols], BF16)
        self.cols = cols
        self.off = 0
        self.marks = []

    def push(self):
        self.marks.append(self.off)

    def pop(self):
        self.off = self.marks.pop()

    def alloc(self, cols):
        cols = (cols + 31) // 32 * 32
        assert self.off + cols <= self.cols, \
            f"arena overflow: {self.off}+{cols} > {self.cols}"
        v = self.t[:, self.off:self.off + cols]
        self.off += cols
        return v

    def alloc3(self, nb, t):
        return self.alloc(nb * t).rearrange("p (b t) -> p b t", b=nb)


def barrier(ctx):
    engs = [ctx.te, ctx.ve, ctx.se, ctx.gc]
    qs = [ctx.qs, ctx.qg]
    cnts = [(e.sem, e.cnt) for e in engs]
    for q in qs:
        for sem, tot in zip(q.sems, q.tot):
            if tot > 0:
                cnts.append((sem, tot))
    for e in engs + qs:
        for sem, v in cnts:
            if getattr(e, "sem", None) is not sem:
                e.wait(sem, v)



def emit_consts(ctx):
    c = {}
    c["ones1b"] = ctx.sb("c_ones1b", [1, P], BF16)
    c["ones1f"] = ctx.sb("c_ones1f", [1, P], F32)
    c["onescolb"] = ctx.sb("c_onescolb", [P, 1], BF16)
    c["identb"] = ctx.sb("c_identb", [P, P], BF16)
    c["cpool"] = ctx.sb("c_cpool", [P, 8], F32)
    ve, gc = ctx.ve, ctx.gc
    ve.op("memset", c["ones1b"][:], 1.0)
    ve.op("memset", c["ones1f"][:], 1.0)
    ve.op("memset", c["onescolb"][:], 1.0)
    ve.op("memset", c["identb"][:], 1.0)
    ve.op("memset", c["cpool"][:, 0:1], 1e-5)
    ve.op("memset", c["cpool"][:, 1:2], 1e-6)
    ve.op("memset", c["cpool"][:, 2:3], 0.125)
    ve.op("memset", c["cpool"][:, 3:4], 1.0)
    ve.op("memset", c["cpool"][:, 4:5], 0.0)
    ve.op("memset", c["cpool"][:, 5:6], -1.0)
    mv = ve.cnt
    gc.op("affine_select", c["identb"][:], c["identb"][:], [[1, P]],
          ALU.is_equal, 0.0, base=0, channel_multiplier=-1,
          deps=[(ctx.ve, mv)])
    c["ready_gc"] = gc.cnt
    c["ready_ve"] = ve.cnt
    ctx.consts = c
    return c



def emit_ln(ctx, src3d, src_is_f32, dst3d, scr, gamma=None, beta=None,
            src_dep=()):
    te, ve, se = ctx.te, ctx.ve, ctx.se
    c = ctx.consts
    PS = ctx.PS
    src_dep = list(src_dep) + [(ctx.se, ctx.se.cnt), (ctx.ve, ctx.ve.cnt)]

    if src_is_f32:
        cast = scr["cast3d"]
        for b in range(NB):
            ve.op("tensor_copy", cast[:, b, :], src3d[:, b, :], deps=src_dep)
        xb3 = cast
    else:
        xb3 = src3d
    vcast = ve.cnt

    for tc in range(4):
        for b in range(NB):
            te.mm(PS[0:1, 4 + tc, :], c["onescolb"][:],
                  xb3[:, b, 512 * tc:512 * (tc + 1)],
                  start=(b == 0), stop=(b == NB - 1),
                  deps=([(ve, vcast)] + src_dep + [(ve, c["ready_ve"])]),
                  inc=(b == NB - 1))
    sq = scr["sq"]
    for b in range(NB):
        se.op("activation", sq[:, :], xb3[:, b, :], AF.Square,
              deps=[(ve, vcast), (te, te.cnt)] + src_dep)
        scnt = se.cnt
        for tc in range(4):
            te.mm(PS[32:33, 4 + tc, :], c["onescolb"][:],
                  sq[:, 512 * tc:512 * (tc + 1)],
                  start=(b == 0), stop=(b == NB - 1),
                  deps=[(se, scnt)], inc=(tc == 3))
    tsums = te.cnt

    rows = scr["rows"]
    r0 = rows[0:1, 0:T]
    r1 = rows[0:1, T:2 * T]
    r2 = rows[32:33, 0:T]
    r3 = rows[32:33, T:2 * T]
    sum_row = PS[0:1, 4:8, :].rearrange("p a b -> p (a b)")
    sq_row = PS[32:33, 4:8, :].rearrange("p a b -> p (a b)")
    ve.op("tensor_scalar_mul", r0, sum_row, 1.0 / D, deps=[(te, tsums)])
    ve.op("tensor_scalar_mul", r2, sq_row, 1.0 / D, deps=[(te, tsums)])
    ve.op("tensor_mul", r3, r0, r0)
    ve.op("tensor_sub", r2, r2, r3)
    vvar = ve.cnt
    se.op("activation", r3, r2, AF.Sqrt, bias=c["cpool"][0:1, 0:1],
          deps=[(ve, vvar)])
    ssq = se.cnt
    ve.op("reciprocal", r1, r3, deps=[(se, ssq)])
    vrows = ve.cnt

    mub, rsb = scr["mub"], scr["rsb"]
    for what, dstb in ((r0, scr["mub"]), (r1, scr["rsb"])):
        dep = [(ve, ve.cnt)]
        for tc in range(4):
            te.mm(PS[:, tc, :], c["ones1b"][:],
                  what[0:1, 512 * tc:512 * (tc + 1)],
                  start=True, stop=True, deps=dep + [(ve, vrows)], inc=True)
        tb = te.cnt
        for tc in range(4):
            ve.op("tensor_copy", dstb[:, 512 * tc:512 * (tc + 1)],
                  PS[:, tc, :], deps=[(te, tb)])
    vbc = ve.cnt

    for b in range(NB):
        ve.op("tensor_sub", dst3d[:, b, :], xb3[:, b, :], mub[:, :],
              deps=[(ve, vbc)] + src_dep)
        ve.op("tensor_mul", dst3d[:, b, :], dst3d[:, b, :], rsb[:, :])
        if gamma is not None:
            ve.op("tensor_scalar", dst3d[:, b, :], dst3d[:, b, :],
                  gamma[:, b:b + 1], beta[:, b:b + 1],
                  ALU.mult, ALU.add)
    return ve.cnt



def make_ln_scratch(ctx):
    return {
        "sq": ctx.sb("ln_sq", [P, T], BF16),
        "rows": ctx.sb("ln_rows", [33, 2 * T], BF16),
        "mub": ctx.sb("ln_mub", [P, T], BF16),
        "rsb": ctx.sb("ln_rsb", [P, T], BF16),
    }


DR_SHAPES = {
    "xT": [D, T],
    "w_qT": [D, D], "w_kT": [D, D], "w_vT": [D, D],
    "bq_a": [P, NB], "bk_a": [P, NB], "bv_a": [D, 1],
    "w_oT": [D, D], "b_o": [P, NB],
    "ln1_g": [P, NB], "ln1_b": [P, NB], "ln2_g": [P, NB], "ln2_b": [P, NB],
    "wq0": [D, D], "bq0": [P, NB], "wk0": [D, D], "bk0": [P, NB],
    "wv0": [D, D], "bv0": [D, 1], "wp0": [D, D], "bp0": [P, NB],
    "gate_w0": [D, 8], "gate_b0": [1, 8],
    "exp_w1": [8, D, 4 * D], "exp_b1": [8, P, 32],
    "exp_w2": [8, 4 * D, D], "exp_b2": [8, D],
    "wq1": [D, D], "wk1": [D, D], "wv1": [D, D], "wb1": [D, 4],
    "conv_q": [P, NB, 4], "conv_k": [P, NB, 4], "conv_v": [P, NB, 4],
    "fir_long": [P, NB, 64], "fir_short": [P, NB, 5],
    "g_w1": [D + 16, D], "g_b1": [P, NB], "g_w2": [D, 4], "g_b2": [4, 1],
    "log_temp": [1, 4], "base_bias": [4, 4], "crl": [1, 4], "rms_w": [P, 2],
    "wo1": [D, D], "gate_w1": [D, 8],
    "m_w1": [8, D, D], "m_b1": [8, P, 8], "m_w2": [8, D, D], "m_b2": [8, D],
}

L0_NAMES = ["w_qT", "w_kT", "w_vT", "bq_a", "bk_a", "bv_a",
            "w_oT", "b_o", "ln1_g", "ln1_b", "ln2_g",
            "ln2_b", "wq0", "bq0", "wk0", "bk0", "wv0", "bv0", "wp0", "bp0",
            "gate_w0", "gate_b0", "exp_w1", "exp_b1", "exp_w2", "exp_b2"]
L1_NAMES = ["wq1", "wk1", "wv1", "wb1", "conv_q", "conv_k", "conv_v",
            "fir_long", "fir_short", "g_w1", "g_b1", "g_w2", "g_b2",
            "log_temp", "base_bias", "crl", "rms_w", "wo1", "gate_w1",
            "m_w1", "m_b1", "m_w2", "m_b2"]


def build_program(layers=(0, 1), taps=()):
    nc = bass.Bass("TRN2", target_bir_lowering=False, debug=False)
    dr = {"xT": nc.dram_tensor("xT", DR_SHAPES["xT"], F32,
                               kind="ExternalInput").ap()}
    use = (L0_NAMES if 0 in layers else []) + (L1_NAMES if 1 in layers else [])
    for nm in use:
        dr[nm] = nc.dram_tensor(nm, DR_SHAPES[nm], F32,
                                kind="ExternalInput").ap()
    out = nc.dram_tensor("outT", [D, T], F32, kind="ExternalOutput").ap()

    es = contextlib.ExitStack()
    with es:
        block = es.enter_context(nc.Block())
        ctx = Ctx(nc, es, taps)
        ctx.dr = dr
        ctx.PS = es.enter_context(nc.psum_tensor("PS", [P, 8, 512], F32))

        RES = ctx.sb("RES", [P, NB, T], F32)
        SCR_A = ctx.sb("SCR_A", [P, NB, T], BF16)
        ctx.RES, ctx.SCR_A = RES, SCR_A
        ctx.arena = Arena(ctx, "ARENA", 41088)
        assert 41088 >= 41056 + 16
        ctx.fcol = ctx.sb("fcol", [P, 16], F32)
        ctx.rowf = ctx.sb("rowf", [1, 512], F32)
        ctx.S0 = ctx.sb("S0", [P, 2, 256], F32)
        ctx.rowbuf = ctx.sb("rowbuf", [1, D], BF16)
        ctx.cmcolb = ctx.sb("cmcolb", [P, 2], BF16)

        @block.tensor
        def _(tensor):
            ctx.bind()
            lp = ctx.es.enter_context(nc.allow_low_precision(
                reason="bf16 stat rows within stated tolerance"))
            emit_consts(ctx)
            ln_scr = make_ln_scratch(ctx)
            ctx.ln_scr = ln_scr

            xld = ctx.qs.dma(RES[:, :, :],
                             dr["xT"].rearrange("(b p) t -> p b t", p=P),
                             pool="persist")
            ctx.xld = xld

            if 0 in layers:
                ctx.arena.push()
                ln_scr["cast3d"] = ctx.arena.alloc3(NB, T)
                v0 = emit_ln(ctx, RES, True, SCR_A, ln_scr, src_dep=[xld])
                ctx.arena.pop()
                ctx.tap("xn0", SCR_A[:, :, :], [(ctx.ve, v0)], [D, T],
                        rearr=(("(b p) t -> p b t",), {"p": P}))
                emit_layer0(ctx, ln_scr, v0)

            if 1 in layers:
                emit_layer1(ctx, ln_scr)

            outr = out.rearrange("(b p) t -> p b t", p=P)
            st_toks = []
            for chk in range(4):
                q = ctx.qs if chk % 2 == 0 else ctx.qg
                tok = q.dma(outr[:, :, 512 * chk:512 * (chk + 1)],
                            RES[:, :, 512 * chk:512 * (chk + 1)],
                            deps=[(ctx.ve, ctx.res_tt[4 * chk + 3])])
                st_toks.append(tok)
            barrier(ctx)
            for tok in st_toks:
                ctx.qs.wait(tok.sem, tok.val)
                ctx.gc.dep([tok])

    return nc, dr


def emit_layer0(ctx, ln_scr, v_xn0):
    te, ve, se, gc, qs, qg = ctx.te, ctx.ve, ctx.se, ctx.gc, ctx.qs, ctx.qg
    c = ctx.consts
    dr = ctx.dr
    ar = ctx.arena
    xn0, RES = ctx.SCR_A, ctx.RES

    SM = ctx.sb("SM0", [P, 96], F32)
    smap = {}
    qs.open_batch(pool="persist")
    for i, nm in enumerate(["ln1_g", "ln1_b", "ln2_g", "ln2_b", "bq0", "bk0",
                            "bp0", "b_o", "bq_a", "bk_a"]):
        smap[nm] = SM[:, 8 * i:8 * (i + 1)]
        sm_tok = qs.dma(smap[nm], dr[nm])
    qs.close_batch()
    rowbuf = ctx.rowbuf
    rb_tok = qg.dma(rowbuf[:, :], dr["bv0"].rearrange("d o -> o d"),
                    pool="persist")

    ar.push()

    h2 = ar.alloc3(NB, T)
    v_h2 = emit_ln(ctx, xn0, False, h2, ln_scr, gamma=smap["ln1_g"],
                   beta=smap["ln1_b"],
                   src_dep=[(ve, v_xn0), sm_tok])
    ctx.tap("h2", h2[:, :, :], [(ve, v_h2)], [D, T],
            rearr=(("(b p) t -> p b t",), {"p": P}))

    ACC = h2

    def evac_inner(ps_ap, mb, tc, tecnt):
        ve.op("scalar_tensor_tensor", ACC[:, mb, 512 * tc:512 * (tc + 1)],
              ps_ap, smap["bp0"][:, mb:mb + 1],
              xn0[:, mb, 512 * tc:512 * (tc + 1)],
              ALU.add, ALU.add, deps=[(te, tecnt), (ve, v_xn0)])

    w_in = {"wq": dr["wq0"], "wk": dr["wk0"], "wv": dr["wv0"],
            "wo": dr["wp0"], "bqf": smap["bq0"], "bkf": smap["bk0"],
            "bvr": rowbuf[:, :]}
    r = emit_mha(ctx, h2, [(ve, v_h2), rb_tok, sm_tok], w_in,
                 causal=True, evac_out=evac_inner, name="att0i",
                 spill_o=True)
    v_acc = r["ve"]
    ctx.tap("acc", ACC[:, :, :], [(ve, v_acc)], [D, T],
            rearr=(("(b p) t -> p b t",), {"p": P}))

    for b in range(NB):
        ve.op("tensor_add", RES[:, b, :], RES[:, b, :], ACC[:, b, :],
              deps=[(ve, v_acc)])
    v_res1 = ve.cnt

    moein = ACC
    v_moein = emit_ln(ctx, ACC, False, moein, ln_scr, gamma=smap["ln2_g"],
                      beta=smap["ln2_b"], src_dep=[(ve, v_res1)])
    ctx.tap("moein", moein[:, :, :], [(ve, v_moein)], [D, T],
            rearr=(("(b p) t -> p b t",), {"p": P}))

    rb_tok = qg.dma(rowbuf[:, :], dr["bv_a"].rearrange("d o -> o d"),
                    deps=[(ctx.te, r["te"])], pool="persist")
    def evac_outer(ps_ap, mb, tc, tecnt):
        ve.op("scalar_tensor_tensor", RES[:, mb, 512 * tc:512 * (tc + 1)],
              ps_ap, smap["b_o"][:, mb:mb + 1],
              RES[:, mb, 512 * tc:512 * (tc + 1)],
              ALU.add, ALU.add, deps=[(te, tecnt), (ve, v_res1)])

    w_out = {"wq": dr["w_qT"], "wk": dr["w_kT"], "wv": dr["w_vT"],
             "wo": dr["w_oT"], "bqf": smap["bq_a"], "bkf": smap["bk_a"],
             "bvr": rowbuf[:, :]}
    r = emit_mha(ctx, xn0, [(ve, v_xn0), rb_tok, sm_tok], w_out,
                 causal=False, evac_out=evac_outer, name="att0o",
                 spill_o=True)
    v_res2 = r["ve"]
    ctx.tap("res_preMoE", RES[:, :, :], [(ve, v_res2)], [D, T],
            rearr=(("(b p) t -> p b t",), {"p": P}))

    w1bufs = [ctx.SCR_A[:, :, 0:1024], ctx.SCR_A[:, :, 1024:2048]]
    emit_moe(ctx, moein, [(ve, v_moein)], dr["exp_w1"], dr["exp_b1"],
             dr["exp_w2"], dr["exp_b2"], dr["gate_w0"], dr["gate_b0"],
             H=4 * D, softmax_then_topk=False, w1bufs=w1bufs,
             res_dep=v_res2, name="moe0")
    ar.pop()


def emit_layer1(ctx, ln_scr):
    te, ve, se, gc, qs, qg = ctx.te, ctx.ve, ctx.se, ctx.gc, ctx.qs, ctx.qg
    c = ctx.consts
    dr = ctx.dr
    PS = ctx.PS
    ar = ctx.arena
    xn1, RES = ctx.SCR_A, ctx.RES
    rows = ln_scr["rows"]
    sq = ln_scr["sq"]
    mub = ln_scr["mub"]
    rsb = ln_scr["rsb"]

    ar.push()
    mskA = ar.alloc(2 * P)
    mskL = ar.alloc(P)
    self_j = ar.alloc(4 * P)
    beta_r = ar.alloc(T)
    wsl_flat = ar.alloc(2048)
    wsl = wsl_flat.rearrange("p (b f) -> p b f", b=NB)
    FIRL = ar.alloc(NB * 64).rearrange("p (b k) -> p b k", b=NB)
    stm = ar.alloc(64)
    st16 = ar.alloc(P)
    idt4 = ar.alloc(32)
    W2S = ar.alloc(NB * 4).rearrange("p (b k) -> p b k", b=NB)
    ve.op("memset", mskA[:, 0:P], -1.0)
    ve.op("memset", mskA[:, P:2 * P], -1.0)
    ve.op("memset", mskL[:, :], 1.0)
    ve.op("memset", self_j[:, :], 1.0)
    mv = ve.cnt
    gc.op("affine_select", mskA[:, 0:P], mskA[:, 0:P], [[-1, P]], ALU.is_gt,
          0.0, base=0, channel_multiplier=1, deps=[(ve, mv)])
    gc.op("affine_select", mskA[:, P:2 * P], mskA[:, P:2 * P], [[1, P]],
          ALU.is_gt, 0.0, base=0, channel_multiplier=-1)
    gc.op("affine_select", mskL[:, :], mskL[:, :], [[1, P]], ALU.is_ge,
          0.0, base=0, channel_multiplier=-1)
    for j in range(4):
        gc.op("affine_select", self_j[0:4, P * j:P * (j + 1)],
              self_j[0:4, P * j:P * (j + 1)], [[0, P]], ALU.is_equal, 0.0,
              base=-j, channel_multiplier=1)
    g_msk = gc.cnt
    w2s_tok = qg.dma(W2S[:, :, :],
                     dr["g_w2"].rearrange("(b p) f -> p b f", p=P),
                     pool="persist")

    firl_tok = qg.dma(FIRL[:, :, :], dr["fir_long"], pool="persist")

    SM = ctx.sb("SM1", [P, 16], F32)
    sm_tok = qs.open_batch(pool="persist")
    qs.dma(SM[:, 0:8], dr["g_b1"])
    qs.dma(SM[:, 8:10], dr["rms_w"])
    CW = ctx.sb("l1_cw", [P, NB, 12], F32)
    qs.dma(CW[:, :, 0:4], dr["conv_q"])
    qs.dma(CW[:, :, 4:8], dr["conv_k"])
    qs.dma(CW[:, :, 8:12], dr["conv_v"])
    CWB = ctx.sb("l1_cwb", [P, NB, 12], BF16)
    qg.open_batch(pool="persist")
    qg.dma(CWB[:, :, 0:4], dr["conv_q"])
    qg.dma(CWB[:, :, 4:8], dr["conv_k"])
    cwb_tok = qg.dma(CWB[:, :, 8:12], dr["conv_v"])
    qg.close_batch()
    FIRS = ctx.sb("l1_firs", [P, NB, 5], F32)
    qs.dma(FIRS[:, :, :], dr["fir_short"])
    SR = ctx.sb("l1_sr", [4, 24], F32)
    qs.dma(SR[0:1, 0:4], dr["log_temp"])
    qs.dma(SR[0:1, 4:8], dr["crl"])
    BB = ctx.sb("l1_bb", [4, 8], F32)
    qs.dma(BB[0:4, 0:4], dr["base_bias"])
    qs.dma(BB[0:4, 4:5], dr["g_b2"])
    sm_tok = qs.close_batch()

    ar.push()
    ln_scr["cast3d"] = ar.alloc3(NB, T)
    v_res = ve.cnt
    v_xn1 = emit_ln(ctx, RES, True, xn1, ln_scr,
                    src_dep=[(ve, v_res), ctx.xld])
    ar.pop()
    ctx.tap("xn1", xn1[:, :, :], [(ve, v_xn1)], [D, T],
            rearr=(("(b p) t -> p b t",), {"p": P}))

    wbs = ar.alloc3(NB, 4)
    wb_ld = qg.dma(wbs, dr["wb1"].rearrange("(b p) f -> p b f", p=P),
                   pool="persist", deps=[(ctx.ve, v_xn1), (ctx.se, se.cnt)])

    emit_l1_heads(ctx, xn1, v_xn1, beta_r, (wbs, wb_ld), wsl,
                  dict(sm=sm_tok, SM=SM, CW=CW, FIRS=FIRS, FIRL=FIRL, SR=SR,
                       BB=BB, mskA=mskA, mskL=mskL, self_j=self_j,
                       g_msk=g_msk, wsl_flat=wsl_flat, firl=firl_tok,
                       idt4=idt4, W2S=W2S, w2s_tok=w2s_tok, stm=stm,
                       st16=st16, CWB=CWB, cwb_tok=cwb_tok))
    v_res2 = ctx.ve.cnt
    ar.pop()
    ctx.tap("res_preMoE1", RES[:, :, :], [(ve, v_res2)], [D, T],
            rearr=(("(b p) t -> p b t",), {"p": P}))

    ar.push()
    w1bufs = [ar.alloc3(NB, 1024), ar.alloc3(NB, 1024)]
    emit_moe(ctx, xn1, [(ve, v_xn1)], dr["m_w1"], dr["m_b1"],
             dr["m_w2"], dr["m_b2"], dr["gate_w1"], None,
             H=D, softmax_then_topk=True, w1bufs=w1bufs,
             res_dep=v_res2, name="moe1")
    ar.pop()


def emit_l1_heads(ctx, xn1, v_xn1, beta_r, wb, wsl, aux):
    wbs, wb_ld = wb
    te, ve, se, gc, qs, qg = ctx.te, ctx.ve, ctx.se, ctx.gc, ctx.qs, ctx.qg
    RES = ctx.RES
    c = ctx.consts
    dr = ctx.dr
    PS = ctx.PS
    ar = ctx.arena
    rows = ctx.ln_scr["rows"]
    sq = ctx.ln_scr["sq"]
    mub = ctx.ln_scr["mub"]
    rsb = ctx.ln_scr["rsb"]
    SM, CW, FIRS, FIRL = aux["SM"], aux["CW"], aux["FIRS"], aux["FIRL"]
    SR, BB = aux["SR"], aux["BB"]
    mskA, mskL, self_j = aux["mskA"], aux["mskL"], aux["self_j"]
    sm_tok = aux["sm"]
    NCH = 16

    S0 = ctx.S0

    idt4 = aux["idt4"][0:4, 0:4]
    W2S = aux["W2S"]
    w2s_tok = aux["w2s_tok"]
    FLOOR = ctx.sb("l1_floor", [P, 16], F32)
    stm = aux["stm"]
    st16 = aux["st16"][0:16, :]
    CWB = aux["CWB"]
    cwb_tok = aux["cwb_tok"]
    st16_war = []
    for jj in range(4):
        ve.op("memset", FLOOR[:, 4 * jj:4 * jj + 1], 0.05)
        ve.op("memset", FLOOR[:, 4 * jj + 1:4 * jj + 2], 0.05)
        ve.op("memset", FLOOR[:, 4 * jj + 2:4 * jj + 4], 0.0)
    v_floor = ve.cnt

    se.op("activation", SR[0:1, 8:12], SR[0:1, 0:4], AF.Abs, deps=[sm_tok])
    se.op("activation", SR[0:1, 8:12], SR[0:1, 8:12], AF.Exp,
          scale=ctx.consts["cpool"][0:1, 5:6])
    se.op("activation", SR[0:1, 8:12], SR[0:1, 8:12], AF.Ln,
          bias=c["cpool"][0:1, 3:4])
    s_sp = se.cnt
    ve.op("tensor_scalar_max", SR[0:1, 16:20], SR[0:1, 0:4], 0.0,
          deps=[sm_tok, (se, s_sp)])
    ve.op("tensor_add", SR[0:1, 16:20], SR[0:1, 16:20], SR[0:1, 8:12])
    ve.op("tensor_scalar_add", SR[0:1, 16:20], SR[0:1, 16:20], 1e-4)
    ve.op("reciprocal", SR[0:1, 8:12], SR[0:1, 16:20])
    v_itemp = ve.cnt
    se.op("activation", SR[0:1, 12:16], SR[0:1, 4:8], AF.Sigmoid,
          deps=[sm_tok, (ve, v_itemp)])
    s_crl = se.cnt
    ve.op("tensor_scalar_mul", SR[0:1, 12:16], SR[0:1, 12:16], 0.5,
          deps=[(se, s_crl)])
    v_crl = ve.cnt

    bank_ve = {b: 0 for b in range(8)}
    bank_se = {b: 0 for b in range(8)}

    for h in range(4):
        ar.push()
        Bv = ar.alloc(2 * (64 + T)).rearrange("p (b t) -> p b t", b=2)
        delta = ar.alloc3(2, T)
        Bq = ar.alloc3(2, T)
        Bk = ar.alloc3(2, T)
        o_h = ar.alloc3(2, T)

        ar.push()
        A_pad = ar.alloc(2 * (64 + T)).rearrange("p (b t) -> p b t", b=2)
        ve.op("memset", A_pad[:, :, 0:64], 0.0)
        ve.op("memset", Bv[:, :, 0:64], 0.0)
        v_pad = ve.cnt
        conv_done = {}
        for ti, (wnm, cw0, Bdst, off) in enumerate(
                (("wq1", 0, Bq, 0), ("wk1", 4, Bk, 0), ("wv1", 8, Bv, 64))):
            w_ld = wload3(ctx, wsl[:, :, 0:256],
                          dr[wnm][:, 256 * h:256 * (h + 1)],
                          deps=[(te, te.cnt)])
            for mb in range(2):
                for kb in range(NB):
                    for tc in range(4):
                        te.mm(PS[:, 4 + tc, :],
                              wsl[:, kb, 128 * mb:128 * (mb + 1)],
                              xn1[:, kb, 512 * tc:512 * (tc + 1)],
                              start=(kb == 0), stop=(kb == NB - 1),
                              deps=[w_ld, (ve, v_xn1), (ve, bank_ve[4 + tc]),
                                    (se, bank_se[4 + tc])],
                              inc=(kb == NB - 1 and tc == 3))
                tcnt = te.cnt
                for tc in range(4):
                    ve.op("tensor_copy",
                          A_pad[:, mb, 64 + 512 * tc:64 + 512 * (tc + 1)],
                          PS[:, 4 + tc, :], deps=[(te, tcnt), (ve, v_pad)])
                    bank_ve[4 + tc] = ve.cnt
            dt4 = [sq[:, 128 * jj:128 * (jj + 1)] for jj in range(8)]
            gdeps = [(se, se.cnt), (ve, ve.cnt), cwb_tok, (te, te.cnt)]
            for mb in range(2):
                gb = 2 * h + mb
                for j in range(4):
                    gc.op("affine_select", dt4[4 * mb + j],
                          CWB[:, gb, cw0 + j:cw0 + j + 1].broadcast_to([P, P]),
                          [[1, P]], ALU.is_equal, 0.0, base=0,
                          channel_multiplier=-1, deps=gdeps)
            gcv = gc.cnt
            for mb in range(2):
                for j in range(4):
                    for tc in range(4):
                        te.mm(PS[:, 4 + tc, :], dt4[4 * mb + j],
                              A_pad[:, mb, 61 + j + 512 * tc:
                                    61 + j + 512 * (tc + 1)],
                              start=(j == 0), stop=(j == 3),
                              deps=[(gc, gcv), (ve, bank_ve[4 + tc]),
                                    (se, bank_se[4 + tc])],
                              inc=(j == 3 and tc == 3))
                tcnt = te.cnt
                for tc in range(4):
                    se.op("activation",
                          Bdst[:, mb, off + 512 * tc:off + 512 * (tc + 1)],
                          PS[:, 4 + tc, :], AF.Silu, deps=[(te, tcnt)])
                    bank_se[4 + tc] = se.cnt
            conv_done[ti] = se.cnt
            conv_done[ti] = se.cnt
        ar.pop()

        for Bsrc, sdep in ((Bq, conv_done[0]), (Bk, conv_done[1])):
            for mb in range(2):
                se.op("activation", sq[:, :], Bsrc[:, mb, :], AF.Square,
                      deps=[(se, sdep), (te, te.cnt)])
                scnt = se.cnt
                for tc in range(4):
                    te.mm(PS[0:1, 4 + tc, :], c["onescolb"][:, :],
                          sq[:, 512 * tc:512 * (tc + 1)],
                          start=(mb == 0), stop=(mb == 1),
                          deps=[(se, scnt), (ve, bank_ve[4 + tc]),
                                (se, bank_se[4 + tc])],
                          inc=(tc == 3))
            tsum = te.cnt
            se.op("activation", rows[0:1, 0:T],
                  PS[0:1, 4:8, :].rearrange("p a b -> p (a b)"), AF.Sqrt,
                  bias=c["cpool"][0:1, 1:2], deps=[(te, tsum)])
            ssq_ = se.cnt
            for tc in range(4):
                bank_se[4 + tc] = ssq_
            ve.op("reciprocal", rows[0:1, 0:T], rows[0:1, 0:T],
                  deps=[(se, ssq_)])
            vr = ve.cnt
            for tc in range(4):
                te.mm(PS[:, tc, :], c["ones1b"][:, :],
                      rows[0:1, 512 * tc:512 * (tc + 1)],
                      start=True, stop=True,
                      deps=[(ve, vr), (ve, bank_ve[tc])], inc=True)
            tbc = te.cnt
            for tc in range(4):
                ve.op("tensor_copy", mub[:, 512 * tc:512 * (tc + 1)],
                      PS[:, tc, :], deps=[(te, tbc)])
                bank_ve[tc] = ve.cnt
            for mb in range(2):
                ve.op("tensor_mul", Bsrc[:, mb, :], Bsrc[:, mb, :], mub[:, :])
            for tc in range(4):
                bank_ve[4 + tc] = ve.cnt
        v_norm = ve.cnt

        ar.push()
        kb_fm = ar.alloc3(2, T)
        vb_fm = ar.alloc3(2, T)
        Xb = ar.alloc(16 * P).rearrange("p (i q) -> p i q", i=16)
        XTb = ar.alloc(16 * P).rearrange("p (i q) -> p i q", i=16)
        for tc in range(4):
            for kb in range(NB):
                te.mm(PS[0:1, 4 + tc, :], wbs[:, kb, h:h + 1],
                      xn1[:, kb, 512 * tc:512 * (tc + 1)],
                      start=(kb == 0), stop=(kb == NB - 1),
                      deps=[wb_ld, (ve, v_xn1), (se, bank_se[4 + tc]),
                            (ve, bank_ve[4 + tc])],
                      inc=(kb == NB - 1))
        tcnt = te.cnt
        se.op("activation", beta_r[0:1, 0:T],
              PS[0:1, 4:8, :].rearrange("p a b -> p (a b)"), AF.Sigmoid,
              deps=[(te, tcnt)])
        s_beta = se.cnt
        for tc in range(4):
            bank_se[4 + tc] = s_beta
        for tc in range(4):
            te.mm(PS[:, tc, :], c["ones1b"][:, :],
                  beta_r[0:1, 512 * tc:512 * (tc + 1)],
                  start=True, stop=True,
                  deps=[(se, s_beta), (ve, v_norm), (ve, bank_ve[tc])],
                  inc=True)
        tbc = te.cnt
        for tc in range(4):
            ve.op("tensor_copy", mub[:, 512 * tc:512 * (tc + 1)],
                  PS[:, tc, :], deps=[(te, tbc)])
            bank_ve[tc] = ve.cnt
        for mb in range(2):
            ve.op("tensor_mul", kb_fm[:, mb, :], Bk[:, mb, :], mub[:, :],
                  deps=[(se, conv_done[1])])
            ve.op("tensor_mul", vb_fm[:, mb, :], Bv[:, mb, 64:64 + T],
                  mub[:, :], deps=[(se, conv_done[2])])
        v_kbvb = ve.cnt

        ve.op("memset", S0[:, :, :], 0.0, deps=[(te, te.cnt)])
        ve.op("memset", rsb[:, 1536:2048], 0.0)
        v_s = ve.cnt
        bank_ve = {b: max(bank_ve[b], v_s) for b in range(8)}

        Nc = sq[:, 0:512].rearrange("p (i q) -> p i q", i=4)
        NTc = sq[:, 512:1024].rearrange("p (i q) -> p i q", i=4)
        for grp in range(4):
            g0 = 4 * grp
            for ii in range(4):
                i = g0 + ii
                csl = slice(P * i, P * (i + 1))
                for mb in range(2):
                    te.mm(PS[:, 0, P * ii:P * (ii + 1)],
                          kb_fm[:, mb, csl], Bk[:, mb, csl],
                          start=(mb == 0), stop=(mb == 1),
                          deps=[(ve, v_kbvb), (ve, bank_ve[0])],
                          inc=(mb == 1))
                    te.mm(PS[:, 1, P * ii:P * (ii + 1)],
                          Bk[:, mb, csl], kb_fm[:, mb, csl],
                          start=(mb == 0), stop=(mb == 1),
                          deps=[(ve, bank_ve[1])], inc=(mb == 1))
            tA = te.cnt
            ve.op("tensor_mul", Nc[:, :, :], PS[:, 0, :].rearrange(
                "p (i q) -> p i q", i=4),
                mskA[:, 0:P].rearrange("p (o q) -> p o q", o=1
                                       ).broadcast_to([P, 4, P]),
                deps=[(te, tA), (gc, aux["g_msk"])])
            bank_ve[0] = ve.cnt
            ve.op("tensor_mul", NTc[:, :, :], PS[:, 1, :].rearrange(
                "p (i q) -> p i q", i=4),
                mskA[:, P:2 * P].rearrange("p (o q) -> p o q", o=1
                                           ).broadcast_to([P, 4, P]))
            bank_ve[1] = ve.cnt
            ve.op("tensor_add", Xb[:, g0:g0 + 4, :], Nc[:, :, :],
                  c["identb"][:, :].rearrange("p (o q) -> p o q", o=1
                                              ).broadcast_to([P, 4, P]),
                  deps=[(te, te.cnt)])
            ve.op("tensor_add", XTb[:, g0:g0 + 4, :], NTc[:, :, :],
                  c["identb"][:, :].rearrange("p (o q) -> p o q", o=1
                                              ).broadcast_to([P, 4, P]))
            vX = ve.cnt
            for lv in range(6):
                for ii in range(4):
                    te.mm(PS[:, 2, P * ii:P * (ii + 1)], NTc[:, ii, :],
                          Nc[:, ii, :], start=True, stop=True,
                          deps=[(ve, ve.cnt), (se, se.cnt), (ve, bank_ve[2]),
                                (se, bank_se[2])],
                          inc=(ii == 3))
                    te.mm(PS[:, 3, P * ii:P * (ii + 1)], Nc[:, ii, :],
                          NTc[:, ii, :], start=True, stop=True,
                          deps=[(ve, bank_ve[3]), (se, bank_se[3])],
                          inc=(ii == 3))
                tSq = te.cnt
                ve.op("tensor_copy", Nc[:, :, :], PS[:, 2, :].rearrange(
                    "p (i q) -> p i q", i=4), deps=[(te, tSq)])
                bank_ve[2] = ve.cnt
                ve.op("tensor_copy", NTc[:, :, :], PS[:, 3, :].rearrange(
                    "p (i q) -> p i q", i=4), deps=[(te, tSq)])
                bank_ve[3] = ve.cnt
                vsq = ve.cnt
                for ii in range(4):
                    te.mm(PS[:, 4, P * ii:P * (ii + 1)], NTc[:, ii, :],
                          Xb[:, g0 + ii, :], start=True, stop=False,
                          deps=[(ve, vsq), (se, se.cnt), (ve, bank_ve[4]),
                                (se, bank_se[4])])
                    te.mm(PS[:, 4, P * ii:P * (ii + 1)], c["identb"][:, :],
                          Xb[:, g0 + ii, :], start=False, stop=True,
                          inc=(ii == 3))
                    te.mm(PS[:, 5, P * ii:P * (ii + 1)], Xb[:, g0 + ii, :],
                          NTc[:, ii, :], start=True, stop=False,
                          deps=[(ve, vsq), (se, se.cnt), (ve, bank_ve[5]),
                                (se, bank_se[5])])
                    te.mm(PS[:, 5, P * ii:P * (ii + 1)], c["identb"][:, :],
                          XTb[:, g0 + ii, :], start=False, stop=True,
                          inc=(ii == 3))
                tL = te.cnt
                ve.op("tensor_copy", Xb[:, g0:g0 + 4, :],
                      PS[:, 4, :].rearrange("p (i q) -> p i q", i=4),
                      deps=[(te, tL)])
                bank_ve[4] = ve.cnt
                ve.op("tensor_copy", XTb[:, g0:g0 + 4, :],
                      PS[:, 5, :].rearrange("p (i q) -> p i q", i=4))
                bank_ve[5] = ve.cnt
        vXdone = ve.cnt

        kn2 = [rsb[:, 0:256], rsb[:, 256:512]]
        u2 = [rsb[:, 512:768], rsb[:, 768:1024]]
        wneg2 = [rsb[:, 1024:1280], rsb[:, 1280:1536]]
        S_bf2 = [rsb[:, 1536:2048].rearrange("p (b d) -> p b d", b=2),
                 mub[:, 0:512].rearrange("p (b d) -> p b d", b=2)]
        kb_cm2 = mub[:, 512:768]
        vb_cm2 = mub[:, 768:1024]
        ALT2 = [mub[:, 1024:1152], mub[:, 1152:1280]]
        ve.op("memset", S_bf2[1][:, :, :], 0.0, deps=[(te, te.cnt)])
        v_s2 = ve.cnt
        v_sbf = [v_s, v_s2]
        v_upre = [0, 0]
        v_wneg = [0, 0]
        v_alt = [0, 0]
        v_u = [0, 0]

        def scan_pre(jj):
            p = jj % 2
            csl = slice(P * jj, P * (jj + 1))
            for mb in range(2):
                te.mm(PS[:, 3, 256:384], Bk[:, mb, csl], Bq[:, mb, csl],
                      start=(mb == 0), stop=(mb == 1),
                      deps=[(ve, bank_ve[3]), (ve, v_kbvb)])
            for mb in range(2):
                te.mm(PS[:, 2, P * mb:P * (mb + 1)],
                      Bk[:, mb, csl], c["identb"][:, :],
                      start=True, stop=True,
                      deps=[(ve, bank_ve[2])])
                te.mm(PS[:, 2, 256 + P * mb:256 + P * (mb + 1)],
                      kb_fm[:, mb, csl], c["identb"][:, :],
                      start=True, stop=True, inc=(mb == 1))
                te.mm(PS[:, 3, P * mb:P * (mb + 1)],
                      vb_fm[:, mb, csl], c["identb"][:, :],
                      start=True, stop=True, inc=(mb == 1))
            t_t = te.cnt
            ve.op("tensor_copy", kn2[p][:, :], PS[:, 2, 0:256],
                  deps=[(te, t_t)])
            ve.op("tensor_copy", kb_cm2[:, :], PS[:, 2, 256:512])
            bank_ve[2] = ve.cnt
            ve.op("tensor_copy", vb_cm2[:, :], PS[:, 3, 0:256])
            ve.op("tensor_mul", ALT2[p][:, :], PS[:, 3, 256:384],
                  mskL[:, :], deps=[(gc, aux["g_msk"])])
            bank_ve[3] = ve.cnt
            v_alt[p] = ve.cnt
            v_evac1 = ve.cnt
            for mb in range(2):
                te.mm(PS[:, 4, P * mb:P * (mb + 1)],
                      kb_cm2[:, P * mb:P * (mb + 1)], XTb[:, jj, :],
                      start=True, stop=True,
                      deps=[(ve, v_evac1), (ve, bank_ve[4]), (ve, vXdone)],
                      inc=(mb == 1))
            t_w = te.cnt
            ve.op("tensor_scalar_mul", wneg2[p][:, :], PS[:, 4, 0:256],
                  -1.0, deps=[(te, t_w)])
            bank_ve[4] = ve.cnt
            v_wneg[p] = ve.cnt
            te.mm(PS[:, 6 + p, 0:256], XTb[:, jj, :], vb_cm2[:, :],
                  start=True, stop=False,
                  deps=[(ve, v_evac1), (ve, bank_ve[6 + p]), (ve, vXdone),
                        (se, bank_se[6 + p])])

        def scan_chain(jj):
            p = jj % 2
            for mb in range(2):
                te.mm(PS[:, 6 + p, 0:256],
                      wneg2[p][:, P * mb:P * (mb + 1)],
                      S_bf2[1 - p][:, mb, :], start=False, stop=(mb == 1),
                      deps=[(ve, v_wneg[p]), (ve, v_sbf[1 - p])],
                      inc=(mb == 1))
            t_u = te.cnt
            ve.op("tensor_copy", u2[p][:, :], PS[:, 6 + p, 0:256],
                  deps=[(te, t_u)])
            bank_ve[6 + p] = ve.cnt
            v_u[p] = ve.cnt
            for mb in range(2):
                te.mm(PS[:, 1, 256 * mb:256 * (mb + 1)],
                      kn2[p][:, P * mb:P * (mb + 1)], u2[p][:, :],
                      start=True, stop=True,
                      deps=[(ve, v_u[p]), (ve, bank_ve[1])],
                      inc=(mb == 1))
            t_s = te.cnt
            ve.op("tensor_add", S_bf2[p][:, :, :], S0[:, :, :],
                  PS[:, 1, :].rearrange("p (b d) -> p b d", b=2),
                  deps=[(te, t_s)])
            v_sbf[p] = ve.cnt
            ve.op("tensor_add", S0[:, :, :], S0[:, :, :],
                  PS[:, 1, :].rearrange("p (b d) -> p b d", b=2))
            bank_ve[1] = ve.cnt

        def scan_o(jj):
            p = jj % 2
            csl = slice(P * jj, P * (jj + 1))
            for dvb in range(2):
                for mb in range(2):
                    te.mm(PS[:, 0, P * dvb:P * (dvb + 1)],
                          S_bf2[1 - p][:, mb, P * dvb:P * (dvb + 1)],
                          Bq[:, mb, csl], start=(mb == 0), stop=False,
                          deps=[(ve, bank_ve[0]), (ve, v_sbf[1 - p]),
                                (ve, v_u[p]), (ve, v_alt[p])])
                te.mm(PS[:, 0, P * dvb:P * (dvb + 1)],
                      u2[p][:, P * dvb:P * (dvb + 1)], ALT2[p][:, :],
                      start=False, stop=True, inc=(dvb == 1))
            t_o = te.cnt
            ve.op("tensor_copy", delta[:, :, csl],
                  PS[:, 0, 0:256].rearrange("p (b q) -> p b q", b=2),
                  deps=[(te, t_o)])
            bank_ve[0] = ve.cnt

        scan_pre(0)
        scan_pre(1)
        for jj in range(16):
            scan_chain(jj)
            scan_o(jj)
            if jj + 2 < 16:
                scan_pre(jj + 2)
        v_delta = ve.cnt
        ar.pop()

        ar.push()
        W1q = ar.alloc3(NB, 256)
        stats16 = ar.alloc(T)
        p_tm = ar.alloc(64).rearrange("p (tt j) -> p tt j", tt=16)
        h1t = ar.alloc(512)
        W1B = aux["wsl_flat"][0:16, 0:1024]
        w1b_tok = ctx.qg.dma(W1B, dr["g_w1"][D:D + 16, :],
                             deps=[(te, te.cnt)])

        for mb in range(2):
            gb = 2 * h + mb
            ve.op("tensor_scalar_mul", Bq[:, mb, :], Bv[:, mb, 60:60 + T],
                  FIRS[:, gb, 0:1], deps=[(ve, v_delta), sm_tok])
            for j in range(1, 5):
                ve.op("scalar_tensor_tensor", Bq[:, mb, :],
                      Bv[:, mb, 60 + j:60 + j + T], FIRS[:, gb, j:j + 1],
                      Bq[:, mb, :], ALU.mult, ALU.add)
        v_fs = ve.cnt
        half = [[sq[:, 128 * jj:128 * (jj + 1)] for jj in range(8)],
                [sq[:, 1024 + 128 * jj:1024 + 128 * (jj + 1)]
                 for jj in range(8)]]
        for mb in range(2):
            gb = 2 * h + mb
            half_war = [te.cnt, te.cnt]
            for jg in range(8):
                par = jg % 2
                gdeps = [sm_tok, aux["firl"], (te, half_war[par]),
                         (ve, v_delta)]
                for jj in range(8):
                    j = 8 * jg + jj
                    gc.op("affine_select", half[par][jj],
                          FIRL[:, gb, j:j + 1].broadcast_to([P, P]),
                          [[1, P]], ALU.is_equal, 0.0, base=0,
                          channel_multiplier=-1, deps=gdeps)
                gcd = gc.cnt
                for jj in range(8):
                    j = 8 * jg + jj
                    for tc in range(4):
                        bank = 4 + tc
                        te.mm(PS[:, bank, :], half[par][jj],
                              Bv[:, mb, 1 + j + 512 * tc:
                                 1 + j + 512 * (tc + 1)],
                              start=(j == 0), stop=(j == 63),
                              deps=[(gc, gcd), (ve, bank_ve[bank]),
                                    (se, bank_se[bank])],
                              inc=(jj == 7 and tc == 3))
                half_war[par] = te.cnt
            tfl = te.cnt
            for tc in range(4):
                bank = 4 + tc
                ve.op("tensor_copy", Bk[:, mb, 512 * tc:512 * (tc + 1)],
                      PS[:, bank, :], deps=[(te, tfl)])
                bank_ve[bank] = ve.cnt
        v_fl = ve.cnt

        st_toks = []
        r_mean = rows[0:1, 0:T]
        r_var = rows[0:1, T:2 * T]
        r_abs = rows[32:33, 0:T]
        r_l2 = rows[32:33, T:2 * T]
        sqb = [sq, mub]
        st_row_war = []
        t3_prev = te.cnt
        for si, (src2, off) in enumerate(
                ((Bq, 0), (Bk, 0), (delta, 0), (Bv, 64))):
            sdep = [(ve, v_fl), (ve, v_fs)]
            for tc in range(4):
                for mb in range(2):
                    te.mm(PS[0:1, 4 + tc, :], c["onescolb"][:, :],
                          src2[:, mb, off + 512 * tc:off + 512 * (tc + 1)],
                          start=(mb == 0), stop=(mb == 1),
                          deps=sdep + [(ve, bank_ve[4 + tc]),
                                       (se, bank_se[4 + tc])],
                          inc=(mb == 1 and tc == 3))
            t1 = te.cnt
            for mb in range(2):
                se.op("activation", sqb[mb][:, :],
                      src2[:, mb, off:off + T], AF.Square,
                      deps=sdep + [(te, t3_prev)])
            s_sq = se.cnt
            for tc in range(4):
                for mb in range(2):
                    te.mm(PS[32:33, 4 + tc, :], c["onescolb"][:, :],
                          sqb[mb][:, 512 * tc:512 * (tc + 1)],
                          start=(mb == 0), stop=(mb == 1),
                          deps=[(se, s_sq)], inc=(mb == 1 and tc == 3))
            t2 = te.cnt
            for mb in range(2):
                se.op("activation", sqb[mb][:, :],
                      src2[:, mb, off:off + T], AF.Abs, deps=[(te, t2)])
            s_ab = se.cnt
            for tc in range(4):
                for mb in range(2):
                    te.mm(PS[64:65, 4 + tc, :], c["onescolb"][:, :],
                          sqb[mb][:, 512 * tc:512 * (tc + 1)],
                          start=(mb == 0), stop=(mb == 1),
                          deps=[(se, s_ab)], inc=(mb == 1 and tc == 3))
            t3 = te.cnt
            t3_prev = t3
            s1 = PS[0:1, 4:8, :].rearrange("p a b -> p (a b)")
            s2 = PS[32:33, 4:8, :].rearrange("p a b -> p (a b)")
            s3 = PS[64:65, 4:8, :].rearrange("p a b -> p (a b)")
            ve.op("tensor_scalar_mul", r_mean, s1, 1.0 / 256,
                  deps=[(te, t1)] + st_row_war)
            ve.op("tensor_mul", r_var, r_mean, r_mean, deps=[(te, t2)])
            ve.op("scalar_tensor_tensor", r_var, s2, 1.0 / 256, r_var,
                  ALU.mult, ALU.subtract)
            ve.op("tensor_scalar_mul", r_abs, s3, 1.0 / 256,
                  deps=[(te, t3)])
            v_rows = ve.cnt
            se.op("activation", r_l2, s2, AF.Sqrt,
                  deps=[(te, t2), (ve, v_rows)] + st_row_war)
            s_rows = se.cnt
            for tc in range(4):
                bank_ve[4 + tc] = v_rows
                bank_se[4 + tc] = s_rows
            for j, (rsrc, dep) in enumerate(((r_mean, (ve, v_rows)),
                                             (r_var, (ve, v_rows)),
                                             (r_abs, (ve, v_rows)),
                                             (r_l2, (se, s_rows)))):
                tok = qg.dma(stats16[4 * si + j:4 * si + j + 1, :],
                             rsrc, deps=[dep], pool="aux")
                st_toks.append(tok)
            st_row_war = st_toks[-4:]

        ve.op("tensor_add", BB[0:4, 5:6], BB[0:4, 4:5], BB[0:4, h:h + 1],
              deps=[sm_tok])
        v_bb = ve.cnt
        lg_fm = rows[0:4, T:2 * T]
        lg_war = [se.cnt, se.cnt, se.cnt, se.cnt]
        for qd in range(4):
            w1_ld = wload3(ctx, W1q, dr["g_w1"][0:D, 256 * qd:256 * (qd + 1)],
                           deps=[(te, te.cnt)])
            for fb2 in range(2):
                fblk = 2 * qd + fb2
                for kb in range(NB):
                    for tc in range(4):
                        te.mm(PS[:, tc, :],
                              W1q[:, kb, 128 * fb2:128 * (fb2 + 1)],
                              xn1[:, kb, 512 * tc:512 * (tc + 1)],
                              start=(kb == 0), stop=False,
                              deps=[w1_ld, w1b_tok, (se, bank_se[tc]),
                                    (ve, bank_ve[tc]), (ve, v_xn1)] + st_toks,
                              inc=False)
                for tc in range(4):
                    te.mm(PS[:, tc, :],
                          W1B[0:16, 128 * fblk:128 * (fblk + 1)],
                          stats16[0:16, 512 * tc:512 * (tc + 1)],
                          start=False, stop=True, inc=True)
                tcnt = te.cnt
                for tc in range(4):
                    se.op("activation", h1t[:, :], PS[:, tc, :], AF.Gelu,
                          bias=SM[:, fblk:fblk + 1],
                          deps=[(te, tcnt), (te, te.cnt)])
                    scnt = se.cnt
                    bank_se[tc] = scnt
                    te.mm(PS[0:4, 4 + tc, :], W2S[:, fblk, :], h1t[:, :],
                          start=(fblk == 0), stop=(fblk == 7),
                          deps=[(se, scnt), w2s_tok, (se, lg_war[tc]),
                                (ve, bank_ve[4 + tc])],
                          inc=True)
                    if fblk == 7:
                        se.op("activation", lg_fm[:, 512 * tc:512 * (tc + 1)],
                              PS[0:4, 4 + tc, :], AF.Copy,
                              deps=[(te, te.cnt)])
                        lg_war[tc] = se.cnt
                        bank_se[4 + tc] = se.cnt
        v_lg_se = se.cnt
        ve.op("tensor_scalar_add", lg_fm[:, :], lg_fm[:, :], BB[0:4, 5:6],
              deps=[(se, v_lg_se), (ve, v_bb)])
        v_lg = ve.cnt

        te.mm(PS[0:4, 2, 0:1], c["ones1f"][0:1, 0:4], SR[0:1, 8 + h:9 + h],
              start=True, stop=True,
              deps=[(ve, v_itemp), (ve, bank_ve[2])], inc=True)
        tid = te.cnt
        ve.op("tensor_scalar_mul", BB[0:4, 6:7], PS[0:4, 2, 0:1], 1.0,
              deps=[(te, tid)])
        ve.op("tensor_scalar_mul", idt4[:, :], c["identb"][0:4, 0:4],
              BB[0:4, 6:7])
        v_idt = ve.cnt
        bank_ve[2] = v_idt

        for tg in range(4):
            for k in range(4):
                tt = 4 * tg + k
                te.mm(PS[:, 3, 4 * k:4 * k + 4],
                      lg_fm[:, P * tt:P * (tt + 1)], idt4[:, :],
                      start=True, stop=True,
                      deps=[(ve, v_lg), (ve, v_idt), (ve, bank_ve[3])],
                      inc=(k == 3))
            tcnt = te.cnt
            g16 = PS[:, 3, 0:16].rearrange("p (a b) -> p a b", b=4)
            ve.op("tensor_reduce", stm[:, 0:4], g16, AX.X, ALU.max,
                  deps=[(te, tcnt)])
            ve.op("tensor_sub", stm[:, 16:32].rearrange(
                "p (a b) -> p a b", b=4), g16,
                stm[:, 0:4].rearrange("p (a b) -> p a b", b=1
                                      ).broadcast_to([P, 4, 4]))
            vsm = ve.cnt
            se.op("activation", stm[:, 16:32], stm[:, 16:32], AF.Exp,
                  deps=[(ve, vsm)])
            ssm = se.cnt
            bank_ve[3] = ve.cnt
            e16 = stm[:, 16:32].rearrange("p (a b) -> p a b", b=4)
            ve.op("tensor_reduce", stm[:, 0:4], e16, AX.X, ALU.add,
                  deps=[(se, ssm)])
            ve.op("reciprocal", stm[:, 4:8], stm[:, 0:4])
            ve.op("tensor_mul", e16, e16,
                  stm[:, 4:8].rearrange("p (a b) -> p a b", b=1
                                        ).broadcast_to([P, 4, 4]))
            ve.op("tensor_max", stm[:, 16:32], stm[:, 16:32], FLOOR[:, :],
                  deps=[(ve, v_floor)])
            ve.op("tensor_reduce", stm[:, 0:4], e16, AX.X, ALU.add)
            ve.op("reciprocal", stm[:, 4:8], stm[:, 0:4])
            ve.op("tensor_mul", p_tm[:, 4 * tg:4 * tg + 4, :], e16,
                  stm[:, 4:8].rearrange("p (a b) -> p a b", b=1
                                        ).broadcast_to([P, 4, 4]))
        v_ptm = ve.cnt

        prow4 = rows[0:4, 0:T]
        for tt in range(16):
            te.mm(PS[0:4, 2, P * (tt % 4):P * (tt % 4) + P]
                  if False else PS[0:4, 2, 0:P],
                  p_tm[:, tt, :], c["identb"][:, :],
                  start=True, stop=True,
                  deps=[(ve, v_ptm), (se, se.cnt), (ve, bank_ve[2])],
                  inc=True)
            tcnt = te.cnt
            se.op("activation", prow4[:, P * tt:P * (tt + 1)],
                  PS[0:4, 2, 0:P], AF.Copy, deps=[(te, tcnt)])
            bank_ve[2] = ve.cnt
        s_prow = se.cnt
        te.mm(PS[:, 2, 0:1], c["ones1f"][:, :], SR[0:1, 12 + h:13 + h],
              start=True, stop=True, deps=[(ve, v_crl), (se, s_prow)],
              inc=True)
        tcnt = te.cnt
        ve.op("tensor_scalar_mul", ctx.fcol[:, 15:16], PS[:, 2, 0:1], 1.0,
              deps=[(te, tcnt)])
        v_crlc = ve.cnt
        bank_ve[2] = v_crlc

        tmp = h1t
        srcs = [(Bq, 0), (Bk, 0), (delta, 0), (Bv, 64)]
        comb_war = ve.cnt
        for j in range(4):
            for tc in range(4):
                bank = j % 2
                te.mm(PS[:, bank, :], self_j[0:4, P * j:P * (j + 1)],
                      prow4[:, 512 * tc:512 * (tc + 1)],
                      start=True, stop=True,
                      deps=[(se, s_prow), (ve, bank_ve[bank]),
                            (gc, aux["g_msk"])], inc=True)
                tcnt = te.cnt
                src2, off = srcs[j]
                for mb in range(2):
                    if j == 0:
                        ve.op("tensor_mul",
                              o_h[:, mb, 512 * tc:512 * (tc + 1)],
                              src2[:, mb, off + 512 * tc:off + 512 * (tc + 1)],
                              PS[:, bank, :], deps=[(te, tcnt), (ve, comb_war)])
                    else:
                        ve.op("tensor_mul", tmp[:, :],
                              src2[:, mb, off + 512 * tc:off + 512 * (tc + 1)],
                              PS[:, bank, :], deps=[(te, tcnt)])
                        ve.op("tensor_add",
                              o_h[:, mb, 512 * tc:512 * (tc + 1)],
                              o_h[:, mb, 512 * tc:512 * (tc + 1)], tmp[:, :])
                bank_ve[bank] = ve.cnt
        for mb in range(2):
            ve.op("scalar_tensor_tensor", o_h[:, mb, :], Bq[:, mb, :],
                  ctx.fcol[:, 15:16], o_h[:, mb, :], ALU.mult, ALU.add,
                  deps=[(ve, v_crlc)])
        v_comb = ve.cnt

        sqb2 = [sq, mub]
        for mb in range(2):
            se.op("activation", sqb2[mb][:, :], o_h[:, mb, :],
                  AF.Square, deps=[(ve, v_comb), (te, te.cnt)])
        s_rs = se.cnt
        for tt in range(16):
            for mb in range(2):
                te.mm(PS[:, 6, tt:tt + 1], sqb2[mb][:, P * tt:P * (tt + 1)],
                      c["onescolb"][:, :], start=(mb == 0), stop=(mb == 1),
                      deps=[(se, s_rs), (ve, bank_ve[6])],
                      inc=(tt == 15 and mb == 1))
        trs = te.cnt
        ve.op("tensor_scalar_mul", stm[:, 0:16], PS[:, 6, 0:16], 1.0 / 256,
              deps=[(te, trs)])
        bank_ve[6] = ve.cnt
        vv = ve.cnt
        se.op("activation", stm[:, 16:32], stm[:, 0:16], AF.Sqrt,
              bias=c["cpool"][:, 0:1], deps=[(ve, vv)])
        srs = se.cnt
        ve.op("reciprocal", stm[:, 0:16], stm[:, 16:32], deps=[(se, srs)])
        v_rstd = ve.cnt
        te.mm(PS[0:16, 2, 0:P], stm[:, 0:16], c["identb"][:, :],
              start=True, stop=True,
              deps=[(ve, v_rstd), (ve, bank_ve[2])], inc=True)
        tcnt = te.cnt
        ve.op("tensor_copy", st16[:, :], PS[0:16, 2, 0:P],
              deps=[(te, tcnt)] + st16_war)
        bank_ve[2] = ve.cnt
        v16 = ve.cnt
        rrow = rows[0:1, 0:T]
        rs_tok = qg.dma(rrow.rearrange("o (a b) -> o a b", a=16), st16[:, :],
                        deps=[(ve, v16)], pool="aux")
        st16_war = [rs_tok]
        for tc in range(4):
            te.mm(PS[:, tc, :], c["ones1b"][:, :],
                  rrow[0:1, 512 * tc:512 * (tc + 1)],
                  start=True, stop=True,
                  deps=[rs_tok, (ve, bank_ve[tc])], inc=True)
        tbc = te.cnt
        for tc in range(4):
            ve.op("tensor_copy", mub[:, 512 * tc:512 * (tc + 1)],
                  PS[:, tc, :], deps=[(te, tbc)])
            bank_ve[tc] = ve.cnt
        for mb in range(2):
            ve.op("scalar_tensor_tensor", o_h[:, mb, :], o_h[:, mb, :],
                  SM[:, 8 + mb:9 + mb], mub[:, :], ALU.mult, ALU.mult)
        v_on = ve.cnt
        s_on = se.cnt

        wo_war = te.cnt
        for quarter in range(8):
            wo_ld = wload3(ctx, wsl[:, 0:2, 0:P],
                           dr["wo1"][256 * h:256 * (h + 1),
                                     P * quarter:P * (quarter + 1)],
                           deps=[(te, wo_war)])
            mbk = quarter
            for tc in range(4):
                bank = tc % 2
                for kb in range(2):
                    te.mm(PS[:, bank, :], wsl[:, kb, 0:P],
                          o_h[:, kb, 512 * tc:512 * (tc + 1)],
                          start=(kb == 0), stop=(kb == 1),
                          deps=[wo_ld, (ve, v_on), (se, s_on),
                                (ve, bank_ve[bank]), (se, bank_se[bank])],
                          inc=(kb == 1))
                tcnt = te.cnt
                ve.op("tensor_add", RES[:, mbk, 512 * tc:512 * (tc + 1)],
                      RES[:, mbk, 512 * tc:512 * (tc + 1)], PS[:, bank, :],
                      deps=[(te, tcnt)])
                bank_ve[bank] = ve.cnt
            wo_war = te.cnt
        ar.pop()
        ar.pop()



def host_layout(inputs, core):
    g = lambda k: np.asarray(inputs[k])
    fmv = lambda v: v.reshape(NB, P).T
    m = {}
    m["xT"] = g("L0_attn_x")[core].T
    wqkvT = g("L0_attn_W_qkv").T.reshape(D, 16, 3, 64)
    m["w_qT"] = wqkvT[:, :, 0, :].reshape(D, D)
    m["w_kT"] = wqkvT[:, :, 1, :].reshape(D, D)
    m["w_vT"] = wqkvT[:, :, 2, :].reshape(D, D)
    bqkv = g("L0_attn_b_qkv").reshape(16, 3, 64)
    m["bq_a"] = fmv(bqkv[:, 0, :].reshape(-1))
    m["bk_a"] = fmv(bqkv[:, 1, :].reshape(-1))
    m["bv_a"] = bqkv[:, 2, :].reshape(-1, 1)
    m["w_oT"] = g("L0_attn_W_o").T
    m["b_o"] = fmv(g("L0_attn_b_o"))
    m["ln1_g"] = fmv(g("L0_ffn_ln1_g"))
    m["ln1_b"] = fmv(g("L0_ffn_ln1_b"))
    m["ln2_g"] = fmv(g("L0_ffn_ln2_g"))
    m["ln2_b"] = fmv(g("L0_ffn_ln2_b"))
    m["wq0"] = g("L0_ffn_Wq"); m["bq0"] = fmv(g("L0_ffn_bq"))
    m["wk0"] = g("L0_ffn_Wk"); m["bk0"] = fmv(g("L0_ffn_bk"))
    m["wv0"] = g("L0_ffn_Wv"); m["bv0"] = g("L0_ffn_bv").reshape(-1, 1)
    m["wp0"] = g("L0_ffn_Wp"); m["bp0"] = fmv(g("L0_ffn_bp"))
    m["gate_w0"] = g("L0_ffn_gate_W")
    m["gate_b0"] = g("L0_ffn_gate_b").reshape(1, -1)
    m["exp_w1"] = g("L0_ffn_exp_W1")
    m["exp_b1"] = g("L0_ffn_exp_b1").reshape(8, 32, P).transpose(0, 2, 1)
    m["exp_w2"] = g("L0_ffn_exp_W2"); m["exp_b2"] = g("L0_ffn_exp_b2")
    m["wq1"] = g("L1_attn_Wq"); m["wk1"] = g("L1_attn_Wk")
    m["wv1"] = g("L1_attn_Wv"); m["wb1"] = g("L1_attn_Wb")
    sw = lambda v, k: v.reshape(NB, P, k).transpose(1, 0, 2)
    m["conv_q"] = sw(g("L1_attn_conv_q"), 4)
    m["conv_k"] = sw(g("L1_attn_conv_k"), 4)
    m["conv_v"] = sw(g("L1_attn_conv_v"), 4)
    m["fir_long"] = sw(g("L1_attn_fir_long").reshape(D, 64), 64)
    m["fir_short"] = sw(g("L1_attn_fir_short").reshape(D, 5), 5)
    m["g_w1"] = g("L1_attn_W1"); m["g_b1"] = fmv(g("L1_attn_b1"))
    m["g_w2"] = g("L1_attn_W2"); m["g_b2"] = g("L1_attn_b2").reshape(-1, 1)
    m["log_temp"] = g("L1_attn_log_temp").reshape(1, -1)
    m["base_bias"] = g("L1_attn_base_bias").T
    m["crl"] = g("L1_attn_conv_residual_logit").reshape(1, -1)
    m["rms_w"] = g("L1_attn_rms_w").reshape(2, P).T
    m["wo1"] = g("L1_attn_Wo")
    m["gate_w1"] = g("L1_ffn_gate_w").T
    m["m_w1"] = g("L1_ffn_w1")
    m["m_b1"] = g("L1_ffn_b1").reshape(8, 8, P).transpose(0, 2, 1)
    m["m_w2"] = g("L1_ffn_w2"); m["m_b2"] = g("L1_ffn_b2")
    return {k: np.ascontiguousarray(v, dtype=np.float32) for k, v in m.items()}


def kernel(**inputs):
    nc, dr = build_program(layers=(0, 1))
    in_names = set(dr.keys())
    maps = []
    for core in range(8):
        full = host_layout(inputs, core)
        maps.append({k: v for k, v in full.items() if k in in_names})
    res = run_bass_kernel_spmd(nc, maps, core_ids=list(range(8)))
    out = np.stack([np.ascontiguousarray(r["outT"].T) for r in res.results])
    return out.astype(np.float32)



def wload3(ctx, dst3, dram_ap, deps=()):
    return ctx.qg.dma(dst3, dram_ap.rearrange("(b p) f -> p b f", p=P), deps=deps)


def emit_mha(ctx, x3d, x_dep, w, causal, evac_out, name, NG=2,
             spill_o=False):
    te, ve, se, gc, qg = ctx.te, ctx.ve, ctx.se, ctx.gc, ctx.qg
    c = ctx.consts
    PS = ctx.PS
    rows = ctx.ln_scr["rows"]
    fcol = ctx.fcol

    ar = ctx.arena
    ar.push()
    x_dep = list(x_dep) + [(ctx.se, ctx.se.cnt), (ctx.ve, ctx.ve.cnt),
                           (ctx.gc, ctx.gc.cnt)]
    nbg = 1
    if spill_o:
        o_scr = ctx.nc.dram_tensor(name + "_oscr", [D, T], BF16,
                                   kind="Internal").ap()
        ostg = [ar.alloc(512) for _ in range(2)]
        ostg_tok = [None, None]
        ostg_all = {}
        O3 = None
    else:
        O3 = ar.alloc3(NB, T)
    qf = ar.alloc3(nbg, T)
    kf = ar.alloc3(nbg, T)
    vtm = ar.alloc(16 * NG * 65).rearrange("p (tt h e) -> p tt h e", tt=16, h=NG)
    sq1 = ctx.ln_scr["sq"]
    sq2 = ctx.ln_scr["mub"]
    rb = ctx.ln_scr["rsb"][:, 0:512]
    cmcol = ctx.cmcolb
    wqs = ar.alloc3(NB, NG * 64)
    wks = ar.alloc3(NB, NG * 64)
    wvs = ar.alloc3(NB, NG * 64)
    wos = wvs

    ve.op("memset", vtm[:, :, :, 64:65], 1.0)

    assert NG == 2
    n_groups = 16 // NG
    pt3 = [ctx.ln_scr["rsb"][:, 512:1536], ctx.ln_scr["mub"][:, 0:1024],
           sq1[:, 0:1024]]
    pt_war = [0, 0, 0]
    st_war = [0, 0]
    se0 = se.cnt
    bank_war = {b: 0 for b in range(8)}
    bank_wars = {b: 0 for b in range(8)}
    wbuf_war = {"q": 0, "k": 0, "v": 0}

    for g in range(n_groups):
        cs = slice(g * NG * 64, (g + 1) * NG * 64)
        wq_ld = wload3(ctx, wqs, w["wq"][:, cs], deps=[(te, wbuf_war["q"])])
        wk_ld = wload3(ctx, wks, w["wk"][:, cs], deps=[(te, wbuf_war["k"])])
        wv_ld = wload3(ctx, wvs, w["wv"][:, cs], deps=[(te, wbuf_war["v"])])

        for which, wt, dstf, ld, bfm in (("q", wqs, qf, wq_ld, w["bqf"]),
                                         ("k", wks, kf, wk_ld, w["bkf"])):
            for kb in range(NB):
                for tc in range(4):
                    te.mm(PS[:, 4 + tc, :], wt[:, kb, 0:P],
                          x3d[:, kb, 512 * tc:512 * (tc + 1)],
                          start=(kb == 0), stop=(kb == NB - 1),
                          deps=[ld, (ve, bank_war[4 + tc]),
                                (se, bank_wars[4 + tc])] + x_dep,
                          inc=(kb == NB - 1 and tc == 3))
            tcnt = te.cnt
            for tc in range(4):
                ve.op("tensor_scalar_add",
                      dstf[:, 0, 512 * tc:512 * (tc + 1)],
                      PS[:, 4 + tc, :], bfm[:, g:g + 1],
                      deps=[(te, tcnt)])
                bank_war[4 + tc] = ve.cnt
            wbuf_war[which] = te.cnt
        vqk = ve.cnt

        for tt in range(16):
            bank = 2 + (tt % 2)
            te.mm(PS[:, bank, 0:NG * 64], c["ones1b"][0:1, :],
                  w["bvr"][0:1, cs], start=True, stop=False,
                  deps=[wv_ld, (ve, bank_war[bank]), (se, st_war[1]),
                        (ve, c["ready_ve"])] + x_dep)
            for kb in range(NB):
                te.mm(PS[:, bank, 0:NG * 64], x3d[:, kb, P * tt:P * (tt + 1)],
                      wvs[:, kb, :], start=False, stop=(kb == NB - 1),
                      inc=(kb == NB - 1))
            tcnt = te.cnt
            for hh in range(NG):
                ve.op("tensor_copy", vtm[:, tt, hh, 0:64],
                      PS[:, bank, hh * 64:hh * 64 + 64], deps=[(te, tcnt)])
            bank_war[bank] = ve.cnt
        wbuf_war["v"] = te.cnt

        sqs = {}
        for j, (srcf, dsts) in enumerate(((qf, sq1), (kf, sq2))):
            se.op("activation", dsts[:, :], srcf[:, 0, :], AF.Square,
                  deps=[(ve, vqk), (te, pt_war[1]), (te, pt_war[2])])
            sqs[j] = se.cnt
        for j, dsts in ((0, sq1), (1, sq2)):
            for tmb in range(16):
                te.mm(PS[:, 6 + j, tmb:tmb + 1],
                      dsts[:, P * tmb:P * (tmb + 1)],
                      c["onescolb"][:, :], start=True, stop=True,
                      deps=[(se, sqs[j]), (ve, bank_war[6 + j])],
                      inc=(tmb == 15))
        tq = te.cnt
        for j in range(2):
            ve.op("reduce_max", cmcol[:, j:j + 1], PS[:, 6 + j, 0:16], AX.X,
                  deps=[(te, tq)])
            bank_war[6 + j] = ve.cnt
        vc = ve.cnt
        for j in range(2):
            te.mm(PS[0:1, 6 + j, 0:128], cmcol[:, j:j + 1], c["identb"][:],
                  start=True, stop=True,
                  deps=[(ve, vc), (ve, bank_war[6 + j]), (gc, c["ready_gc"])],
                  inc=True)
        tcnt = te.cnt
        for j in range(2):
            ve.op("reduce_max", rows[0:1, 1500 + j:1501 + j],
                  PS[0:1, 6 + j, 0:128], AX.X, deps=[(te, tcnt)])
            bank_war[6 + j] = ve.cnt
        vmx = ve.cnt
        se.op("activation", rows[0:1, 1500:1502], rows[0:1, 1500:1502],
              AF.Sqrt, deps=[(ve, vmx)])
        ssq = se.cnt
        ve.op("tensor_mul", rows[0:1, 1502:1503], rows[0:1, 1500:1501],
              rows[0:1, 1501:1502], deps=[(se, ssq)])
        vprod = ve.cnt
        te.mm(PS[:, 7, 0:1], c["ones1b"][:], rows[0:1, 1502:1503],
              start=True, stop=True,
              deps=[(ve, vprod), (ve, bank_war[7])], inc=True)
        tcnt = te.cnt
        ve.op("tensor_scalar_mul", fcol[:, 2 * g:2 * g + 1], PS[:, 7, 0:1],
              -0.125, deps=[(te, tcnt), (se, se0)])
        bank_war[7] = ve.cnt
        vcb = ve.cnt

        for qc in range(4):
            kt_hi = 16 if not causal else 4 * (qc + 1)
            pv_depq = {}
            for kt in range(kt_hi + 2):
                if kt < kt_hi:
                    par = kt % 3
                    sb = kt % 2
                    for hh in range(NG):
                        base = 64 * hh
                        bank = 2 * sb + hh
                        te.mm(PS[:, bank, :],
                              kf[base:base + 64, 0, P * kt:P * (kt + 1)],
                              qf[base:base + 64, 0, 512 * qc:512 * (qc + 1)],
                              start=True, stop=True,
                              deps=[(ve, vcb), (se, st_war[sb]),
                                    (ve, bank_war[bank])],
                              inc=True)
                    tcnt = te.cnt
                    se.op("activation", pt3[par][:, :],
                          PS[:, 2 * sb:2 * sb + 2, :].rearrange(
                              "p a b -> p (a b)"),
                          AF.Exp, bias=fcol[:, 2 * g:2 * g + 1],
                          scale=c["cpool"][:, 2:3],
                          deps=[(te, tcnt), (te, pt_war[par])])
                    scnt = se.cnt
                    st_war[sb] = scnt
                    for hh in range(NG):
                        dep_pv = (se, scnt)
                        if causal and kt >= 4 * qc:
                            gc.op("affine_select",
                                  pt3[par][:, 512 * hh:512 * (hh + 1)],
                                  pt3[par][:, 512 * hh:512 * (hh + 1)],
                                  [[1, 512]], ALU.is_ge,
                                  0.0, base=qc * 512 - kt * P,
                                  channel_multiplier=-1, deps=[(se, scnt)])
                            dep_pv = (gc, gc.cnt)
                        pv_depq.setdefault(kt, {})[hh] = dep_pv
                if kt >= 2:
                    kp = kt - 2
                    parp = kp % 3
                    for hh in range(NG):
                        te.mm(PS[0:65, 6 + hh, 0:512], vtm[:, kp, hh, :],
                              pt3[parp][:, 512 * hh:512 * (hh + 1)],
                              start=(kp == 0), stop=(kp == kt_hi - 1),
                              deps=[pv_depq[kp][hh],
                                    (ve, bank_war[6 + hh])],
                              inc=(kp == kt_hi - 1))
                        pt_war[parp] = te.cnt
            for hh in range(NG):
                h = g * NG + hh
                oblk, obase = h // 2, 64 * (h % 2)
                tpv = te.cnt
                ve.op("tensor_copy", rows[0:1, 512 * hh:512 * (hh + 1)],
                      PS[64:65, 6 + hh, :], deps=[(te, tpv)])
                vr_ = ve.cnt
                te.mm(PS[0:64, 4 + hh, :], c["ones1b"][0:1, 0:64],
                      rows[0:1, 512 * hh:512 * (hh + 1)],
                      start=True, stop=True,
                      deps=[(ve, vr_), (ve, bank_war[4 + hh])], inc=True)
                tcnt = te.cnt
                ve.op("reciprocal", rb[0:64, :], PS[0:64, 4 + hh, :],
                      deps=[(te, tcnt)])
                bank_war[4 + hh] = ve.cnt
                if spill_o:
                    par = (4 * h + qc) % 2
                    dep_st = [ostg_tok[par]] if ostg_tok[par] else []
                    ve.op("tensor_mul", ostg[par][0:64, :],
                          PS[0:64, 6 + hh, 0:512], rb[0:64, :], deps=dep_st)
                    ostg_tok[par] = ctx.qs.dma(
                        o_scr[64 * h:64 * h + 64, 512 * qc:512 * (qc + 1)],
                        ostg[par][0:64, :], deps=[(ve, ve.cnt)])
                    ostg_all[ostg_tok[par].sem.name] = ostg_tok[par]
                else:
                    ve.op("tensor_mul",
                          O3[obase:obase + 64, oblk, 512 * qc:512 * (qc + 1)],
                          PS[0:64, 6 + hh, 0:512], rb[0:64, :])
                bank_war[6 + hh] = ve.cnt
    vO = ve.cnt

    assert spill_o
    ord_toks = list(ostg_all.values())
    wo_buf = ar.alloc3(NB, 512)
    Ost = [ar.alloc3(NB, 512) for _ in range(2)]
    ost_tok = [None, None]
    ost_war = [0, 0]
    wo_war = wbuf_war["v"]
    for half in range(2):
        wo_ld = wload3(ctx, wo_buf,
                       w["wo"][:, 512 * half:512 * (half + 1)],
                       deps=[(te, wo_war)])
        for tc in range(4):
            par = tc % 2
            ost_tok[par] = ctx.qs.dma(
                Ost[par][:, :, :],
                o_scr[:, 512 * tc:512 * (tc + 1)].rearrange(
                    "(b p) t -> p b t", p=P),
                deps=[(te, ost_war[par])] + ord_toks)
            for q2 in range(4):
                quarter = 4 * half + q2
                bank = 4 + q2
                for kb in range(NB):
                    te.mm(PS[:, bank, :], wo_buf[:, kb, P * q2:P * (q2 + 1)],
                          Ost[par][:, kb, :],
                          start=(kb == 0), stop=(kb == NB - 1),
                          deps=[wo_ld, ost_tok[par], (ve, bank_war[bank]),
                                (se, bank_wars[bank])],
                          inc=(kb == NB - 1))
                evac_out(PS[:, bank, :], quarter, tc, te.cnt)
                bank_war[bank] = ve.cnt
            ost_war[par] = te.cnt
        wo_war = te.cnt
    ar.pop()
    return {"ve": ve.cnt, "te": te.cnt}



def emit_moe(ctx, x3d, x_dep, w1_d, b1_d, w2_d, b2_d, gw_d, gb_d, H,
             softmax_then_topk, w1bufs, res_dep, name):
    te, ve, se, gc, qs, qg = ctx.te, ctx.ve, ctx.se, ctx.gc, ctx.qs, ctx.qg
    c = ctx.consts
    PS = ctx.PS
    RES = ctx.RES
    x_dep = list(x_dep) + [(ctx.se, ctx.se.cnt), (ctx.ve, ctx.ve.cnt),
                           (ctx.gc, ctx.gc.cnt), (ctx.ve, res_dep)]
    NFC = H // 1024
    ar = ctx.arena
    ar.push()
    macc = ar.alloc(16 * 1024).rearrange("p (tt d) -> p tt d", tt=16)
    sq3 = ctx.ln_scr["sq"].rearrange("p (b t) -> p b t", b=4)
    mub3 = ctx.ln_scr["mub"].rearrange("p (b t) -> p b t", b=4)
    midv = lambda fb: (sq3 if fb % 2 == 0 else mub3)[:, fb // 2, :]
    w2buf = ar.alloc3(NB, 1024)
    gws = ar.alloc3(NB, 8)
    gbr = ar.alloc(32)
    b2r = ctx.rowbuf
    if not hasattr(ctx, "moe_sb"):
        ctx.moe_sb = dict(
            iota8=ctx.sb("moe_iota", [P, 8], F32),
            wf=ctx.sb("moe_wf", [P, 16, 8], F32),
            gsc=ctx.sb("moe_gsc", [P, 20], F32),
            i8=ctx.sb("moe_i8", [P, 8], U32),
            b1f=ctx.sb("moe_b1f", [P, 32], F32))
    iota8 = ctx.moe_sb["iota8"]
    wf = ctx.moe_sb["wf"]
    gsc = ctx.moe_sb["gsc"]
    i8 = ctx.moe_sb["i8"]
    b1f = ctx.moe_sb["b1f"]

    gc.op("iota", iota8[:, :], [[1, 8]], channel_multiplier=0,
          allow_small_or_imprecise_dtypes=True)
    g_iota = gc.cnt

    gw_ld = ctx.qg.dma(gws, gw_d.rearrange("(b p) f -> p b f", p=P),
                       pool="persist")
    gb_tok = gw_ld
    if gb_d is not None:
        gb_tok = ctx.qg.dma(gbr[0:1, 0:8], gb_d, pool="persist")

    gate_war = []
    for tt in range(16):
        if gb_d is not None:
            te.mm(PS[:, 6, 0:8], c["ones1b"][0:1, :], gbr[0:1, 0:8],
                  start=True, stop=False,
                  deps=[gw_ld, gb_tok, (ve, c["ready_ve"])] + x_dep + gate_war)
            first = False
        else:
            first = True
        for kb in range(NB):
            te.mm(PS[:, 6, 0:8], x3d[:, kb, P * tt:P * (tt + 1)],
                  gws[:, kb, :], start=(first and kb == 0),
                  stop=(kb == NB - 1),
                  deps=[gw_ld, gb_tok, (ve, c["ready_ve"])] + x_dep + gate_war,
                  inc=(kb == NB - 1))
        tcnt = te.cnt
        lg = gsc[:, 0:8]
        if softmax_then_topk:
            ve.op("reduce_max", gsc[:, 8:9], PS[:, 6, 0:8], AX.X,
                  negate=True, deps=[(te, tcnt)])
            se.op("activation", lg, PS[:, 6, 0:8], AF.Exp,
                  bias=gsc[:, 8:9], accum_out=gsc[:, 9:10],
                  deps=[(ve, ve.cnt), (te, tcnt)])
            ve.op("reciprocal", gsc[:, 10:11], gsc[:, 9:10],
                  deps=[(se, se.cnt)])
            ve.op("tensor_scalar_mul", lg, lg, gsc[:, 10:11])
        else:
            ve.op("tensor_copy", lg, PS[:, 6, 0:8], deps=[(te, tcnt)])
        vlg = ve.cnt
        gate_war = [(ve, ve.cnt), (se, se.cnt)]
        ve.op("max", gsc[:, 12:20], lg)
        ve.op("max_index", i8[:, :], gsc[:, 12:20], lg)
        ve.op("tensor_copy", gsc[:, 8:10], i8[:, 0:2])
        if softmax_then_topk:
            ve.op("tensor_copy", gsc[:, 10:11], gsc[:, 12:13])
            ve.op("tensor_copy", gsc[:, 11:12], gsc[:, 13:14])
            vw = ve.cnt
        else:
            ve.op("tensor_sub", gsc[:, 10:11], gsc[:, 12:13], gsc[:, 13:14])
            se.op("activation", gsc[:, 10:11], gsc[:, 10:11], AF.Sigmoid,
                  deps=[(ve, ve.cnt)])
            ve.op("tensor_scalar", gsc[:, 11:12], gsc[:, 10:11], -1.0, 1.0,
                  ALU.mult, ALU.add, deps=[(se, se.cnt)])
            vw = ve.cnt
        ve.op("scalar_tensor_tensor", wf[:, tt, :], iota8[:, :],
              gsc[:, 8:9], gsc[:, 10:11].broadcast_to([P, 8]),
              ALU.is_equal, ALU.mult, deps=[(gc, g_iota), (ve, vw)])
        ve.op("scalar_tensor_tensor", gsc[:, 12:20], iota8[:, :],
              gsc[:, 9:10], gsc[:, 11:12].broadcast_to([P, 8]),
              ALU.is_equal, ALU.mult)
        ve.op("tensor_add", wf[:, tt, :], wf[:, tt, :], gsc[:, 12:20])
    v_wf = ve.cnt
    bank_war = {b: 0 for b in range(8)}
    bank_war_se = {b: 0 for b in range(8)}
    bank_war[6] = v_wf

    w1_war = [te.cnt, te.cnt]
    w2_war = te.cnt
    b_war = se.cnt
    mid_se = {fb: 0 for fb in range(NB)}
    for e in range(8):
        b1_ld = qs.dma(b1f[:, 0:H // P], b1_d[e], deps=[(se, b_war)])
        b2_ld = qg.dma(b2r[0:1, 0:D], b2_d[e:e + 1, :], deps=[(te, w2_war)])
        for fc in range(NFC):
            wbi = (e * NFC + fc) % 2
            wb = w1bufs[wbi]
            w1_ld = wload3(ctx, wb, w1_d[e, :, 1024 * fc:1024 * (fc + 1)],
                           deps=[(te, w1_war[wbi])])
            w2_ld = wload3(ctx, w2buf,
                           w2_d[e, 1024 * fc:1024 * (fc + 1), :],
                           deps=[(te, w2_war)])
            for ch in range(4):
                t0 = 512 * ch
                for fb in range(NB):
                    bank = fb % 4
                    for kb in range(NB):
                        te.mm(PS[:, bank, :],
                              wb[:, kb, P * fb:P * (fb + 1)],
                              x3d[:, kb, t0:t0 + 512],
                              start=(kb == 0), stop=(kb == NB - 1),
                              deps=[w1_ld, (se, bank_war_se[bank])] + x_dep,
                              inc=(kb == NB - 1))
                    tcnt = te.cnt
                    se.op("activation", midv(fb)[:, :],
                          PS[:, bank, :], AF.Gelu,
                          bias=b1f[:, 8 * fc + fb:8 * fc + fb + 1],
                          deps=[(te, tcnt), b1_ld])
                    bank_war_se[bank] = se.cnt
                    mid_se[fb] = se.cnt
                for ts in range(4):
                    tt = 4 * ch + ts
                    for n in range(2):
                        ybank = 4 + 2 * (ts % 2) + n
                        if fc == 0:
                            te.mm(PS[:, ybank, :], c["ones1b"][0:1, :],
                                  b2r[0:1, 512 * n:512 * (n + 1)],
                                  start=True, stop=False,
                                  deps=[b2_ld, (ve, bank_war[ybank])])
                    for fb in range(NB):
                        for n in range(2):
                            ybank = 4 + 2 * (ts % 2) + n
                            te.mm(PS[:, ybank, :],
                                  midv(fb)[:, P * ts:P * (ts + 1)],
                                  w2buf[:, fb, 512 * n:512 * (n + 1)],
                                  start=(fc != 0 and fb == 0),
                                  stop=(fb == NB - 1),
                                  deps=[w2_ld, (se, mid_se[fb]),
                                        (ve, bank_war[ybank])],
                                  inc=(fb == NB - 1 and n == 1))
                    tcnt = te.cnt
                    for n in range(2):
                        ybank = 4 + 2 * (ts % 2) + n
                        dst = macc[:, tt, 512 * n:512 * (n + 1)]
                        if e == 0 and fc == 0:
                            ve.op("tensor_scalar_mul", dst,
                                  PS[:, ybank, :], wf[:, tt, e:e + 1],
                                  deps=[(te, tcnt), (ve, v_wf)])
                        else:
                            ve.op("scalar_tensor_tensor", dst,
                                  PS[:, ybank, :], wf[:, tt, e:e + 1],
                                  dst, ALU.mult, ALU.add,
                                  deps=[(te, tcnt)])
                        bank_war[ybank] = ve.cnt
            w1_war[wbi] = te.cnt
            w2_war = te.cnt
        b_war = se.cnt
    v_macc = ve.cnt

    ctx.res_tt = getattr(ctx, "res_tt", {})
    for tt in range(16):
        for db in range(NB):
            bank = (tt * NB + db) % 2
            te.mm(PS[:, bank, 0:128], macc[:, tt, P * db:P * (db + 1)],
                  c["identb"][:], start=True, stop=True,
                  deps=[(ve, v_macc), (se, bank_war_se[bank]),
                        (ve, bank_war[bank]), (gc, c["ready_gc"])],
                  inc=True)
            tcnt = te.cnt
            ve.op("tensor_add", RES[:, db, P * tt:P * (tt + 1)],
                  RES[:, db, P * tt:P * (tt + 1)], PS[:, bank, 0:128],
                  deps=[(te, tcnt), (ve, res_dep)])
            bank_war[bank] = ve.cnt
        ctx.res_tt[tt] = ve.cnt
    ar.pop()
    return ve.cnt


def run_layer1_build(taps=()):
    return build_program(layers=(1,), taps=taps)


# version 36
# speedup vs baseline: 1.0455x; 1.0288x over previous
import contextlib
import numpy as np
import ml_dtypes
import concourse.bass as bass
import concourse.mybir as mybir
from concourse.bass_utils import run_bass_kernel_spmd

F32 = mybir.dt.float32
BF16 = mybir.dt.bfloat16
U32 = mybir.dt.uint32
AF = mybir.ActivationFunctionType
ALU = mybir.AluOpType
AX = mybir.AxisListType

P = 128
T = 2048
D = 1024
NB = D // P



class Sem:
    def __init__(self, nc, es, name):
        self.h = es.enter_context(nc.semaphore(name))
        self.name = name


class Seq:

    def __init__(self, sem, autosync):
        self.raw = None
        self.sem = sem
        self.cnt = 0
        self.autosync = autosync
        self.waited = {}

    def wait(self, sem, val):
        if val <= 0:
            return
        if self.waited.get(sem.name, -1) >= val:
            return
        self.raw.wait_ge(sem.h, val)
        self.waited[sem.name] = val

    def dep(self, deps):
        for d in deps:
            if isinstance(d, Tok):
                i = d.sem._pool_i
                self.wait(d.sem, d.val)
                d.sem._pool_q.consumers[i].append((self, self.cnt + 1))
            else:
                s, v = d
                self.wait(s.sem if isinstance(s, (Seq, Q)) else s, v)

    def op(self, name, *a, deps=(), **kw):
        self.dep(deps)
        if self.autosync and self.cnt:
            self.wait(self.sem, self.cnt)
        inst = getattr(self.raw, name)(*a, **kw)
        inst.then_inc(self.sem.h, 1)
        self.cnt += 1
        return self.cnt

    def mm(self, out, lhsT, rhs, start, stop, deps=(), inc=False, **kw):
        self.dep(deps)
        inst = self.raw.matmul(out, lhsT, rhs, start=start, stop=stop, **kw)
        if inc:
            inst.then_inc(self.sem.h, 1)
            self.cnt += 1
        return self.cnt


class Tok:

    def __init__(self, sem, val):
        self.sem = sem
        self.val = val


class Q:

    def __init__(self, nc, es, name, raw_get, npool, naux=0, npers=0):
        self.raw = None
        self._raw_get = raw_get
        n = npool + naux + npers
        self.sems = [Sem(nc, es, f"{name}{i}") for i in range(n)]
        self.tot = [0] * n
        self.consumers = [[] for _ in range(n)]
        self.idx = 0
        self.npool = npool
        self.aux_lo = npool
        self.aux_n = naux
        self.aux_idx = 0
        self.pers_lo = npool + naux
        self.pers_n = npers
        self.pers_idx = 0
        self.waited = {}
        self.cur_batch = None

    def wait(self, sem, val):
        if val <= 0:
            return
        if self.waited.get(sem.name, -1) >= val:
            return
        self.raw.wait_ge(sem.h, val)
        self.waited[sem.name] = val

    def dep(self, deps):
        for d in deps:
            if isinstance(d, Tok):
                i = d.sem._pool_i
                self.wait(d.sem, d.val)
                d.sem._pool_q.consumers[i].append((self, None))
            else:
                s, v = d
                self.wait(s.sem if isinstance(s, (Seq, Q)) else s, v)

    def open_batch(self, pool="main"):
        self.cur_batch = self._next_tok(pool)
        return self.cur_batch

    def close_batch(self):
        t = self.cur_batch
        self.cur_batch = None
        return t

    def _next_tok(self, pool="main"):
        if pool == "main":
            i = self.idx % self.npool
            self.idx += 1
        elif pool == "aux":
            i = self.aux_lo + self.aux_idx % self.aux_n
            self.aux_idx += 1
        else:
            assert self.pers_idx < self.pers_n, "persist sems exhausted"
            i = self.pers_lo + self.pers_idx
            self.pers_idx += 1
        sem = self.sems[i]
        sem._pool_i = i
        sem._pool_q = self
        gate = self.consumers[i]
        self.consumers[i] = []
        for c in gate:
            if isinstance(c[0], Seq):
                self.wait(c[0].sem, c[1])
            elif isinstance(c[0], Q):
                pass
        return Tok(sem, self.tot[i])

    def dma(self, out, in_, deps=(), batch=None, pool="main", **kw):
        self.dep(deps)
        if batch is None and self.cur_batch is not None:
            batch = self.cur_batch
        tok = batch if batch is not None else self._next_tok(pool)
        i = tok.sem._pool_i
        self.raw.dma_start(out=out, in_=in_, **kw).then_inc(tok.sem.h, 16)
        self.tot[i] += 16
        tok.val = self.tot[i]
        return tok


class Ctx:
    def __init__(self, nc, es, taps=()):
        self.nc = nc
        self.es = es
        self.taps = set(taps)
        self.tap_list = []
        self.s_te = Sem(nc, es, "s_te")
        self.s_ve = Sem(nc, es, "s_ve")
        self.s_se = Sem(nc, es, "s_se")
        self.s_gc = Sem(nc, es, "s_gc")
        self.te = Seq(self.s_te, autosync=False)
        self.ve = Seq(self.s_ve, autosync=True)
        self.se = Seq(self.s_se, autosync=True)
        self.gc = Seq(self.s_gc, autosync=True)
        self.qs = Q(nc, es, "qs", None, 6, naux=0, npers=6)
        self.qg = Q(nc, es, "qg", None, 10, naux=20, npers=12)

    def bind(self):
        nc = self.nc
        self.te.raw = nc.tensor
        self.ve.raw = nc.vector
        self.se.raw = nc.scalar
        self.gc.raw = nc.gpsimd
        self.qs.raw = nc.sync
        self.qg.raw = nc.gpsimd

    def sb(self, name, shape, dtype):
        return self.es.enter_context(self.nc.sbuf_tensor(name, shape, dtype))

    def tap(self, name, src_ap, deps, shape, rearr=None):
        if name not in self.taps:
            return
        t = self.nc.dram_tensor("tap_" + name, shape, F32,
                                kind="ExternalOutput").ap()
        dst = t if rearr is None else t.rearrange(*rearr[0], **rearr[1])
        self.qg.dma(dst, src_ap, deps=deps)
        self.tap_list.append(name)


class Arena:

    def __init__(self, ctx, name, cols):
        self.t = ctx.sb(name, [P, cols], BF16)
        self.cols = cols
        self.off = 0
        self.marks = []

    def push(self):
        self.marks.append(self.off)

    def pop(self):
        self.off = self.marks.pop()

    def alloc(self, cols):
        cols = (cols + 31) // 32 * 32
        assert self.off + cols <= self.cols, \
            f"arena overflow: {self.off}+{cols} > {self.cols}"
        v = self.t[:, self.off:self.off + cols]
        self.off += cols
        return v

    def alloc3(self, nb, t):
        return self.alloc(nb * t).rearrange("p (b t) -> p b t", b=nb)


def barrier(ctx):
    engs = [ctx.te, ctx.ve, ctx.se, ctx.gc]
    qs = [ctx.qs, ctx.qg]
    cnts = [(e.sem, e.cnt) for e in engs]
    for q in qs:
        for sem, tot in zip(q.sems, q.tot):
            if tot > 0:
                cnts.append((sem, tot))
    for e in engs + qs:
        for sem, v in cnts:
            if getattr(e, "sem", None) is not sem:
                e.wait(sem, v)



def emit_consts(ctx):
    c = {}
    c["ones1b"] = ctx.sb("c_ones1b", [1, P], BF16)
    c["ones1f"] = ctx.sb("c_ones1f", [1, P], F32)
    c["onescolb"] = ctx.sb("c_onescolb", [P, 1], BF16)
    c["identb"] = ctx.sb("c_identb", [P, P], BF16)
    c["cpool"] = ctx.sb("c_cpool", [P, 8], F32)
    ve, gc = ctx.ve, ctx.gc
    ve.op("memset", c["ones1b"][:], 1.0)
    ve.op("memset", c["ones1f"][:], 1.0)
    ve.op("memset", c["onescolb"][:], 1.0)
    ve.op("memset", c["identb"][:], 1.0)
    ve.op("memset", c["cpool"][:, 0:1], 1e-5)
    ve.op("memset", c["cpool"][:, 1:2], 1e-6)
    ve.op("memset", c["cpool"][:, 2:3], 0.125)
    ve.op("memset", c["cpool"][:, 3:4], 1.0)
    ve.op("memset", c["cpool"][:, 4:5], 0.0)
    ve.op("memset", c["cpool"][:, 5:6], -1.0)
    mv = ve.cnt
    gc.op("affine_select", c["identb"][:], c["identb"][:], [[1, P]],
          ALU.is_equal, 0.0, base=0, channel_multiplier=-1,
          deps=[(ctx.ve, mv)])
    c["ready_gc"] = gc.cnt
    c["ready_ve"] = ve.cnt
    ctx.consts = c
    return c



def emit_ln(ctx, src3d, src_is_f32, dst3d, scr, gamma=None, beta=None,
            src_dep=()):
    te, ve, se = ctx.te, ctx.ve, ctx.se
    c = ctx.consts
    PS = ctx.PS
    src_dep = list(src_dep) + [(ctx.se, ctx.se.cnt), (ctx.ve, ctx.ve.cnt)]

    if src_is_f32:
        cast = scr["cast3d"]
        for b in range(NB):
            ve.op("tensor_copy", cast[:, b, :], src3d[:, b, :], deps=src_dep)
        xb3 = cast
    else:
        xb3 = src3d
    vcast = ve.cnt

    for tc in range(4):
        for b in range(NB):
            te.mm(PS[0:1, 4 + tc, :], c["onescolb"][:],
                  xb3[:, b, 512 * tc:512 * (tc + 1)],
                  start=(b == 0), stop=(b == NB - 1),
                  deps=([(ve, vcast)] + src_dep + [(ve, c["ready_ve"])]),
                  inc=(b == NB - 1))
    sq = scr["sq"]
    for b in range(NB):
        se.op("activation", sq[:, :], xb3[:, b, :], AF.Square,
              deps=[(ve, vcast), (te, te.cnt)] + src_dep)
        scnt = se.cnt
        for tc in range(4):
            te.mm(PS[32:33, 4 + tc, :], c["onescolb"][:],
                  sq[:, 512 * tc:512 * (tc + 1)],
                  start=(b == 0), stop=(b == NB - 1),
                  deps=[(se, scnt)], inc=(tc == 3))
    tsums = te.cnt

    rows = scr["rows"]
    r0 = rows[0:1, 0:T]
    r1 = rows[0:1, T:2 * T]
    r2 = rows[32:33, 0:T]
    r3 = rows[32:33, T:2 * T]
    sum_row = PS[0:1, 4:8, :].rearrange("p a b -> p (a b)")
    sq_row = PS[32:33, 4:8, :].rearrange("p a b -> p (a b)")
    ve.op("tensor_scalar_mul", r0, sum_row, 1.0 / D, deps=[(te, tsums)])
    ve.op("tensor_scalar_mul", r2, sq_row, 1.0 / D, deps=[(te, tsums)])
    ve.op("tensor_mul", r3, r0, r0)
    ve.op("tensor_sub", r2, r2, r3)
    vvar = ve.cnt
    se.op("activation", r3, r2, AF.Sqrt, bias=c["cpool"][0:1, 0:1],
          deps=[(ve, vvar)])
    ssq = se.cnt
    ve.op("reciprocal", r1, r3, deps=[(se, ssq)])
    vrows = ve.cnt

    mub, rsb = scr["mub"], scr["rsb"]
    for what, dstb in ((r0, scr["mub"]), (r1, scr["rsb"])):
        dep = [(ve, ve.cnt)]
        for tc in range(4):
            te.mm(PS[:, tc, :], c["ones1b"][:],
                  what[0:1, 512 * tc:512 * (tc + 1)],
                  start=True, stop=True, deps=dep + [(ve, vrows)], inc=True)
        tb = te.cnt
        for tc in range(4):
            ve.op("tensor_copy", dstb[:, 512 * tc:512 * (tc + 1)],
                  PS[:, tc, :], deps=[(te, tb)])
    vbc = ve.cnt

    for b in range(NB):
        ve.op("tensor_sub", dst3d[:, b, :], xb3[:, b, :], mub[:, :],
              deps=[(ve, vbc)] + src_dep)
        ve.op("tensor_mul", dst3d[:, b, :], dst3d[:, b, :], rsb[:, :])
        if gamma is not None:
            ve.op("tensor_scalar", dst3d[:, b, :], dst3d[:, b, :],
                  gamma[:, b:b + 1], beta[:, b:b + 1],
                  ALU.mult, ALU.add)
    return ve.cnt



def make_ln_scratch(ctx):
    return {
        "sq": ctx.sb("ln_sq", [P, T], BF16),
        "rows": ctx.sb("ln_rows", [33, 2 * T], BF16),
        "mub": ctx.sb("ln_mub", [P, T], BF16),
        "rsb": ctx.sb("ln_rsb", [P, T], BF16),
    }


DR_SHAPES = {
    "xT": [D, T],
    "w_qT": [D, D], "w_kT": [D, D], "w_vT": [D, D],
    "bq_a": [P, NB], "bk_a": [P, NB], "bv_a": [D, 1],
    "w_oT": [D, D], "b_o": [P, NB],
    "ln1_g": [P, NB], "ln1_b": [P, NB], "ln2_g": [P, NB], "ln2_b": [P, NB],
    "wq0": [D, D], "bq0": [P, NB], "wk0": [D, D], "bk0": [P, NB],
    "wv0": [D, D], "bv0": [D, 1], "wp0": [D, D], "bp0": [P, NB],
    "gate_w0": [D, 8], "gate_b0": [1, 8],
    "exp_w1": [8, D, 4 * D], "exp_b1": [8, P, 32],
    "exp_w2": [8, 4 * D, D], "exp_b2": [8, D],
    "wq1": [D, D], "wk1": [D, D], "wv1": [D, D], "wb1": [D, 4],
    "conv_q": [P, NB, 4], "conv_k": [P, NB, 4], "conv_v": [P, NB, 4],
    "fir_long": [P, NB, 64], "fir_short": [P, NB, 5],
    "g_w1": [D + 16, D], "g_b1": [P, NB], "g_w2": [D, 4], "g_b2": [4, 1],
    "log_temp": [1, 4], "base_bias": [4, 4], "crl": [1, 4], "rms_w": [P, 2],
    "wo1": [D, D], "gate_w1": [D, 8],
    "m_w1": [8, D, D], "m_b1": [8, P, 8], "m_w2": [8, D, D], "m_b2": [8, D],
}

L0_NAMES = ["w_qT", "w_kT", "w_vT", "bq_a", "bk_a", "bv_a",
            "w_oT", "b_o", "ln1_g", "ln1_b", "ln2_g",
            "ln2_b", "wq0", "bq0", "wk0", "bk0", "wv0", "bv0", "wp0", "bp0",
            "gate_w0", "gate_b0", "exp_w1", "exp_b1", "exp_w2", "exp_b2"]
L1_NAMES = ["wq1", "wk1", "wv1", "wb1", "conv_q", "conv_k", "conv_v",
            "fir_long", "fir_short", "g_w1", "g_b1", "g_w2", "g_b2",
            "log_temp", "base_bias", "crl", "rms_w", "wo1", "gate_w1",
            "m_w1", "m_b1", "m_w2", "m_b2"]


def build_program(layers=(0, 1), taps=()):
    nc = bass.Bass("TRN2", target_bir_lowering=False, debug=False)
    dr = {"xT": nc.dram_tensor("xT", DR_SHAPES["xT"], F32,
                               kind="ExternalInput").ap()}
    use = (L0_NAMES if 0 in layers else []) + (L1_NAMES if 1 in layers else [])
    for nm in use:
        dr[nm] = nc.dram_tensor(nm, DR_SHAPES[nm], F32,
                                kind="ExternalInput").ap()
    out = nc.dram_tensor("outT", [D, T], F32, kind="ExternalOutput").ap()

    es = contextlib.ExitStack()
    with es:
        block = es.enter_context(nc.Block())
        ctx = Ctx(nc, es, taps)
        ctx.dr = dr
        ctx.PS = es.enter_context(nc.psum_tensor("PS", [P, 8, 512], F32))

        RES = ctx.sb("RES", [P, NB, T], F32)
        SCR_A = ctx.sb("SCR_A", [P, NB, T], BF16)
        ctx.RES, ctx.SCR_A = RES, SCR_A
        ctx.arena = Arena(ctx, "ARENA", 41088)
        assert 41088 >= 41056 + 16
        ctx.fcol = ctx.sb("fcol", [P, 16], F32)
        ctx.rowf = ctx.sb("rowf", [1, 512], F32)
        ctx.S0 = ctx.sb("S0", [P, 2, 256], F32)
        ctx.rowbuf = ctx.sb("rowbuf", [1, D], BF16)
        ctx.cmcolb = ctx.sb("cmcolb", [P, 2], BF16)

        @block.tensor
        def _(tensor):
            ctx.bind()
            lp = ctx.es.enter_context(nc.allow_low_precision(
                reason="bf16 stat rows within stated tolerance"))
            emit_consts(ctx)
            ln_scr = make_ln_scratch(ctx)
            ctx.ln_scr = ln_scr

            xld = ctx.qs.dma(RES[:, :, :],
                             dr["xT"].rearrange("(b p) t -> p b t", p=P),
                             pool="persist")
            ctx.xld = xld

            if 0 in layers:
                ctx.arena.push()
                ln_scr["cast3d"] = ctx.arena.alloc3(NB, T)
                v0 = emit_ln(ctx, RES, True, SCR_A, ln_scr, src_dep=[xld])
                ctx.arena.pop()
                ctx.tap("xn0", SCR_A[:, :, :], [(ctx.ve, v0)], [D, T],
                        rearr=(("(b p) t -> p b t",), {"p": P}))
                emit_layer0(ctx, ln_scr, v0)

            if 1 in layers:
                emit_layer1(ctx, ln_scr)

            barrier(ctx)
            st_tok = ctx.qs.dma(out.rearrange("(b p) t -> p b t", p=P),
                                RES[:, :, :])
            ctx.qs.wait(st_tok.sem, st_tok.val)
            ctx.gc.dep([st_tok])

    return nc, dr


def emit_layer0(ctx, ln_scr, v_xn0):
    te, ve, se, gc, qs, qg = ctx.te, ctx.ve, ctx.se, ctx.gc, ctx.qs, ctx.qg
    c = ctx.consts
    dr = ctx.dr
    ar = ctx.arena
    xn0, RES = ctx.SCR_A, ctx.RES

    SM = ctx.sb("SM0", [P, 96], F32)
    smap = {}
    qs.open_batch(pool="persist")
    for i, nm in enumerate(["ln1_g", "ln1_b", "ln2_g", "ln2_b", "bq0", "bk0",
                            "bp0", "b_o", "bq_a", "bk_a"]):
        smap[nm] = SM[:, 8 * i:8 * (i + 1)]
        sm_tok = qs.dma(smap[nm], dr[nm])
    qs.close_batch()
    rowbuf = ctx.rowbuf
    rb_tok = qg.dma(rowbuf[:, :], dr["bv0"].rearrange("d o -> o d"),
                    pool="persist")

    ar.push()

    h2 = ar.alloc3(NB, T)
    v_h2 = emit_ln(ctx, xn0, False, h2, ln_scr, gamma=smap["ln1_g"],
                   beta=smap["ln1_b"],
                   src_dep=[(ve, v_xn0), sm_tok])
    ctx.tap("h2", h2[:, :, :], [(ve, v_h2)], [D, T],
            rearr=(("(b p) t -> p b t",), {"p": P}))

    ACC = h2

    def evac_inner(ps_ap, mb, tc, tecnt):
        ve.op("scalar_tensor_tensor", ACC[:, mb, 512 * tc:512 * (tc + 1)],
              ps_ap, smap["bp0"][:, mb:mb + 1],
              xn0[:, mb, 512 * tc:512 * (tc + 1)],
              ALU.add, ALU.add, deps=[(te, tecnt), (ve, v_xn0)])

    w_in = {"wq": dr["wq0"], "wk": dr["wk0"], "wv": dr["wv0"],
            "wo": dr["wp0"], "bqf": smap["bq0"], "bkf": smap["bk0"],
            "bvr": rowbuf[:, :]}
    r = emit_mha(ctx, h2, [(ve, v_h2), rb_tok, sm_tok], w_in,
                 causal=True, evac_out=evac_inner, name="att0i",
                 spill_o=True)
    v_acc = r["ve"]
    ctx.tap("acc", ACC[:, :, :], [(ve, v_acc)], [D, T],
            rearr=(("(b p) t -> p b t",), {"p": P}))

    for b in range(NB):
        ve.op("tensor_add", RES[:, b, :], RES[:, b, :], ACC[:, b, :],
              deps=[(ve, v_acc)])
    v_res1 = ve.cnt

    moein = ACC
    v_moein = emit_ln(ctx, ACC, False, moein, ln_scr, gamma=smap["ln2_g"],
                      beta=smap["ln2_b"], src_dep=[(ve, v_res1)])
    ctx.tap("moein", moein[:, :, :], [(ve, v_moein)], [D, T],
            rearr=(("(b p) t -> p b t",), {"p": P}))

    rb_tok = qg.dma(rowbuf[:, :], dr["bv_a"].rearrange("d o -> o d"),
                    deps=[(ctx.te, r["te"])], pool="persist")
    def evac_outer(ps_ap, mb, tc, tecnt):
        ve.op("scalar_tensor_tensor", RES[:, mb, 512 * tc:512 * (tc + 1)],
              ps_ap, smap["b_o"][:, mb:mb + 1],
              RES[:, mb, 512 * tc:512 * (tc + 1)],
              ALU.add, ALU.add, deps=[(te, tecnt), (ve, v_res1)])

    w_out = {"wq": dr["w_qT"], "wk": dr["w_kT"], "wv": dr["w_vT"],
             "wo": dr["w_oT"], "bqf": smap["bq_a"], "bkf": smap["bk_a"],
             "bvr": rowbuf[:, :]}
    r = emit_mha(ctx, xn0, [(ve, v_xn0), rb_tok, sm_tok], w_out,
                 causal=False, evac_out=evac_outer, name="att0o",
                 spill_o=True)
    v_res2 = r["ve"]
    ctx.tap("res_preMoE", RES[:, :, :], [(ve, v_res2)], [D, T],
            rearr=(("(b p) t -> p b t",), {"p": P}))

    w1bufs = [ctx.SCR_A[:, :, 0:1024], ctx.SCR_A[:, :, 1024:2048]]
    emit_moe(ctx, moein, [(ve, v_moein)], dr["exp_w1"], dr["exp_b1"],
             dr["exp_w2"], dr["exp_b2"], dr["gate_w0"], dr["gate_b0"],
             H=4 * D, softmax_then_topk=False, w1bufs=w1bufs,
             res_dep=v_res2, name="moe0")
    ar.pop()


def emit_layer1(ctx, ln_scr):
    te, ve, se, gc, qs, qg = ctx.te, ctx.ve, ctx.se, ctx.gc, ctx.qs, ctx.qg
    c = ctx.consts
    dr = ctx.dr
    PS = ctx.PS
    ar = ctx.arena
    xn1, RES = ctx.SCR_A, ctx.RES
    rows = ln_scr["rows"]
    sq = ln_scr["sq"]
    mub = ln_scr["mub"]
    rsb = ln_scr["rsb"]

    ar.push()
    mskA = ar.alloc(2 * P)
    mskL = ar.alloc(P)
    self_j = ar.alloc(4 * P)
    beta_r = ar.alloc(T)
    wsl_flat = ar.alloc(2048)
    wsl = wsl_flat.rearrange("p (b f) -> p b f", b=NB)
    FIRL = ar.alloc(NB * 64).rearrange("p (b k) -> p b k", b=NB)
    stm = ar.alloc(64)
    st16 = ar.alloc(P)
    idt4 = ar.alloc(32)
    W2S = ar.alloc(NB * 4).rearrange("p (b k) -> p b k", b=NB)
    ve.op("memset", mskA[:, 0:P], -1.0)
    ve.op("memset", mskA[:, P:2 * P], -1.0)
    ve.op("memset", mskL[:, :], 1.0)
    ve.op("memset", self_j[:, :], 1.0)
    mv = ve.cnt
    gc.op("affine_select", mskA[:, 0:P], mskA[:, 0:P], [[-1, P]], ALU.is_gt,
          0.0, base=0, channel_multiplier=1, deps=[(ve, mv)])
    gc.op("affine_select", mskA[:, P:2 * P], mskA[:, P:2 * P], [[1, P]],
          ALU.is_gt, 0.0, base=0, channel_multiplier=-1)
    gc.op("affine_select", mskL[:, :], mskL[:, :], [[1, P]], ALU.is_ge,
          0.0, base=0, channel_multiplier=-1)
    for j in range(4):
        gc.op("affine_select", self_j[0:4, P * j:P * (j + 1)],
              self_j[0:4, P * j:P * (j + 1)], [[0, P]], ALU.is_equal, 0.0,
              base=-j, channel_multiplier=1)
    g_msk = gc.cnt
    w2s_tok = qg.dma(W2S[:, :, :],
                     dr["g_w2"].rearrange("(b p) f -> p b f", p=P),
                     pool="persist")

    firl_tok = qg.dma(FIRL[:, :, :], dr["fir_long"], pool="persist")

    SM = ctx.sb("SM1", [P, 16], F32)
    sm_tok = qs.open_batch(pool="persist")
    qs.dma(SM[:, 0:8], dr["g_b1"])
    qs.dma(SM[:, 8:10], dr["rms_w"])
    CW = ctx.sb("l1_cw", [P, NB, 12], F32)
    qs.dma(CW[:, :, 0:4], dr["conv_q"])
    qs.dma(CW[:, :, 4:8], dr["conv_k"])
    qs.dma(CW[:, :, 8:12], dr["conv_v"])
    CWB = ctx.sb("l1_cwb", [P, NB, 12], BF16)
    qg.open_batch(pool="persist")
    qg.dma(CWB[:, :, 0:4], dr["conv_q"])
    qg.dma(CWB[:, :, 4:8], dr["conv_k"])
    cwb_tok = qg.dma(CWB[:, :, 8:12], dr["conv_v"])
    qg.close_batch()
    FIRS = ctx.sb("l1_firs", [P, NB, 5], F32)
    qs.dma(FIRS[:, :, :], dr["fir_short"])
    SR = ctx.sb("l1_sr", [4, 24], F32)
    qs.dma(SR[0:1, 0:4], dr["log_temp"])
    qs.dma(SR[0:1, 4:8], dr["crl"])
    BB = ctx.sb("l1_bb", [4, 8], F32)
    qs.dma(BB[0:4, 0:4], dr["base_bias"])
    qs.dma(BB[0:4, 4:5], dr["g_b2"])
    sm_tok = qs.close_batch()

    ar.push()
    ln_scr["cast3d"] = ar.alloc3(NB, T)
    v_res = ve.cnt
    v_xn1 = emit_ln(ctx, RES, True, xn1, ln_scr,
                    src_dep=[(ve, v_res), ctx.xld])
    ar.pop()
    ctx.tap("xn1", xn1[:, :, :], [(ve, v_xn1)], [D, T],
            rearr=(("(b p) t -> p b t",), {"p": P}))

    wbs = ar.alloc3(NB, 4)
    wb_ld = qg.dma(wbs, dr["wb1"].rearrange("(b p) f -> p b f", p=P),
                   pool="persist", deps=[(ctx.ve, v_xn1), (ctx.se, se.cnt)])

    emit_l1_heads(ctx, xn1, v_xn1, beta_r, (wbs, wb_ld), wsl,
                  dict(sm=sm_tok, SM=SM, CW=CW, FIRS=FIRS, FIRL=FIRL, SR=SR,
                       BB=BB, mskA=mskA, mskL=mskL, self_j=self_j,
                       g_msk=g_msk, wsl_flat=wsl_flat, firl=firl_tok,
                       idt4=idt4, W2S=W2S, w2s_tok=w2s_tok, stm=stm,
                       st16=st16, CWB=CWB, cwb_tok=cwb_tok))
    v_res2 = ctx.ve.cnt
    ar.pop()
    ctx.tap("res_preMoE1", RES[:, :, :], [(ve, v_res2)], [D, T],
            rearr=(("(b p) t -> p b t",), {"p": P}))

    ar.push()
    w1bufs = [ar.alloc3(NB, 1024), ar.alloc3(NB, 1024)]
    emit_moe(ctx, xn1, [(ve, v_xn1)], dr["m_w1"], dr["m_b1"],
             dr["m_w2"], dr["m_b2"], dr["gate_w1"], None,
             H=D, softmax_then_topk=True, w1bufs=w1bufs,
             res_dep=v_res2, name="moe1")
    ar.pop()


def emit_l1_heads(ctx, xn1, v_xn1, beta_r, wb, wsl, aux):
    wbs, wb_ld = wb
    te, ve, se, gc, qs, qg = ctx.te, ctx.ve, ctx.se, ctx.gc, ctx.qs, ctx.qg
    RES = ctx.RES
    c = ctx.consts
    dr = ctx.dr
    PS = ctx.PS
    ar = ctx.arena
    rows = ctx.ln_scr["rows"]
    sq = ctx.ln_scr["sq"]
    mub = ctx.ln_scr["mub"]
    rsb = ctx.ln_scr["rsb"]
    SM, CW, FIRS, FIRL = aux["SM"], aux["CW"], aux["FIRS"], aux["FIRL"]
    SR, BB = aux["SR"], aux["BB"]
    mskA, mskL, self_j = aux["mskA"], aux["mskL"], aux["self_j"]
    sm_tok = aux["sm"]
    NCH = 16

    S0 = ctx.S0

    idt4 = aux["idt4"][0:4, 0:4]
    W2S = aux["W2S"]
    w2s_tok = aux["w2s_tok"]
    FLOOR = ctx.sb("l1_floor", [P, 16], F32)
    stm = aux["stm"]
    st16 = aux["st16"][0:16, :]
    CWB = aux["CWB"]
    cwb_tok = aux["cwb_tok"]
    st16_war = []
    for jj in range(4):
        ve.op("memset", FLOOR[:, 4 * jj:4 * jj + 1], 0.05)
        ve.op("memset", FLOOR[:, 4 * jj + 1:4 * jj + 2], 0.05)
        ve.op("memset", FLOOR[:, 4 * jj + 2:4 * jj + 4], 0.0)
    v_floor = ve.cnt

    se.op("activation", SR[0:1, 8:12], SR[0:1, 0:4], AF.Abs, deps=[sm_tok])
    se.op("activation", SR[0:1, 8:12], SR[0:1, 8:12], AF.Exp,
          scale=ctx.consts["cpool"][0:1, 5:6])
    se.op("activation", SR[0:1, 8:12], SR[0:1, 8:12], AF.Ln,
          bias=c["cpool"][0:1, 3:4])
    s_sp = se.cnt
    ve.op("tensor_scalar_max", SR[0:1, 16:20], SR[0:1, 0:4], 0.0,
          deps=[sm_tok, (se, s_sp)])
    ve.op("tensor_add", SR[0:1, 16:20], SR[0:1, 16:20], SR[0:1, 8:12])
    ve.op("tensor_scalar_add", SR[0:1, 16:20], SR[0:1, 16:20], 1e-4)
    ve.op("reciprocal", SR[0:1, 8:12], SR[0:1, 16:20])
    v_itemp = ve.cnt
    se.op("activation", SR[0:1, 12:16], SR[0:1, 4:8], AF.Sigmoid,
          deps=[sm_tok, (ve, v_itemp)])
    s_crl = se.cnt
    ve.op("tensor_scalar_mul", SR[0:1, 12:16], SR[0:1, 12:16], 0.5,
          deps=[(se, s_crl)])
    v_crl = ve.cnt

    bank_ve = {b: 0 for b in range(8)}
    bank_se = {b: 0 for b in range(8)}

    for h in range(4):
        ar.push()
        Bv = ar.alloc(2 * (64 + T)).rearrange("p (b t) -> p b t", b=2)
        delta = ar.alloc3(2, T)
        Bq = ar.alloc3(2, T)
        Bk = ar.alloc3(2, T)
        o_h = ar.alloc3(2, T)

        ar.push()
        A_pad = ar.alloc(2 * (64 + T)).rearrange("p (b t) -> p b t", b=2)
        ve.op("memset", A_pad[:, :, 0:64], 0.0)
        ve.op("memset", Bv[:, :, 0:64], 0.0)
        v_pad = ve.cnt
        conv_done = {}
        for ti, (wnm, cw0, Bdst, off) in enumerate(
                (("wq1", 0, Bq, 0), ("wk1", 4, Bk, 0), ("wv1", 8, Bv, 64))):
            w_ld = wload3(ctx, wsl[:, :, 0:256],
                          dr[wnm][:, 256 * h:256 * (h + 1)],
                          deps=[(te, te.cnt)])
            for mb in range(2):
                for kb in range(NB):
                    for tc in range(4):
                        te.mm(PS[:, 4 + tc, :],
                              wsl[:, kb, 128 * mb:128 * (mb + 1)],
                              xn1[:, kb, 512 * tc:512 * (tc + 1)],
                              start=(kb == 0), stop=(kb == NB - 1),
                              deps=[w_ld, (ve, v_xn1), (ve, bank_ve[4 + tc]),
                                    (se, bank_se[4 + tc])],
                              inc=(kb == NB - 1 and tc == 3))
                tcnt = te.cnt
                for tc in range(4):
                    ve.op("tensor_copy",
                          A_pad[:, mb, 64 + 512 * tc:64 + 512 * (tc + 1)],
                          PS[:, 4 + tc, :], deps=[(te, tcnt), (ve, v_pad)])
                    bank_ve[4 + tc] = ve.cnt
            dt4 = [sq[:, 128 * jj:128 * (jj + 1)] for jj in range(8)]
            gdeps = [(se, se.cnt), (ve, ve.cnt), cwb_tok, (te, te.cnt)]
            for mb in range(2):
                gb = 2 * h + mb
                for j in range(4):
                    gc.op("affine_select", dt4[4 * mb + j],
                          CWB[:, gb, cw0 + j:cw0 + j + 1].broadcast_to([P, P]),
                          [[1, P]], ALU.is_equal, 0.0, base=0,
                          channel_multiplier=-1, deps=gdeps)
            gcv = gc.cnt
            for mb in range(2):
                for j in range(4):
                    for tc in range(4):
                        te.mm(PS[:, 4 + tc, :], dt4[4 * mb + j],
                              A_pad[:, mb, 61 + j + 512 * tc:
                                    61 + j + 512 * (tc + 1)],
                              start=(j == 0), stop=(j == 3),
                              deps=[(gc, gcv), (ve, bank_ve[4 + tc]),
                                    (se, bank_se[4 + tc])],
                              inc=(j == 3 and tc == 3))
                tcnt = te.cnt
                for tc in range(4):
                    se.op("activation",
                          Bdst[:, mb, off + 512 * tc:off + 512 * (tc + 1)],
                          PS[:, 4 + tc, :], AF.Silu, deps=[(te, tcnt)])
                    bank_se[4 + tc] = se.cnt
            conv_done[ti] = se.cnt
            conv_done[ti] = se.cnt
        ar.pop()

        for Bsrc, sdep in ((Bq, conv_done[0]), (Bk, conv_done[1])):
            for mb in range(2):
                se.op("activation", sq[:, :], Bsrc[:, mb, :], AF.Square,
                      deps=[(se, sdep), (te, te.cnt)])
                scnt = se.cnt
                for tc in range(4):
                    te.mm(PS[0:1, 4 + tc, :], c["onescolb"][:, :],
                          sq[:, 512 * tc:512 * (tc + 1)],
                          start=(mb == 0), stop=(mb == 1),
                          deps=[(se, scnt), (ve, bank_ve[4 + tc]),
                                (se, bank_se[4 + tc])],
                          inc=(tc == 3))
            tsum = te.cnt
            se.op("activation", rows[0:1, 0:T],
                  PS[0:1, 4:8, :].rearrange("p a b -> p (a b)"), AF.Sqrt,
                  bias=c["cpool"][0:1, 1:2], deps=[(te, tsum)])
            ssq_ = se.cnt
            for tc in range(4):
                bank_se[4 + tc] = ssq_
            ve.op("reciprocal", rows[0:1, 0:T], rows[0:1, 0:T],
                  deps=[(se, ssq_)])
            vr = ve.cnt
            for tc in range(4):
                te.mm(PS[:, tc, :], c["ones1b"][:, :],
                      rows[0:1, 512 * tc:512 * (tc + 1)],
                      start=True, stop=True,
                      deps=[(ve, vr), (ve, bank_ve[tc])], inc=True)
            tbc = te.cnt
            for tc in range(4):
                ve.op("tensor_copy", mub[:, 512 * tc:512 * (tc + 1)],
                      PS[:, tc, :], deps=[(te, tbc)])
                bank_ve[tc] = ve.cnt
            for mb in range(2):
                ve.op("tensor_mul", Bsrc[:, mb, :], Bsrc[:, mb, :], mub[:, :])
            for tc in range(4):
                bank_ve[4 + tc] = ve.cnt
        v_norm = ve.cnt

        ar.push()
        kb_fm = ar.alloc3(2, T)
        vb_fm = ar.alloc3(2, T)
        Xb = ar.alloc(16 * P).rearrange("p (i q) -> p i q", i=16)
        XTb = ar.alloc(16 * P).rearrange("p (i q) -> p i q", i=16)
        for tc in range(4):
            for kb in range(NB):
                te.mm(PS[0:1, 4 + tc, :], wbs[:, kb, h:h + 1],
                      xn1[:, kb, 512 * tc:512 * (tc + 1)],
                      start=(kb == 0), stop=(kb == NB - 1),
                      deps=[wb_ld, (ve, v_xn1), (se, bank_se[4 + tc]),
                            (ve, bank_ve[4 + tc])],
                      inc=(kb == NB - 1))
        tcnt = te.cnt
        se.op("activation", beta_r[0:1, 0:T],
              PS[0:1, 4:8, :].rearrange("p a b -> p (a b)"), AF.Sigmoid,
              deps=[(te, tcnt)])
        s_beta = se.cnt
        for tc in range(4):
            bank_se[4 + tc] = s_beta
        for tc in range(4):
            te.mm(PS[:, tc, :], c["ones1b"][:, :],
                  beta_r[0:1, 512 * tc:512 * (tc + 1)],
                  start=True, stop=True,
                  deps=[(se, s_beta), (ve, v_norm), (ve, bank_ve[tc])],
                  inc=True)
        tbc = te.cnt
        for tc in range(4):
            ve.op("tensor_copy", mub[:, 512 * tc:512 * (tc + 1)],
                  PS[:, tc, :], deps=[(te, tbc)])
            bank_ve[tc] = ve.cnt
        for mb in range(2):
            ve.op("tensor_mul", kb_fm[:, mb, :], Bk[:, mb, :], mub[:, :],
                  deps=[(se, conv_done[1])])
            ve.op("tensor_mul", vb_fm[:, mb, :], Bv[:, mb, 64:64 + T],
                  mub[:, :], deps=[(se, conv_done[2])])
        v_kbvb = ve.cnt

        ve.op("memset", S0[:, :, :], 0.0, deps=[(te, te.cnt)])
        ve.op("memset", rsb[:, 1536:2048], 0.0)
        v_s = ve.cnt
        bank_ve = {b: max(bank_ve[b], v_s) for b in range(8)}

        Nc = sq[:, 0:512].rearrange("p (i q) -> p i q", i=4)
        NTc = sq[:, 512:1024].rearrange("p (i q) -> p i q", i=4)
        for grp in range(4):
            g0 = 4 * grp
            for ii in range(4):
                i = g0 + ii
                csl = slice(P * i, P * (i + 1))
                for mb in range(2):
                    te.mm(PS[:, 0, P * ii:P * (ii + 1)],
                          kb_fm[:, mb, csl], Bk[:, mb, csl],
                          start=(mb == 0), stop=(mb == 1),
                          deps=[(ve, v_kbvb), (ve, bank_ve[0])],
                          inc=(mb == 1))
                    te.mm(PS[:, 1, P * ii:P * (ii + 1)],
                          Bk[:, mb, csl], kb_fm[:, mb, csl],
                          start=(mb == 0), stop=(mb == 1),
                          deps=[(ve, bank_ve[1])], inc=(mb == 1))
            tA = te.cnt
            ve.op("tensor_mul", Nc[:, :, :], PS[:, 0, :].rearrange(
                "p (i q) -> p i q", i=4),
                mskA[:, 0:P].rearrange("p (o q) -> p o q", o=1
                                       ).broadcast_to([P, 4, P]),
                deps=[(te, tA), (gc, aux["g_msk"])])
            bank_ve[0] = ve.cnt
            ve.op("tensor_mul", NTc[:, :, :], PS[:, 1, :].rearrange(
                "p (i q) -> p i q", i=4),
                mskA[:, P:2 * P].rearrange("p (o q) -> p o q", o=1
                                           ).broadcast_to([P, 4, P]))
            bank_ve[1] = ve.cnt
            ve.op("tensor_add", Xb[:, g0:g0 + 4, :], Nc[:, :, :],
                  c["identb"][:, :].rearrange("p (o q) -> p o q", o=1
                                              ).broadcast_to([P, 4, P]),
                  deps=[(te, te.cnt)])
            ve.op("tensor_add", XTb[:, g0:g0 + 4, :], NTc[:, :, :],
                  c["identb"][:, :].rearrange("p (o q) -> p o q", o=1
                                              ).broadcast_to([P, 4, P]))
            vX = ve.cnt
            for lv in range(6):
                for ii in range(4):
                    te.mm(PS[:, 2, P * ii:P * (ii + 1)], NTc[:, ii, :],
                          Nc[:, ii, :], start=True, stop=True,
                          deps=[(ve, ve.cnt), (se, se.cnt), (ve, bank_ve[2]),
                                (se, bank_se[2])],
                          inc=(ii == 3))
                    te.mm(PS[:, 3, P * ii:P * (ii + 1)], Nc[:, ii, :],
                          NTc[:, ii, :], start=True, stop=True,
                          deps=[(ve, bank_ve[3]), (se, bank_se[3])],
                          inc=(ii == 3))
                tSq = te.cnt
                ve.op("tensor_copy", Nc[:, :, :], PS[:, 2, :].rearrange(
                    "p (i q) -> p i q", i=4), deps=[(te, tSq)])
                bank_ve[2] = ve.cnt
                ve.op("tensor_copy", NTc[:, :, :], PS[:, 3, :].rearrange(
                    "p (i q) -> p i q", i=4), deps=[(te, tSq)])
                bank_ve[3] = ve.cnt
                vsq = ve.cnt
                for ii in range(4):
                    te.mm(PS[:, 4, P * ii:P * (ii + 1)], NTc[:, ii, :],
                          Xb[:, g0 + ii, :], start=True, stop=False,
                          deps=[(ve, vsq), (se, se.cnt), (ve, bank_ve[4]),
                                (se, bank_se[4])])
                    te.mm(PS[:, 4, P * ii:P * (ii + 1)], c["identb"][:, :],
                          Xb[:, g0 + ii, :], start=False, stop=True,
                          inc=(ii == 3))
                    te.mm(PS[:, 5, P * ii:P * (ii + 1)], Xb[:, g0 + ii, :],
                          NTc[:, ii, :], start=True, stop=False,
                          deps=[(ve, vsq), (se, se.cnt), (ve, bank_ve[5]),
                                (se, bank_se[5])])
                    te.mm(PS[:, 5, P * ii:P * (ii + 1)], c["identb"][:, :],
                          XTb[:, g0 + ii, :], start=False, stop=True,
                          inc=(ii == 3))
                tL = te.cnt
                ve.op("tensor_copy", Xb[:, g0:g0 + 4, :],
                      PS[:, 4, :].rearrange("p (i q) -> p i q", i=4),
                      deps=[(te, tL)])
                bank_ve[4] = ve.cnt
                ve.op("tensor_copy", XTb[:, g0:g0 + 4, :],
                      PS[:, 5, :].rearrange("p (i q) -> p i q", i=4))
                bank_ve[5] = ve.cnt
        vXdone = ve.cnt

        kn2 = [rsb[:, 0:256], rsb[:, 256:512]]
        u2 = [rsb[:, 512:768], rsb[:, 768:1024]]
        wneg2 = [rsb[:, 1024:1280], rsb[:, 1280:1536]]
        S_bf2 = [rsb[:, 1536:2048].rearrange("p (b d) -> p b d", b=2),
                 mub[:, 0:512].rearrange("p (b d) -> p b d", b=2)]
        kb_cm2 = mub[:, 512:768]
        vb_cm2 = mub[:, 768:1024]
        ALT2 = [mub[:, 1024:1152], mub[:, 1152:1280]]
        ve.op("memset", S_bf2[1][:, :, :], 0.0, deps=[(te, te.cnt)])
        v_s2 = ve.cnt
        v_sbf = [v_s, v_s2]
        v_upre = [0, 0]
        v_wneg = [0, 0]
        v_alt = [0, 0]
        v_u = [0, 0]

        def scan_pre(jj):
            p = jj % 2
            csl = slice(P * jj, P * (jj + 1))
            for mb in range(2):
                te.mm(PS[:, 3, 256:384], Bk[:, mb, csl], Bq[:, mb, csl],
                      start=(mb == 0), stop=(mb == 1),
                      deps=[(ve, bank_ve[3]), (ve, v_kbvb)])
            for mb in range(2):
                te.mm(PS[:, 2, P * mb:P * (mb + 1)],
                      Bk[:, mb, csl], c["identb"][:, :],
                      start=True, stop=True,
                      deps=[(ve, bank_ve[2])])
                te.mm(PS[:, 2, 256 + P * mb:256 + P * (mb + 1)],
                      kb_fm[:, mb, csl], c["identb"][:, :],
                      start=True, stop=True, inc=(mb == 1))
                te.mm(PS[:, 3, P * mb:P * (mb + 1)],
                      vb_fm[:, mb, csl], c["identb"][:, :],
                      start=True, stop=True, inc=(mb == 1))
            t_t = te.cnt
            ve.op("tensor_copy", kn2[p][:, :], PS[:, 2, 0:256],
                  deps=[(te, t_t)])
            ve.op("tensor_copy", kb_cm2[:, :], PS[:, 2, 256:512])
            bank_ve[2] = ve.cnt
            ve.op("tensor_copy", vb_cm2[:, :], PS[:, 3, 0:256])
            ve.op("tensor_mul", ALT2[p][:, :], PS[:, 3, 256:384],
                  mskL[:, :], deps=[(gc, aux["g_msk"])])
            bank_ve[3] = ve.cnt
            v_alt[p] = ve.cnt
            v_evac1 = ve.cnt
            for mb in range(2):
                te.mm(PS[:, 4, P * mb:P * (mb + 1)],
                      kb_cm2[:, P * mb:P * (mb + 1)], XTb[:, jj, :],
                      start=True, stop=True,
                      deps=[(ve, v_evac1), (ve, bank_ve[4]), (ve, vXdone)],
                      inc=(mb == 1))
            t_w = te.cnt
            ve.op("tensor_scalar_mul", wneg2[p][:, :], PS[:, 4, 0:256],
                  -1.0, deps=[(te, t_w)])
            bank_ve[4] = ve.cnt
            v_wneg[p] = ve.cnt
            te.mm(PS[:, 6 + p, 0:256], XTb[:, jj, :], vb_cm2[:, :],
                  start=True, stop=False,
                  deps=[(ve, v_evac1), (ve, bank_ve[6 + p]), (ve, vXdone),
                        (se, bank_se[6 + p])])

        def scan_chain(jj):
            p = jj % 2
            for mb in range(2):
                te.mm(PS[:, 6 + p, 0:256],
                      wneg2[p][:, P * mb:P * (mb + 1)],
                      S_bf2[1 - p][:, mb, :], start=False, stop=(mb == 1),
                      deps=[(ve, v_wneg[p]), (ve, v_sbf[1 - p])],
                      inc=(mb == 1))
            t_u = te.cnt
            ve.op("tensor_copy", u2[p][:, :], PS[:, 6 + p, 0:256],
                  deps=[(te, t_u)])
            bank_ve[6 + p] = ve.cnt
            v_u[p] = ve.cnt
            for mb in range(2):
                te.mm(PS[:, 1, 256 * mb:256 * (mb + 1)],
                      kn2[p][:, P * mb:P * (mb + 1)], u2[p][:, :],
                      start=True, stop=True,
                      deps=[(ve, v_u[p]), (ve, bank_ve[1])],
                      inc=(mb == 1))
            t_s = te.cnt
            ve.op("tensor_add", S_bf2[p][:, :, :], S0[:, :, :],
                  PS[:, 1, :].rearrange("p (b d) -> p b d", b=2),
                  deps=[(te, t_s)])
            v_sbf[p] = ve.cnt
            ve.op("tensor_add", S0[:, :, :], S0[:, :, :],
                  PS[:, 1, :].rearrange("p (b d) -> p b d", b=2))
            bank_ve[1] = ve.cnt

        def scan_o(jj):
            p = jj % 2
            csl = slice(P * jj, P * (jj + 1))
            for dvb in range(2):
                for mb in range(2):
                    te.mm(PS[:, 0, P * dvb:P * (dvb + 1)],
                          S_bf2[1 - p][:, mb, P * dvb:P * (dvb + 1)],
                          Bq[:, mb, csl], start=(mb == 0), stop=False,
                          deps=[(ve, bank_ve[0]), (ve, v_sbf[1 - p]),
                                (ve, v_u[p]), (ve, v_alt[p])])
                te.mm(PS[:, 0, P * dvb:P * (dvb + 1)],
                      u2[p][:, P * dvb:P * (dvb + 1)], ALT2[p][:, :],
                      start=False, stop=True, inc=(dvb == 1))
            t_o = te.cnt
            ve.op("tensor_copy", delta[:, :, csl],
                  PS[:, 0, 0:256].rearrange("p (b q) -> p b q", b=2),
                  deps=[(te, t_o)])
            bank_ve[0] = ve.cnt

        scan_pre(0)
        scan_pre(1)
        for jj in range(16):
            scan_chain(jj)
            scan_o(jj)
            if jj + 2 < 16:
                scan_pre(jj + 2)
        v_delta = ve.cnt
        ar.pop()

        ar.push()
        W1q = ar.alloc3(NB, 256)
        stats16 = ar.alloc(T)
        p_tm = ar.alloc(64).rearrange("p (tt j) -> p tt j", tt=16)
        h1t = ar.alloc(512)
        W1B = aux["wsl_flat"][0:16, 0:1024]
        w1b_tok = ctx.qg.dma(W1B, dr["g_w1"][D:D + 16, :],
                             deps=[(te, te.cnt)])

        for mb in range(2):
            gb = 2 * h + mb
            ve.op("tensor_scalar_mul", Bq[:, mb, :], Bv[:, mb, 60:60 + T],
                  FIRS[:, gb, 0:1], deps=[(ve, v_delta), sm_tok])
            for j in range(1, 5):
                ve.op("scalar_tensor_tensor", Bq[:, mb, :],
                      Bv[:, mb, 60 + j:60 + j + T], FIRS[:, gb, j:j + 1],
                      Bq[:, mb, :], ALU.mult, ALU.add)
        v_fs = ve.cnt
        half = [[sq[:, 128 * jj:128 * (jj + 1)] for jj in range(8)],
                [sq[:, 1024 + 128 * jj:1024 + 128 * (jj + 1)]
                 for jj in range(8)]]
        for mb in range(2):
            gb = 2 * h + mb
            half_war = [te.cnt, te.cnt]
            for jg in range(8):
                par = jg % 2
                gdeps = [sm_tok, aux["firl"], (te, half_war[par]),
                         (ve, v_delta)]
                for jj in range(8):
                    j = 8 * jg + jj
                    gc.op("affine_select", half[par][jj],
                          FIRL[:, gb, j:j + 1].broadcast_to([P, P]),
                          [[1, P]], ALU.is_equal, 0.0, base=0,
                          channel_multiplier=-1, deps=gdeps)
                gcd = gc.cnt
                for jj in range(8):
                    j = 8 * jg + jj
                    for tc in range(4):
                        bank = 4 + tc
                        te.mm(PS[:, bank, :], half[par][jj],
                              Bv[:, mb, 1 + j + 512 * tc:
                                 1 + j + 512 * (tc + 1)],
                              start=(j == 0), stop=(j == 63),
                              deps=[(gc, gcd), (ve, bank_ve[bank]),
                                    (se, bank_se[bank])],
                              inc=(jj == 7 and tc == 3))
                half_war[par] = te.cnt
            tfl = te.cnt
            for tc in range(4):
                bank = 4 + tc
                ve.op("tensor_copy", Bk[:, mb, 512 * tc:512 * (tc + 1)],
                      PS[:, bank, :], deps=[(te, tfl)])
                bank_ve[bank] = ve.cnt
        v_fl = ve.cnt

        st_toks = []
        r_mean = rows[0:1, 0:T]
        r_var = rows[0:1, T:2 * T]
        r_abs = rows[32:33, 0:T]
        r_l2 = rows[32:33, T:2 * T]
        sqb = [sq, mub]
        st_row_war = []
        t3_prev = te.cnt
        for si, (src2, off) in enumerate(
                ((Bq, 0), (Bk, 0), (delta, 0), (Bv, 64))):
            sdep = [(ve, v_fl), (ve, v_fs)]
            for tc in range(4):
                for mb in range(2):
                    te.mm(PS[0:1, 4 + tc, :], c["onescolb"][:, :],
                          src2[:, mb, off + 512 * tc:off + 512 * (tc + 1)],
                          start=(mb == 0), stop=(mb == 1),
                          deps=sdep + [(ve, bank_ve[4 + tc]),
                                       (se, bank_se[4 + tc])],
                          inc=(mb == 1 and tc == 3))
            t1 = te.cnt
            for mb in range(2):
                se.op("activation", sqb[mb][:, :],
                      src2[:, mb, off:off + T], AF.Square,
                      deps=sdep + [(te, t3_prev)])
            s_sq = se.cnt
            for tc in range(4):
                for mb in range(2):
                    te.mm(PS[32:33, 4 + tc, :], c["onescolb"][:, :],
                          sqb[mb][:, 512 * tc:512 * (tc + 1)],
                          start=(mb == 0), stop=(mb == 1),
                          deps=[(se, s_sq)], inc=(mb == 1 and tc == 3))
            t2 = te.cnt
            for mb in range(2):
                se.op("activation", sqb[mb][:, :],
                      src2[:, mb, off:off + T], AF.Abs, deps=[(te, t2)])
            s_ab = se.cnt
            for tc in range(4):
                for mb in range(2):
                    te.mm(PS[64:65, 4 + tc, :], c["onescolb"][:, :],
                          sqb[mb][:, 512 * tc:512 * (tc + 1)],
                          start=(mb == 0), stop=(mb == 1),
                          deps=[(se, s_ab)], inc=(mb == 1 and tc == 3))
            t3 = te.cnt
            t3_prev = t3
            s1 = PS[0:1, 4:8, :].rearrange("p a b -> p (a b)")
            s2 = PS[32:33, 4:8, :].rearrange("p a b -> p (a b)")
            s3 = PS[64:65, 4:8, :].rearrange("p a b -> p (a b)")
            ve.op("tensor_scalar_mul", r_mean, s1, 1.0 / 256,
                  deps=[(te, t1)] + st_row_war)
            ve.op("tensor_mul", r_var, r_mean, r_mean, deps=[(te, t2)])
            ve.op("scalar_tensor_tensor", r_var, s2, 1.0 / 256, r_var,
                  ALU.mult, ALU.subtract)
            ve.op("tensor_scalar_mul", r_abs, s3, 1.0 / 256,
                  deps=[(te, t3)])
            v_rows = ve.cnt
            se.op("activation", r_l2, s2, AF.Sqrt,
                  deps=[(te, t2), (ve, v_rows)] + st_row_war)
            s_rows = se.cnt
            for tc in range(4):
                bank_ve[4 + tc] = v_rows
                bank_se[4 + tc] = s_rows
            for j, (rsrc, dep) in enumerate(((r_mean, (ve, v_rows)),
                                             (r_var, (ve, v_rows)),
                                             (r_abs, (ve, v_rows)),
                                             (r_l2, (se, s_rows)))):
                tok = qg.dma(stats16[4 * si + j:4 * si + j + 1, :],
                             rsrc, deps=[dep], pool="aux")
                st_toks.append(tok)
            st_row_war = st_toks[-4:]

        ve.op("tensor_add", BB[0:4, 5:6], BB[0:4, 4:5], BB[0:4, h:h + 1],
              deps=[sm_tok])
        v_bb = ve.cnt
        lg_fm = rows[0:4, T:2 * T]
        lg_war = [se.cnt, se.cnt, se.cnt, se.cnt]
        for qd in range(4):
            w1_ld = wload3(ctx, W1q, dr["g_w1"][0:D, 256 * qd:256 * (qd + 1)],
                           deps=[(te, te.cnt)])
            for fb2 in range(2):
                fblk = 2 * qd + fb2
                for kb in range(NB):
                    for tc in range(4):
                        te.mm(PS[:, tc, :],
                              W1q[:, kb, 128 * fb2:128 * (fb2 + 1)],
                              xn1[:, kb, 512 * tc:512 * (tc + 1)],
                              start=(kb == 0), stop=False,
                              deps=[w1_ld, w1b_tok, (se, bank_se[tc]),
                                    (ve, bank_ve[tc]), (ve, v_xn1)] + st_toks,
                              inc=False)
                for tc in range(4):
                    te.mm(PS[:, tc, :],
                          W1B[0:16, 128 * fblk:128 * (fblk + 1)],
                          stats16[0:16, 512 * tc:512 * (tc + 1)],
                          start=False, stop=True, inc=True)
                tcnt = te.cnt
                for tc in range(4):
                    se.op("activation", h1t[:, :], PS[:, tc, :], AF.Gelu,
                          bias=SM[:, fblk:fblk + 1],
                          deps=[(te, tcnt), (te, te.cnt)])
                    scnt = se.cnt
                    bank_se[tc] = scnt
                    te.mm(PS[0:4, 4 + tc, :], W2S[:, fblk, :], h1t[:, :],
                          start=(fblk == 0), stop=(fblk == 7),
                          deps=[(se, scnt), w2s_tok, (se, lg_war[tc]),
                                (ve, bank_ve[4 + tc])],
                          inc=True)
                    if fblk == 7:
                        se.op("activation", lg_fm[:, 512 * tc:512 * (tc + 1)],
                              PS[0:4, 4 + tc, :], AF.Copy,
                              deps=[(te, te.cnt)])
                        lg_war[tc] = se.cnt
                        bank_se[4 + tc] = se.cnt
        v_lg_se = se.cnt
        ve.op("tensor_scalar_add", lg_fm[:, :], lg_fm[:, :], BB[0:4, 5:6],
              deps=[(se, v_lg_se), (ve, v_bb)])
        v_lg = ve.cnt

        te.mm(PS[0:4, 2, 0:1], c["ones1f"][0:1, 0:4], SR[0:1, 8 + h:9 + h],
              start=True, stop=True,
              deps=[(ve, v_itemp), (ve, bank_ve[2])], inc=True)
        tid = te.cnt
        ve.op("tensor_scalar_mul", BB[0:4, 6:7], PS[0:4, 2, 0:1], 1.0,
              deps=[(te, tid)])
        ve.op("tensor_scalar_mul", idt4[:, :], c["identb"][0:4, 0:4],
              BB[0:4, 6:7])
        v_idt = ve.cnt
        bank_ve[2] = v_idt

        for tg in range(4):
            for k in range(4):
                tt = 4 * tg + k
                te.mm(PS[:, 3, 4 * k:4 * k + 4],
                      lg_fm[:, P * tt:P * (tt + 1)], idt4[:, :],
                      start=True, stop=True,
                      deps=[(ve, v_lg), (ve, v_idt), (ve, bank_ve[3])],
                      inc=(k == 3))
            tcnt = te.cnt
            g16 = PS[:, 3, 0:16].rearrange("p (a b) -> p a b", b=4)
            ve.op("tensor_reduce", stm[:, 0:4], g16, AX.X, ALU.max,
                  deps=[(te, tcnt)])
            ve.op("tensor_sub", stm[:, 16:32].rearrange(
                "p (a b) -> p a b", b=4), g16,
                stm[:, 0:4].rearrange("p (a b) -> p a b", b=1
                                      ).broadcast_to([P, 4, 4]))
            vsm = ve.cnt
            se.op("activation", stm[:, 16:32], stm[:, 16:32], AF.Exp,
                  deps=[(ve, vsm)])
            ssm = se.cnt
            bank_ve[3] = ve.cnt
            e16 = stm[:, 16:32].rearrange("p (a b) -> p a b", b=4)
            ve.op("tensor_reduce", stm[:, 0:4], e16, AX.X, ALU.add,
                  deps=[(se, ssm)])
            ve.op("reciprocal", stm[:, 4:8], stm[:, 0:4])
            ve.op("tensor_mul", e16, e16,
                  stm[:, 4:8].rearrange("p (a b) -> p a b", b=1
                                        ).broadcast_to([P, 4, 4]))
            ve.op("tensor_max", stm[:, 16:32], stm[:, 16:32], FLOOR[:, :],
                  deps=[(ve, v_floor)])
            ve.op("tensor_reduce", stm[:, 0:4], e16, AX.X, ALU.add)
            ve.op("reciprocal", stm[:, 4:8], stm[:, 0:4])
            ve.op("tensor_mul", p_tm[:, 4 * tg:4 * tg + 4, :], e16,
                  stm[:, 4:8].rearrange("p (a b) -> p a b", b=1
                                        ).broadcast_to([P, 4, 4]))
        v_ptm = ve.cnt

        prow4 = rows[0:4, 0:T]
        for tt in range(16):
            te.mm(PS[0:4, 2, P * (tt % 4):P * (tt % 4) + P]
                  if False else PS[0:4, 2, 0:P],
                  p_tm[:, tt, :], c["identb"][:, :],
                  start=True, stop=True,
                  deps=[(ve, v_ptm), (se, se.cnt), (ve, bank_ve[2])],
                  inc=True)
            tcnt = te.cnt
            se.op("activation", prow4[:, P * tt:P * (tt + 1)],
                  PS[0:4, 2, 0:P], AF.Copy, deps=[(te, tcnt)])
            bank_ve[2] = ve.cnt
        s_prow = se.cnt
        te.mm(PS[:, 2, 0:1], c["ones1f"][:, :], SR[0:1, 12 + h:13 + h],
              start=True, stop=True, deps=[(ve, v_crl), (se, s_prow)],
              inc=True)
        tcnt = te.cnt
        ve.op("tensor_scalar_mul", ctx.fcol[:, 15:16], PS[:, 2, 0:1], 1.0,
              deps=[(te, tcnt)])
        v_crlc = ve.cnt
        bank_ve[2] = v_crlc

        tmp = h1t
        srcs = [(Bq, 0), (Bk, 0), (delta, 0), (Bv, 64)]
        comb_war = ve.cnt
        for j in range(4):
            for tc in range(4):
                bank = j % 2
                te.mm(PS[:, bank, :], self_j[0:4, P * j:P * (j + 1)],
                      prow4[:, 512 * tc:512 * (tc + 1)],
                      start=True, stop=True,
                      deps=[(se, s_prow), (ve, bank_ve[bank]),
                            (gc, aux["g_msk"])], inc=True)
                tcnt = te.cnt
                src2, off = srcs[j]
                for mb in range(2):
                    if j == 0:
                        ve.op("tensor_mul",
                              o_h[:, mb, 512 * tc:512 * (tc + 1)],
                              src2[:, mb, off + 512 * tc:off + 512 * (tc + 1)],
                              PS[:, bank, :], deps=[(te, tcnt), (ve, comb_war)])
                    else:
                        ve.op("tensor_mul", tmp[:, :],
                              src2[:, mb, off + 512 * tc:off + 512 * (tc + 1)],
                              PS[:, bank, :], deps=[(te, tcnt)])
                        ve.op("tensor_add",
                              o_h[:, mb, 512 * tc:512 * (tc + 1)],
                              o_h[:, mb, 512 * tc:512 * (tc + 1)], tmp[:, :])
                bank_ve[bank] = ve.cnt
        for mb in range(2):
            ve.op("scalar_tensor_tensor", o_h[:, mb, :], Bq[:, mb, :],
                  ctx.fcol[:, 15:16], o_h[:, mb, :], ALU.mult, ALU.add,
                  deps=[(ve, v_crlc)])
        v_comb = ve.cnt

        sqb2 = [sq, mub]
        for mb in range(2):
            se.op("activation", sqb2[mb][:, :], o_h[:, mb, :],
                  AF.Square, deps=[(ve, v_comb), (te, te.cnt)])
        s_rs = se.cnt
        for tt in range(16):
            for mb in range(2):
                te.mm(PS[:, 6, tt:tt + 1], sqb2[mb][:, P * tt:P * (tt + 1)],
                      c["onescolb"][:, :], start=(mb == 0), stop=(mb == 1),
                      deps=[(se, s_rs), (ve, bank_ve[6])],
                      inc=(tt == 15 and mb == 1))
        trs = te.cnt
        ve.op("tensor_scalar_mul", stm[:, 0:16], PS[:, 6, 0:16], 1.0 / 256,
              deps=[(te, trs)])
        bank_ve[6] = ve.cnt
        vv = ve.cnt
        se.op("activation", stm[:, 16:32], stm[:, 0:16], AF.Sqrt,
              bias=c["cpool"][:, 0:1], deps=[(ve, vv)])
        srs = se.cnt
        ve.op("reciprocal", stm[:, 0:16], stm[:, 16:32], deps=[(se, srs)])
        v_rstd = ve.cnt
        te.mm(PS[0:16, 2, 0:P], stm[:, 0:16], c["identb"][:, :],
              start=True, stop=True,
              deps=[(ve, v_rstd), (ve, bank_ve[2])], inc=True)
        tcnt = te.cnt
        ve.op("tensor_copy", st16[:, :], PS[0:16, 2, 0:P],
              deps=[(te, tcnt)] + st16_war)
        bank_ve[2] = ve.cnt
        v16 = ve.cnt
        rrow = rows[0:1, 0:T]
        rs_tok = qg.dma(rrow.rearrange("o (a b) -> o a b", a=16), st16[:, :],
                        deps=[(ve, v16)], pool="aux")
        st16_war = [rs_tok]
        for tc in range(4):
            te.mm(PS[:, tc, :], c["ones1b"][:, :],
                  rrow[0:1, 512 * tc:512 * (tc + 1)],
                  start=True, stop=True,
                  deps=[rs_tok, (ve, bank_ve[tc])], inc=True)
        tbc = te.cnt
        for tc in range(4):
            ve.op("tensor_copy", mub[:, 512 * tc:512 * (tc + 1)],
                  PS[:, tc, :], deps=[(te, tbc)])
            bank_ve[tc] = ve.cnt
        for mb in range(2):
            ve.op("scalar_tensor_tensor", o_h[:, mb, :], o_h[:, mb, :],
                  SM[:, 8 + mb:9 + mb], mub[:, :], ALU.mult, ALU.mult)
        v_on = ve.cnt
        s_on = se.cnt

        wo_war = te.cnt
        for quarter in range(8):
            wo_ld = wload3(ctx, wsl[:, 0:2, 0:P],
                           dr["wo1"][256 * h:256 * (h + 1),
                                     P * quarter:P * (quarter + 1)],
                           deps=[(te, wo_war)])
            mbk = quarter
            for tc in range(4):
                bank = tc % 2
                for kb in range(2):
                    te.mm(PS[:, bank, :], wsl[:, kb, 0:P],
                          o_h[:, kb, 512 * tc:512 * (tc + 1)],
                          start=(kb == 0), stop=(kb == 1),
                          deps=[wo_ld, (ve, v_on), (se, s_on),
                                (ve, bank_ve[bank]), (se, bank_se[bank])],
                          inc=(kb == 1))
                tcnt = te.cnt
                ve.op("tensor_add", RES[:, mbk, 512 * tc:512 * (tc + 1)],
                      RES[:, mbk, 512 * tc:512 * (tc + 1)], PS[:, bank, :],
                      deps=[(te, tcnt)])
                bank_ve[bank] = ve.cnt
            wo_war = te.cnt
        ar.pop()
        ar.pop()



def host_layout(inputs, core):
    g = lambda k: np.asarray(inputs[k])
    fmv = lambda v: v.reshape(NB, P).T
    m = {}
    m["xT"] = g("L0_attn_x")[core].T
    wqkvT = g("L0_attn_W_qkv").T.reshape(D, 16, 3, 64)
    m["w_qT"] = wqkvT[:, :, 0, :].reshape(D, D)
    m["w_kT"] = wqkvT[:, :, 1, :].reshape(D, D)
    m["w_vT"] = wqkvT[:, :, 2, :].reshape(D, D)
    bqkv = g("L0_attn_b_qkv").reshape(16, 3, 64)
    m["bq_a"] = fmv(bqkv[:, 0, :].reshape(-1))
    m["bk_a"] = fmv(bqkv[:, 1, :].reshape(-1))
    m["bv_a"] = bqkv[:, 2, :].reshape(-1, 1)
    m["w_oT"] = g("L0_attn_W_o").T
    m["b_o"] = fmv(g("L0_attn_b_o"))
    m["ln1_g"] = fmv(g("L0_ffn_ln1_g"))
    m["ln1_b"] = fmv(g("L0_ffn_ln1_b"))
    m["ln2_g"] = fmv(g("L0_ffn_ln2_g"))
    m["ln2_b"] = fmv(g("L0_ffn_ln2_b"))
    m["wq0"] = g("L0_ffn_Wq"); m["bq0"] = fmv(g("L0_ffn_bq"))
    m["wk0"] = g("L0_ffn_Wk"); m["bk0"] = fmv(g("L0_ffn_bk"))
    m["wv0"] = g("L0_ffn_Wv"); m["bv0"] = g("L0_ffn_bv").reshape(-1, 1)
    m["wp0"] = g("L0_ffn_Wp"); m["bp0"] = fmv(g("L0_ffn_bp"))
    m["gate_w0"] = g("L0_ffn_gate_W")
    m["gate_b0"] = g("L0_ffn_gate_b").reshape(1, -1)
    m["exp_w1"] = g("L0_ffn_exp_W1")
    m["exp_b1"] = g("L0_ffn_exp_b1").reshape(8, 32, P).transpose(0, 2, 1)
    m["exp_w2"] = g("L0_ffn_exp_W2"); m["exp_b2"] = g("L0_ffn_exp_b2")
    m["wq1"] = g("L1_attn_Wq"); m["wk1"] = g("L1_attn_Wk")
    m["wv1"] = g("L1_attn_Wv"); m["wb1"] = g("L1_attn_Wb")
    sw = lambda v, k: v.reshape(NB, P, k).transpose(1, 0, 2)
    m["conv_q"] = sw(g("L1_attn_conv_q"), 4)
    m["conv_k"] = sw(g("L1_attn_conv_k"), 4)
    m["conv_v"] = sw(g("L1_attn_conv_v"), 4)
    m["fir_long"] = sw(g("L1_attn_fir_long").reshape(D, 64), 64)
    m["fir_short"] = sw(g("L1_attn_fir_short").reshape(D, 5), 5)
    m["g_w1"] = g("L1_attn_W1"); m["g_b1"] = fmv(g("L1_attn_b1"))
    m["g_w2"] = g("L1_attn_W2"); m["g_b2"] = g("L1_attn_b2").reshape(-1, 1)
    m["log_temp"] = g("L1_attn_log_temp").reshape(1, -1)
    m["base_bias"] = g("L1_attn_base_bias").T
    m["crl"] = g("L1_attn_conv_residual_logit").reshape(1, -1)
    m["rms_w"] = g("L1_attn_rms_w").reshape(2, P).T
    m["wo1"] = g("L1_attn_Wo")
    m["gate_w1"] = g("L1_ffn_gate_w").T
    m["m_w1"] = g("L1_ffn_w1")
    m["m_b1"] = g("L1_ffn_b1").reshape(8, 8, P).transpose(0, 2, 1)
    m["m_w2"] = g("L1_ffn_w2"); m["m_b2"] = g("L1_ffn_b2")
    return {k: np.ascontiguousarray(v, dtype=np.float32) for k, v in m.items()}


def kernel(**inputs):
    nc, dr = build_program(layers=(0, 1))
    in_names = set(dr.keys())
    maps = []
    for core in range(8):
        full = host_layout(inputs, core)
        maps.append({k: v for k, v in full.items() if k in in_names})
    res = run_bass_kernel_spmd(nc, maps, core_ids=list(range(8)))
    out = np.stack([np.ascontiguousarray(r["outT"].T) for r in res.results])
    return out.astype(np.float32)



def wload3(ctx, dst3, dram_ap, deps=()):
    return ctx.qg.dma(dst3, dram_ap.rearrange("(b p) f -> p b f", p=P), deps=deps)


def emit_mha(ctx, x3d, x_dep, w, causal, evac_out, name, NG=2,
             spill_o=False):
    te, ve, se, gc, qg = ctx.te, ctx.ve, ctx.se, ctx.gc, ctx.qg
    c = ctx.consts
    PS = ctx.PS
    rows = ctx.ln_scr["rows"]
    fcol = ctx.fcol

    ar = ctx.arena
    ar.push()
    x_dep = list(x_dep) + [(ctx.se, ctx.se.cnt), (ctx.ve, ctx.ve.cnt),
                           (ctx.gc, ctx.gc.cnt)]
    nbg = 1
    if spill_o:
        o_scr = ctx.nc.dram_tensor(name + "_oscr", [D, T], BF16,
                                   kind="Internal").ap()
        ostg = [ar.alloc(512) for _ in range(2)]
        ostg_tok = [None, None]
        ostg_all = {}
        O3 = None
    else:
        O3 = ar.alloc3(NB, T)
    qf = ar.alloc3(nbg, T)
    kf = ar.alloc3(nbg, T)
    vtm = ar.alloc(16 * NG * 65).rearrange("p (tt h e) -> p tt h e", tt=16, h=NG)
    sq1 = ctx.ln_scr["sq"]
    sq2 = ctx.ln_scr["mub"]
    rb = ctx.ln_scr["rsb"][:, 0:512]
    cmcol = ctx.cmcolb
    wqs = ar.alloc3(NB, NG * 64)
    wks = ar.alloc3(NB, NG * 64)
    wvs = ar.alloc3(NB, NG * 64)
    wos = wvs

    ve.op("memset", vtm[:, :, :, 64:65], 1.0)

    assert NG == 2
    n_groups = 16 // NG
    pt3 = [ctx.ln_scr["rsb"][:, 512:1536], ctx.ln_scr["mub"][:, 0:1024],
           sq1[:, 0:1024]]
    pt_war = [0, 0, 0]
    st_war = [0, 0]
    se0 = se.cnt
    bank_war = {b: 0 for b in range(8)}
    bank_wars = {b: 0 for b in range(8)}
    wbuf_war = {"q": 0, "k": 0, "v": 0}

    for g in range(n_groups):
        cs = slice(g * NG * 64, (g + 1) * NG * 64)
        wq_ld = wload3(ctx, wqs, w["wq"][:, cs], deps=[(te, wbuf_war["q"])])
        wk_ld = wload3(ctx, wks, w["wk"][:, cs], deps=[(te, wbuf_war["k"])])
        wv_ld = wload3(ctx, wvs, w["wv"][:, cs], deps=[(te, wbuf_war["v"])])

        for which, wt, dstf, ld, bfm in (("q", wqs, qf, wq_ld, w["bqf"]),
                                         ("k", wks, kf, wk_ld, w["bkf"])):
            for kb in range(NB):
                for tc in range(4):
                    te.mm(PS[:, 4 + tc, :], wt[:, kb, 0:P],
                          x3d[:, kb, 512 * tc:512 * (tc + 1)],
                          start=(kb == 0), stop=(kb == NB - 1),
                          deps=[ld, (ve, bank_war[4 + tc]),
                                (se, bank_wars[4 + tc])] + x_dep,
                          inc=(kb == NB - 1 and tc == 3))
            tcnt = te.cnt
            for tc in range(4):
                ve.op("tensor_scalar_add",
                      dstf[:, 0, 512 * tc:512 * (tc + 1)],
                      PS[:, 4 + tc, :], bfm[:, g:g + 1],
                      deps=[(te, tcnt)])
                bank_war[4 + tc] = ve.cnt
            wbuf_war[which] = te.cnt
        vqk = ve.cnt

        for tt in range(16):
            bank = 2 + (tt % 2)
            te.mm(PS[:, bank, 0:NG * 64], c["ones1b"][0:1, :],
                  w["bvr"][0:1, cs], start=True, stop=False,
                  deps=[wv_ld, (ve, bank_war[bank]), (se, st_war[1]),
                        (ve, c["ready_ve"])] + x_dep)
            for kb in range(NB):
                te.mm(PS[:, bank, 0:NG * 64], x3d[:, kb, P * tt:P * (tt + 1)],
                      wvs[:, kb, :], start=False, stop=(kb == NB - 1),
                      inc=(kb == NB - 1))
            tcnt = te.cnt
            for hh in range(NG):
                ve.op("tensor_copy", vtm[:, tt, hh, 0:64],
                      PS[:, bank, hh * 64:hh * 64 + 64], deps=[(te, tcnt)])
            bank_war[bank] = ve.cnt
        wbuf_war["v"] = te.cnt

        sqs = {}
        for j, (srcf, dsts) in enumerate(((qf, sq1), (kf, sq2))):
            se.op("activation", dsts[:, :], srcf[:, 0, :], AF.Square,
                  deps=[(ve, vqk), (te, pt_war[1]), (te, pt_war[2])])
            sqs[j] = se.cnt
        for j, dsts in ((0, sq1), (1, sq2)):
            for tmb in range(16):
                te.mm(PS[:, 6 + j, tmb:tmb + 1],
                      dsts[:, P * tmb:P * (tmb + 1)],
                      c["onescolb"][:, :], start=True, stop=True,
                      deps=[(se, sqs[j]), (ve, bank_war[6 + j])],
                      inc=(tmb == 15))
        tq = te.cnt
        for j in range(2):
            ve.op("reduce_max", cmcol[:, j:j + 1], PS[:, 6 + j, 0:16], AX.X,
                  deps=[(te, tq)])
            bank_war[6 + j] = ve.cnt
        vc = ve.cnt
        for j in range(2):
            te.mm(PS[0:1, 6 + j, 0:128], cmcol[:, j:j + 1], c["identb"][:],
                  start=True, stop=True,
                  deps=[(ve, vc), (ve, bank_war[6 + j]), (gc, c["ready_gc"])],
                  inc=True)
        tcnt = te.cnt
        for j in range(2):
            ve.op("reduce_max", rows[0:1, 1500 + j:1501 + j],
                  PS[0:1, 6 + j, 0:128], AX.X, deps=[(te, tcnt)])
            bank_war[6 + j] = ve.cnt
        vmx = ve.cnt
        se.op("activation", rows[0:1, 1500:1502], rows[0:1, 1500:1502],
              AF.Sqrt, deps=[(ve, vmx)])
        ssq = se.cnt
        ve.op("tensor_mul", rows[0:1, 1502:1503], rows[0:1, 1500:1501],
              rows[0:1, 1501:1502], deps=[(se, ssq)])
        vprod = ve.cnt
        te.mm(PS[:, 7, 0:1], c["ones1b"][:], rows[0:1, 1502:1503],
              start=True, stop=True,
              deps=[(ve, vprod), (ve, bank_war[7])], inc=True)
        tcnt = te.cnt
        ve.op("tensor_scalar_mul", fcol[:, 2 * g:2 * g + 1], PS[:, 7, 0:1],
              -0.125, deps=[(te, tcnt), (se, se0)])
        bank_war[7] = ve.cnt
        vcb = ve.cnt

        for qc in range(4):
            kt_hi = 16 if not causal else 4 * (qc + 1)
            pv_depq = {}
            for kt in range(kt_hi + 2):
                if kt < kt_hi:
                    par = kt % 3
                    sb = kt % 2
                    for hh in range(NG):
                        base = 64 * hh
                        bank = 2 * sb + hh
                        te.mm(PS[:, bank, :],
                              kf[base:base + 64, 0, P * kt:P * (kt + 1)],
                              qf[base:base + 64, 0, 512 * qc:512 * (qc + 1)],
                              start=True, stop=True,
                              deps=[(ve, vcb), (se, st_war[sb]),
                                    (ve, bank_war[bank])],
                              inc=True)
                    tcnt = te.cnt
                    se.op("activation", pt3[par][:, :],
                          PS[:, 2 * sb:2 * sb + 2, :].rearrange(
                              "p a b -> p (a b)"),
                          AF.Exp, bias=fcol[:, 2 * g:2 * g + 1],
                          scale=c["cpool"][:, 2:3],
                          deps=[(te, tcnt), (te, pt_war[par])])
                    scnt = se.cnt
                    st_war[sb] = scnt
                    for hh in range(NG):
                        dep_pv = (se, scnt)
                        if causal and kt >= 4 * qc:
                            gc.op("affine_select",
                                  pt3[par][:, 512 * hh:512 * (hh + 1)],
                                  pt3[par][:, 512 * hh:512 * (hh + 1)],
                                  [[1, 512]], ALU.is_ge,
                                  0.0, base=qc * 512 - kt * P,
                                  channel_multiplier=-1, deps=[(se, scnt)])
                            dep_pv = (gc, gc.cnt)
                        pv_depq.setdefault(kt, {})[hh] = dep_pv
                if kt >= 2:
                    kp = kt - 2
                    parp = kp % 3
                    for hh in range(NG):
                        te.mm(PS[0:65, 6 + hh, 0:512], vtm[:, kp, hh, :],
                              pt3[parp][:, 512 * hh:512 * (hh + 1)],
                              start=(kp == 0), stop=(kp == kt_hi - 1),
                              deps=[pv_depq[kp][hh],
                                    (ve, bank_war[6 + hh])],
                              inc=(kp == kt_hi - 1))
                        pt_war[parp] = te.cnt
            for hh in range(NG):
                h = g * NG + hh
                oblk, obase = h // 2, 64 * (h % 2)
                tpv = te.cnt
                ve.op("tensor_copy", rows[0:1, 512 * hh:512 * (hh + 1)],
                      PS[64:65, 6 + hh, :], deps=[(te, tpv)])
                vr_ = ve.cnt
                te.mm(PS[0:64, 4 + hh, :], c["ones1b"][0:1, 0:64],
                      rows[0:1, 512 * hh:512 * (hh + 1)],
                      start=True, stop=True,
                      deps=[(ve, vr_), (ve, bank_war[4 + hh])], inc=True)
                tcnt = te.cnt
                ve.op("reciprocal", rb[0:64, :], PS[0:64, 4 + hh, :],
                      deps=[(te, tcnt)])
                bank_war[4 + hh] = ve.cnt
                if spill_o:
                    par = (4 * h + qc) % 2
                    dep_st = [ostg_tok[par]] if ostg_tok[par] else []
                    ve.op("tensor_mul", ostg[par][0:64, :],
                          PS[0:64, 6 + hh, 0:512], rb[0:64, :], deps=dep_st)
                    ostg_tok[par] = ctx.qs.dma(
                        o_scr[64 * h:64 * h + 64, 512 * qc:512 * (qc + 1)],
                        ostg[par][0:64, :], deps=[(ve, ve.cnt)])
                    ostg_all[ostg_tok[par].sem.name] = ostg_tok[par]
                else:
                    ve.op("tensor_mul",
                          O3[obase:obase + 64, oblk, 512 * qc:512 * (qc + 1)],
                          PS[0:64, 6 + hh, 0:512], rb[0:64, :])
                bank_war[6 + hh] = ve.cnt
    vO = ve.cnt

    assert spill_o
    ord_toks = list(ostg_all.values())
    wo_buf = ar.alloc3(NB, 512)
    Ost = [ar.alloc3(NB, 512) for _ in range(2)]
    ost_tok = [None, None]
    ost_war = [0, 0]
    wo_war = wbuf_war["v"]
    for half in range(2):
        wo_ld = wload3(ctx, wo_buf,
                       w["wo"][:, 512 * half:512 * (half + 1)],
                       deps=[(te, wo_war)])
        for tc in range(4):
            par = tc % 2
            ost_tok[par] = ctx.qs.dma(
                Ost[par][:, :, :],
                o_scr[:, 512 * tc:512 * (tc + 1)].rearrange(
                    "(b p) t -> p b t", p=P),
                deps=[(te, ost_war[par])] + ord_toks)
            for q2 in range(4):
                quarter = 4 * half + q2
                bank = 4 + q2
                for kb in range(NB):
                    te.mm(PS[:, bank, :], wo_buf[:, kb, P * q2:P * (q2 + 1)],
                          Ost[par][:, kb, :],
                          start=(kb == 0), stop=(kb == NB - 1),
                          deps=[wo_ld, ost_tok[par], (ve, bank_war[bank]),
                                (se, bank_wars[bank])],
                          inc=(kb == NB - 1))
                evac_out(PS[:, bank, :], quarter, tc, te.cnt)
                bank_war[bank] = ve.cnt
            ost_war[par] = te.cnt
        wo_war = te.cnt
    ar.pop()
    return {"ve": ve.cnt, "te": te.cnt}



def emit_moe(ctx, x3d, x_dep, w1_d, b1_d, w2_d, b2_d, gw_d, gb_d, H,
             softmax_then_topk, w1bufs, res_dep, name):
    te, ve, se, gc, qs, qg = ctx.te, ctx.ve, ctx.se, ctx.gc, ctx.qs, ctx.qg
    c = ctx.consts
    PS = ctx.PS
    RES = ctx.RES
    x_dep = list(x_dep) + [(ctx.se, ctx.se.cnt), (ctx.ve, ctx.ve.cnt),
                           (ctx.gc, ctx.gc.cnt), (ctx.ve, res_dep)]
    NFC = H // 1024
    ar = ctx.arena
    ar.push()
    macc = ar.alloc(16 * 1024).rearrange("p (tt d) -> p tt d", tt=16)
    sq3 = ctx.ln_scr["sq"].rearrange("p (b t) -> p b t", b=4)
    mub3 = ctx.ln_scr["mub"].rearrange("p (b t) -> p b t", b=4)
    midv = lambda fb: (sq3 if fb % 2 == 0 else mub3)[:, fb // 2, :]
    w2buf = ar.alloc3(NB, 1024)
    gws = ar.alloc3(NB, 8)
    gbr = ar.alloc(32)
    b2r = ctx.rowbuf
    if not hasattr(ctx, "moe_sb"):
        ctx.moe_sb = dict(
            iota8=ctx.sb("moe_iota", [P, 8], F32),
            wf=ctx.sb("moe_wf", [P, 16, 8], F32),
            gsc=ctx.sb("moe_gsc", [P, 20], F32),
            i8=ctx.sb("moe_i8", [P, 8], U32),
            b1f=ctx.sb("moe_b1f", [P, 64], F32))
    iota8 = ctx.moe_sb["iota8"]
    wf = ctx.moe_sb["wf"]
    gsc = ctx.moe_sb["gsc"]
    i8 = ctx.moe_sb["i8"]
    b1f = ctx.moe_sb["b1f"]

    gc.op("iota", iota8[:, :], [[1, 8]], channel_multiplier=0,
          allow_small_or_imprecise_dtypes=True)
    g_iota = gc.cnt

    gw_ld = ctx.qg.dma(gws, gw_d.rearrange("(b p) f -> p b f", p=P),
                       pool="persist")
    gb_tok = gw_ld
    if gb_d is not None:
        gb_tok = ctx.qg.dma(gbr[0:1, 0:8], gb_d, pool="persist")

    gate_war = []
    for tt in range(16):
        if gb_d is not None:
            te.mm(PS[:, 6, 0:8], c["ones1b"][0:1, :], gbr[0:1, 0:8],
                  start=True, stop=False,
                  deps=[gw_ld, gb_tok, (ve, c["ready_ve"])] + x_dep + gate_war)
            first = False
        else:
            first = True
        for kb in range(NB):
            te.mm(PS[:, 6, 0:8], x3d[:, kb, P * tt:P * (tt + 1)],
                  gws[:, kb, :], start=(first and kb == 0),
                  stop=(kb == NB - 1),
                  deps=[gw_ld, gb_tok, (ve, c["ready_ve"])] + x_dep + gate_war,
                  inc=(kb == NB - 1))
        tcnt = te.cnt
        lg = gsc[:, 0:8]
        if softmax_then_topk:
            ve.op("reduce_max", gsc[:, 8:9], PS[:, 6, 0:8], AX.X,
                  negate=True, deps=[(te, tcnt)])
            se.op("activation", lg, PS[:, 6, 0:8], AF.Exp,
                  bias=gsc[:, 8:9], accum_out=gsc[:, 9:10],
                  deps=[(ve, ve.cnt), (te, tcnt)])
            ve.op("reciprocal", gsc[:, 10:11], gsc[:, 9:10],
                  deps=[(se, se.cnt)])
            ve.op("tensor_scalar_mul", lg, lg, gsc[:, 10:11])
        else:
            ve.op("tensor_copy", lg, PS[:, 6, 0:8], deps=[(te, tcnt)])
        vlg = ve.cnt
        gate_war = [(ve, ve.cnt), (se, se.cnt)]
        ve.op("max", gsc[:, 12:20], lg)
        ve.op("max_index", i8[:, :], gsc[:, 12:20], lg)
        ve.op("tensor_copy", gsc[:, 8:10], i8[:, 0:2])
        if softmax_then_topk:
            ve.op("tensor_copy", gsc[:, 10:11], gsc[:, 12:13])
            ve.op("tensor_copy", gsc[:, 11:12], gsc[:, 13:14])
            vw = ve.cnt
        else:
            ve.op("tensor_sub", gsc[:, 10:11], gsc[:, 12:13], gsc[:, 13:14])
            se.op("activation", gsc[:, 10:11], gsc[:, 10:11], AF.Sigmoid,
                  deps=[(ve, ve.cnt)])
            ve.op("tensor_scalar", gsc[:, 11:12], gsc[:, 10:11], -1.0, 1.0,
                  ALU.mult, ALU.add, deps=[(se, se.cnt)])
            vw = ve.cnt
        ve.op("scalar_tensor_tensor", wf[:, tt, :], iota8[:, :],
              gsc[:, 8:9], gsc[:, 10:11].broadcast_to([P, 8]),
              ALU.is_equal, ALU.mult, deps=[(gc, g_iota), (ve, vw)])
        ve.op("scalar_tensor_tensor", gsc[:, 12:20], iota8[:, :],
              gsc[:, 9:10], gsc[:, 11:12].broadcast_to([P, 8]),
              ALU.is_equal, ALU.mult)
        ve.op("tensor_add", wf[:, tt, :], wf[:, tt, :], gsc[:, 12:20])
    v_wf = ve.cnt
    bank_war = {b: 0 for b in range(8)}
    bank_war_se = {b: 0 for b in range(8)}
    bank_war[6] = v_wf

    w1_war = [te.cnt, te.cnt]
    w2_war = te.cnt
    b_war = [se.cnt, se.cnt]
    mid_se = {fb: 0 for fb in range(NB)}
    for e in range(8):
        b1v = b1f[:, 32 * (e % 2):32 * (e % 2) + H // P]
        b1_ld = qs.dma(b1v, b1_d[e], deps=[(se, b_war[e % 2])])
        b2_ld = qg.dma(b2r[0:1, 0:D], b2_d[e:e + 1, :], deps=[(te, w2_war)])
        for fc in range(NFC):
            wbi = (e * NFC + fc) % 2
            wb = w1bufs[wbi]
            w1_ld = wload3(ctx, wb, w1_d[e, :, 1024 * fc:1024 * (fc + 1)],
                           deps=[(te, w1_war[wbi])])
            w2_ld = wload3(ctx, w2buf,
                           w2_d[e, 1024 * fc:1024 * (fc + 1), :],
                           deps=[(te, w2_war)])
            for ch in range(4):
                t0 = 512 * ch
                for fb in range(NB):
                    bank = fb % 4
                    for kb in range(NB):
                        te.mm(PS[:, bank, :],
                              wb[:, kb, P * fb:P * (fb + 1)],
                              x3d[:, kb, t0:t0 + 512],
                              start=(kb == 0), stop=(kb == NB - 1),
                              deps=[w1_ld, (se, bank_war_se[bank])] + x_dep,
                              inc=(kb == NB - 1))
                    tcnt = te.cnt
                    se.op("activation", midv(fb)[:, :],
                          PS[:, bank, :], AF.Gelu,
                          bias=b1v[:, 8 * fc + fb:8 * fc + fb + 1],
                          deps=[(te, tcnt), b1_ld])
                    bank_war_se[bank] = se.cnt
                    mid_se[fb] = se.cnt
                for ts in range(4):
                    tt = 4 * ch + ts
                    for n in range(2):
                        ybank = 4 + 2 * (ts % 2) + n
                        if fc == 0:
                            te.mm(PS[:, ybank, :], c["ones1b"][0:1, :],
                                  b2r[0:1, 512 * n:512 * (n + 1)],
                                  start=True, stop=False,
                                  deps=[b2_ld, (ve, bank_war[ybank])])
                    for fb in range(NB):
                        for n in range(2):
                            ybank = 4 + 2 * (ts % 2) + n
                            te.mm(PS[:, ybank, :],
                                  midv(fb)[:, P * ts:P * (ts + 1)],
                                  w2buf[:, fb, 512 * n:512 * (n + 1)],
                                  start=(fc != 0 and fb == 0),
                                  stop=(fb == NB - 1),
                                  deps=[w2_ld, (se, mid_se[fb]),
                                        (ve, bank_war[ybank])],
                                  inc=(fb == NB - 1 and n == 1))
                    tcnt = te.cnt
                    for n in range(2):
                        ybank = 4 + 2 * (ts % 2) + n
                        dst = macc[:, tt, 512 * n:512 * (n + 1)]
                        if e == 0 and fc == 0:
                            ve.op("tensor_scalar_mul", dst,
                                  PS[:, ybank, :], wf[:, tt, e:e + 1],
                                  deps=[(te, tcnt), (ve, v_wf)])
                        else:
                            ve.op("scalar_tensor_tensor", dst,
                                  PS[:, ybank, :], wf[:, tt, e:e + 1],
                                  dst, ALU.mult, ALU.add,
                                  deps=[(te, tcnt)])
                        bank_war[ybank] = ve.cnt
            w1_war[wbi] = te.cnt
            w2_war = te.cnt
        b_war[e % 2] = se.cnt
    v_macc = ve.cnt

    for tt in range(16):
        for db in range(NB):
            bank = (tt * NB + db) % 2
            te.mm(PS[:, bank, 0:128], macc[:, tt, P * db:P * (db + 1)],
                  c["identb"][:], start=True, stop=True,
                  deps=[(ve, v_macc), (se, bank_war_se[bank]),
                        (ve, bank_war[bank]), (gc, c["ready_gc"])],
                  inc=True)
            tcnt = te.cnt
            ve.op("tensor_add", RES[:, db, P * tt:P * (tt + 1)],
                  RES[:, db, P * tt:P * (tt + 1)], PS[:, bank, 0:128],
                  deps=[(te, tcnt), (ve, res_dep)])
            bank_war[bank] = ve.cnt
    ar.pop()
    return ve.cnt


def run_layer1_build(taps=()):
    return build_program(layers=(1,), taps=taps)


# version 37 (final)
# speedup vs baseline: 1.0625x; 1.0163x over previous
import contextlib
import numpy as np
import ml_dtypes
import concourse.bass as bass
import concourse.mybir as mybir
from concourse.bass_utils import run_bass_kernel_spmd

F32 = mybir.dt.float32
BF16 = mybir.dt.bfloat16
U32 = mybir.dt.uint32
AF = mybir.ActivationFunctionType
ALU = mybir.AluOpType
AX = mybir.AxisListType

P = 128
T = 2048
D = 1024
NB = D // P



class Sem:
    def __init__(self, nc, es, name):
        self.h = es.enter_context(nc.semaphore(name))
        self.name = name


class Seq:

    def __init__(self, sem, autosync):
        self.raw = None
        self.sem = sem
        self.cnt = 0
        self.autosync = autosync
        self.waited = {}

    def wait(self, sem, val):
        if val <= 0:
            return
        if self.waited.get(sem.name, -1) >= val:
            return
        self.raw.wait_ge(sem.h, val)
        self.waited[sem.name] = val

    def dep(self, deps):
        for d in deps:
            if isinstance(d, Tok):
                i = d.sem._pool_i
                self.wait(d.sem, d.val)
                d.sem._pool_q.consumers[i].append((self, self.cnt + 1))
            else:
                s, v = d
                self.wait(s.sem if isinstance(s, (Seq, Q)) else s, v)

    def op(self, name, *a, deps=(), **kw):
        self.dep(deps)
        if self.autosync and self.cnt:
            self.wait(self.sem, self.cnt)
        inst = getattr(self.raw, name)(*a, **kw)
        inst.then_inc(self.sem.h, 1)
        self.cnt += 1
        return self.cnt

    def mm(self, out, lhsT, rhs, start, stop, deps=(), inc=False, **kw):
        self.dep(deps)
        inst = self.raw.matmul(out, lhsT, rhs, start=start, stop=stop, **kw)
        if inc:
            inst.then_inc(self.sem.h, 1)
            self.cnt += 1
        return self.cnt


class Tok:

    def __init__(self, sem, val):
        self.sem = sem
        self.val = val


class Q:

    def __init__(self, nc, es, name, raw_get, npool, naux=0, npers=0):
        self.raw = None
        self._raw_get = raw_get
        n = npool + naux + npers
        self.sems = [Sem(nc, es, f"{name}{i}") for i in range(n)]
        self.tot = [0] * n
        self.consumers = [[] for _ in range(n)]
        self.idx = 0
        self.npool = npool
        self.aux_lo = npool
        self.aux_n = naux
        self.aux_idx = 0
        self.pers_lo = npool + naux
        self.pers_n = npers
        self.pers_idx = 0
        self.waited = {}
        self.cur_batch = None

    def wait(self, sem, val):
        if val <= 0:
            return
        if self.waited.get(sem.name, -1) >= val:
            return
        self.raw.wait_ge(sem.h, val)
        self.waited[sem.name] = val

    def dep(self, deps):
        for d in deps:
            if isinstance(d, Tok):
                i = d.sem._pool_i
                self.wait(d.sem, d.val)
                d.sem._pool_q.consumers[i].append((self, None))
            else:
                s, v = d
                self.wait(s.sem if isinstance(s, (Seq, Q)) else s, v)

    def open_batch(self, pool="main"):
        self.cur_batch = self._next_tok(pool)
        return self.cur_batch

    def close_batch(self):
        t = self.cur_batch
        self.cur_batch = None
        return t

    def _next_tok(self, pool="main"):
        if pool == "main":
            i = self.idx % self.npool
            self.idx += 1
        elif pool == "aux":
            i = self.aux_lo + self.aux_idx % self.aux_n
            self.aux_idx += 1
        else:
            assert self.pers_idx < self.pers_n, "persist sems exhausted"
            i = self.pers_lo + self.pers_idx
            self.pers_idx += 1
        sem = self.sems[i]
        sem._pool_i = i
        sem._pool_q = self
        gate = self.consumers[i]
        self.consumers[i] = []
        for c in gate:
            if isinstance(c[0], Seq):
                self.wait(c[0].sem, c[1])
            elif isinstance(c[0], Q):
                pass
        return Tok(sem, self.tot[i])

    def dma(self, out, in_, deps=(), batch=None, pool="main", **kw):
        self.dep(deps)
        if batch is None and self.cur_batch is not None:
            batch = self.cur_batch
        tok = batch if batch is not None else self._next_tok(pool)
        i = tok.sem._pool_i
        self.raw.dma_start(out=out, in_=in_, **kw).then_inc(tok.sem.h, 16)
        self.tot[i] += 16
        tok.val = self.tot[i]
        return tok


class Ctx:
    def __init__(self, nc, es, taps=()):
        self.nc = nc
        self.es = es
        self.taps = set(taps)
        self.tap_list = []
        self.s_te = Sem(nc, es, "s_te")
        self.s_ve = Sem(nc, es, "s_ve")
        self.s_se = Sem(nc, es, "s_se")
        self.s_gc = Sem(nc, es, "s_gc")
        self.te = Seq(self.s_te, autosync=False)
        self.ve = Seq(self.s_ve, autosync=True)
        self.se = Seq(self.s_se, autosync=True)
        self.gc = Seq(self.s_gc, autosync=True)
        self.qs = Q(nc, es, "qs", None, 6, naux=0, npers=6)
        self.qg = Q(nc, es, "qg", None, 10, naux=20, npers=12)

    def bind(self):
        nc = self.nc
        self.te.raw = nc.tensor
        self.ve.raw = nc.vector
        self.se.raw = nc.scalar
        self.gc.raw = nc.gpsimd
        self.qs.raw = nc.sync
        self.qg.raw = nc.gpsimd

    def sb(self, name, shape, dtype):
        return self.es.enter_context(self.nc.sbuf_tensor(name, shape, dtype))

    def tap(self, name, src_ap, deps, shape, rearr=None):
        if name not in self.taps:
            return
        t = self.nc.dram_tensor("tap_" + name, shape, F32,
                                kind="ExternalOutput").ap()
        dst = t if rearr is None else t.rearrange(*rearr[0], **rearr[1])
        self.qg.dma(dst, src_ap, deps=deps)
        self.tap_list.append(name)


class Arena:

    def __init__(self, ctx, name, cols):
        self.t = ctx.sb(name, [P, cols], BF16)
        self.cols = cols
        self.off = 0
        self.marks = []

    def push(self):
        self.marks.append(self.off)

    def pop(self):
        self.off = self.marks.pop()

    def alloc(self, cols):
        cols = (cols + 31) // 32 * 32
        assert self.off + cols <= self.cols, \
            f"arena overflow: {self.off}+{cols} > {self.cols}"
        v = self.t[:, self.off:self.off + cols]
        self.off += cols
        return v

    def alloc3(self, nb, t):
        return self.alloc(nb * t).rearrange("p (b t) -> p b t", b=nb)


def barrier(ctx):
    engs = [ctx.te, ctx.ve, ctx.se, ctx.gc]
    qs = [ctx.qs, ctx.qg]
    cnts = [(e.sem, e.cnt) for e in engs]
    for q in qs:
        for sem, tot in zip(q.sems, q.tot):
            if tot > 0:
                cnts.append((sem, tot))
    for e in engs + qs:
        for sem, v in cnts:
            if getattr(e, "sem", None) is not sem:
                e.wait(sem, v)



def emit_consts(ctx):
    c = {}
    c["ones1b"] = ctx.sb("c_ones1b", [1, P], BF16)
    c["ones1f"] = ctx.sb("c_ones1f", [1, P], F32)
    c["onescolb"] = ctx.sb("c_onescolb", [P, 1], BF16)
    c["identb"] = ctx.sb("c_identb", [P, P], BF16)
    c["cpool"] = ctx.sb("c_cpool", [P, 8], F32)
    ve, gc = ctx.ve, ctx.gc
    ve.op("memset", c["ones1b"][:], 1.0)
    ve.op("memset", c["ones1f"][:], 1.0)
    ve.op("memset", c["onescolb"][:], 1.0)
    ve.op("memset", c["identb"][:], 1.0)
    ve.op("memset", c["cpool"][:, 0:1], 1e-5)
    ve.op("memset", c["cpool"][:, 1:2], 1e-6)
    ve.op("memset", c["cpool"][:, 2:3], 0.125)
    ve.op("memset", c["cpool"][:, 3:4], 1.0)
    ve.op("memset", c["cpool"][:, 4:5], 0.0)
    ve.op("memset", c["cpool"][:, 5:6], -1.0)
    mv = ve.cnt
    gc.op("affine_select", c["identb"][:], c["identb"][:], [[1, P]],
          ALU.is_equal, 0.0, base=0, channel_multiplier=-1,
          deps=[(ctx.ve, mv)])
    c["ready_gc"] = gc.cnt
    c["ready_ve"] = ve.cnt
    ctx.consts = c
    return c



def emit_ln(ctx, src3d, src_is_f32, dst3d, scr, gamma=None, beta=None,
            src_dep=()):
    te, ve, se = ctx.te, ctx.ve, ctx.se
    c = ctx.consts
    PS = ctx.PS
    src_dep = list(src_dep) + [(ctx.se, ctx.se.cnt), (ctx.ve, ctx.ve.cnt)]

    if src_is_f32:
        cast = scr["cast3d"]
        for b in range(NB):
            ve.op("tensor_copy", cast[:, b, :], src3d[:, b, :], deps=src_dep)
        xb3 = cast
    else:
        xb3 = src3d
    vcast = ve.cnt

    for tc in range(4):
        for b in range(NB):
            te.mm(PS[0:1, 4 + tc, :], c["onescolb"][:],
                  xb3[:, b, 512 * tc:512 * (tc + 1)],
                  start=(b == 0), stop=(b == NB - 1),
                  deps=([(ve, vcast)] + src_dep + [(ve, c["ready_ve"])]),
                  inc=(b == NB - 1))
    sq = scr["sq"]
    for b in range(NB):
        se.op("activation", sq[:, :], xb3[:, b, :], AF.Square,
              deps=[(ve, vcast), (te, te.cnt)] + src_dep)
        scnt = se.cnt
        for tc in range(4):
            te.mm(PS[32:33, 4 + tc, :], c["onescolb"][:],
                  sq[:, 512 * tc:512 * (tc + 1)],
                  start=(b == 0), stop=(b == NB - 1),
                  deps=[(se, scnt)], inc=(tc == 3))
    tsums = te.cnt

    rows = scr["rows"]
    r0 = rows[0:1, 0:T]
    r1 = rows[0:1, T:2 * T]
    r2 = rows[32:33, 0:T]
    r3 = rows[32:33, T:2 * T]
    sum_row = PS[0:1, 4:8, :].rearrange("p a b -> p (a b)")
    sq_row = PS[32:33, 4:8, :].rearrange("p a b -> p (a b)")
    ve.op("tensor_scalar_mul", r0, sum_row, 1.0 / D, deps=[(te, tsums)])
    ve.op("tensor_scalar_mul", r2, sq_row, 1.0 / D, deps=[(te, tsums)])
    ve.op("tensor_mul", r3, r0, r0)
    ve.op("tensor_sub", r2, r2, r3)
    vvar = ve.cnt
    se.op("activation", r3, r2, AF.Sqrt, bias=c["cpool"][0:1, 0:1],
          deps=[(ve, vvar)])
    ssq = se.cnt
    ve.op("reciprocal", r1, r3, deps=[(se, ssq)])
    vrows = ve.cnt

    mub, rsb = scr["mub"], scr["rsb"]
    for what, dstb in ((r0, scr["mub"]), (r1, scr["rsb"])):
        dep = [(ve, ve.cnt)]
        for tc in range(4):
            te.mm(PS[:, tc, :], c["ones1b"][:],
                  what[0:1, 512 * tc:512 * (tc + 1)],
                  start=True, stop=True, deps=dep + [(ve, vrows)], inc=True)
        tb = te.cnt
        for tc in range(4):
            ve.op("tensor_copy", dstb[:, 512 * tc:512 * (tc + 1)],
                  PS[:, tc, :], deps=[(te, tb)])
    vbc = ve.cnt

    for b in range(NB):
        ve.op("tensor_sub", dst3d[:, b, :], xb3[:, b, :], mub[:, :],
              deps=[(ve, vbc)] + src_dep)
        ve.op("tensor_mul", dst3d[:, b, :], dst3d[:, b, :], rsb[:, :])
        if gamma is not None:
            ve.op("tensor_scalar", dst3d[:, b, :], dst3d[:, b, :],
                  gamma[:, b:b + 1], beta[:, b:b + 1],
                  ALU.mult, ALU.add)
    return ve.cnt



def make_ln_scratch(ctx):
    return {
        "sq": ctx.sb("ln_sq", [P, T], BF16),
        "rows": ctx.sb("ln_rows", [33, 2 * T], BF16),
        "mub": ctx.sb("ln_mub", [P, T], BF16),
        "rsb": ctx.sb("ln_rsb", [P, T], BF16),
    }


DR_SHAPES = {
    "xT": [D, T],
    "w_qT": [D, D], "w_kT": [D, D], "w_vT": [D, D],
    "bq_a": [P, NB], "bk_a": [P, NB], "bv_a": [D, 1],
    "w_oT": [D, D], "b_o": [P, NB],
    "ln1_g": [P, NB], "ln1_b": [P, NB], "ln2_g": [P, NB], "ln2_b": [P, NB],
    "wq0": [D, D], "bq0": [P, NB], "wk0": [D, D], "bk0": [P, NB],
    "wv0": [D, D], "bv0": [D, 1], "wp0": [D, D], "bp0": [P, NB],
    "gate_w0": [D, 8], "gate_b0": [1, 8],
    "exp_w1": [8, D, 4 * D], "exp_b1": [8, P, 32],
    "exp_w2": [8, 4 * D, D], "exp_b2": [8, D],
    "wq1": [D, D], "wk1": [D, D], "wv1": [D, D], "wb1": [D, 4],
    "conv_q": [P, NB, 4], "conv_k": [P, NB, 4], "conv_v": [P, NB, 4],
    "fir_long": [P, NB, 64], "fir_short": [P, NB, 5],
    "g_w1": [D + 16, D], "g_b1": [P, NB], "g_w2": [D, 4], "g_b2": [4, 1],
    "log_temp": [1, 4], "base_bias": [4, 4], "crl": [1, 4], "rms_w": [P, 2],
    "wo1": [D, D], "gate_w1": [D, 8],
    "m_w1": [8, D, D], "m_b1": [8, P, 8], "m_w2": [8, D, D], "m_b2": [8, D],
}

L0_NAMES = ["w_qT", "w_kT", "w_vT", "bq_a", "bk_a", "bv_a",
            "w_oT", "b_o", "ln1_g", "ln1_b", "ln2_g",
            "ln2_b", "wq0", "bq0", "wk0", "bk0", "wv0", "bv0", "wp0", "bp0",
            "gate_w0", "gate_b0", "exp_w1", "exp_b1", "exp_w2", "exp_b2"]
L1_NAMES = ["wq1", "wk1", "wv1", "wb1", "conv_q", "conv_k", "conv_v",
            "fir_long", "fir_short", "g_w1", "g_b1", "g_w2", "g_b2",
            "log_temp", "base_bias", "crl", "rms_w", "wo1", "gate_w1",
            "m_w1", "m_b1", "m_w2", "m_b2"]


def build_program(layers=(0, 1), taps=()):
    nc = bass.Bass("TRN2", target_bir_lowering=False, debug=False)
    dr = {"xT": nc.dram_tensor("xT", DR_SHAPES["xT"], F32,
                               kind="ExternalInput").ap()}
    use = (L0_NAMES if 0 in layers else []) + (L1_NAMES if 1 in layers else [])
    for nm in use:
        dr[nm] = nc.dram_tensor(nm, DR_SHAPES[nm], F32,
                                kind="ExternalInput").ap()
    out = nc.dram_tensor("outT", [D, T], F32, kind="ExternalOutput").ap()

    es = contextlib.ExitStack()
    with es:
        block = es.enter_context(nc.Block())
        ctx = Ctx(nc, es, taps)
        ctx.dr = dr
        ctx.PS = es.enter_context(nc.psum_tensor("PS", [P, 8, 512], F32))

        RES = ctx.sb("RES", [P, NB, T], F32)
        SCR_A = ctx.sb("SCR_A", [P, NB, T], BF16)
        ctx.RES, ctx.SCR_A = RES, SCR_A
        ctx.arena = Arena(ctx, "ARENA", 41088)
        assert 41088 >= 41056 + 16
        ctx.fcol = ctx.sb("fcol", [P, 16], F32)
        ctx.rowf = ctx.sb("rowf", [1, 512], F32)
        ctx.S0 = ctx.sb("S0", [P, 2, 256], F32)
        ctx.rowbuf = ctx.sb("rowbuf", [1, D], BF16)
        ctx.cmcolb = ctx.sb("cmcolb", [P, 2], BF16)

        @block.tensor
        def _(tensor):
            ctx.bind()
            lp = ctx.es.enter_context(nc.allow_low_precision(
                reason="bf16 stat rows within stated tolerance"))
            emit_consts(ctx)
            ln_scr = make_ln_scratch(ctx)
            ctx.ln_scr = ln_scr

            xld = ctx.qs.dma(RES[:, :, :],
                             dr["xT"].rearrange("(b p) t -> p b t", p=P),
                             pool="persist")
            ctx.xld = xld

            if 0 in layers:
                ctx.arena.push()
                ln_scr["cast3d"] = ctx.arena.alloc3(NB, T)
                v0 = emit_ln(ctx, RES, True, SCR_A, ln_scr, src_dep=[xld])
                ctx.arena.pop()
                ctx.tap("xn0", SCR_A[:, :, :], [(ctx.ve, v0)], [D, T],
                        rearr=(("(b p) t -> p b t",), {"p": P}))
                emit_layer0(ctx, ln_scr, v0)

            if 1 in layers:
                emit_layer1(ctx, ln_scr)

            barrier(ctx)
            st_tok = ctx.qs.dma(out.rearrange("(b p) t -> p b t", p=P),
                                RES[:, :, :])
            ctx.qs.wait(st_tok.sem, st_tok.val)
            ctx.gc.dep([st_tok])

    return nc, dr


def emit_layer0(ctx, ln_scr, v_xn0):
    te, ve, se, gc, qs, qg = ctx.te, ctx.ve, ctx.se, ctx.gc, ctx.qs, ctx.qg
    c = ctx.consts
    dr = ctx.dr
    ar = ctx.arena
    xn0, RES = ctx.SCR_A, ctx.RES

    SM = ctx.sb("SM0", [P, 96], F32)
    smap = {}
    qs.open_batch(pool="persist")
    for i, nm in enumerate(["ln1_g", "ln1_b", "ln2_g", "ln2_b", "bq0", "bk0",
                            "bp0", "b_o", "bq_a", "bk_a"]):
        smap[nm] = SM[:, 8 * i:8 * (i + 1)]
        sm_tok = qs.dma(smap[nm], dr[nm])
    qs.close_batch()
    rowbuf = ctx.rowbuf
    rb_tok = qg.dma(rowbuf[:, :], dr["bv0"].rearrange("d o -> o d"),
                    pool="persist")

    ar.push()

    h2 = ar.alloc3(NB, T)
    v_h2 = emit_ln(ctx, xn0, False, h2, ln_scr, gamma=smap["ln1_g"],
                   beta=smap["ln1_b"],
                   src_dep=[(ve, v_xn0), sm_tok])
    ctx.tap("h2", h2[:, :, :], [(ve, v_h2)], [D, T],
            rearr=(("(b p) t -> p b t",), {"p": P}))

    ACC = h2

    def evac_inner(ps_ap, mb, tc, tecnt):
        ve.op("scalar_tensor_tensor", ACC[:, mb, 512 * tc:512 * (tc + 1)],
              ps_ap, smap["bp0"][:, mb:mb + 1],
              xn0[:, mb, 512 * tc:512 * (tc + 1)],
              ALU.add, ALU.add, deps=[(te, tecnt), (ve, v_xn0)])

    w_in = {"wq": dr["wq0"], "wk": dr["wk0"], "wv": dr["wv0"],
            "wo": dr["wp0"], "bqf": smap["bq0"], "bkf": smap["bk0"],
            "bvr": rowbuf[:, :]}
    r = emit_mha(ctx, h2, [(ve, v_h2), rb_tok, sm_tok], w_in,
                 causal=True, evac_out=evac_inner, name="att0i",
                 spill_o=True)
    v_acc = r["ve"]
    ctx.tap("acc", ACC[:, :, :], [(ve, v_acc)], [D, T],
            rearr=(("(b p) t -> p b t",), {"p": P}))

    for b in range(NB):
        ve.op("tensor_add", RES[:, b, :], RES[:, b, :], ACC[:, b, :],
              deps=[(ve, v_acc)])
    v_res1 = ve.cnt

    moein = ACC
    v_moein = emit_ln(ctx, ACC, False, moein, ln_scr, gamma=smap["ln2_g"],
                      beta=smap["ln2_b"], src_dep=[(ve, v_res1)])
    ctx.tap("moein", moein[:, :, :], [(ve, v_moein)], [D, T],
            rearr=(("(b p) t -> p b t",), {"p": P}))

    rb_tok = qg.dma(rowbuf[:, :], dr["bv_a"].rearrange("d o -> o d"),
                    deps=[(ctx.te, r["te"])], pool="persist")
    def evac_outer(ps_ap, mb, tc, tecnt):
        ve.op("scalar_tensor_tensor", RES[:, mb, 512 * tc:512 * (tc + 1)],
              ps_ap, smap["b_o"][:, mb:mb + 1],
              RES[:, mb, 512 * tc:512 * (tc + 1)],
              ALU.add, ALU.add, deps=[(te, tecnt), (ve, v_res1)])

    w_out = {"wq": dr["w_qT"], "wk": dr["w_kT"], "wv": dr["w_vT"],
             "wo": dr["w_oT"], "bqf": smap["bq_a"], "bkf": smap["bk_a"],
             "bvr": rowbuf[:, :]}
    r = emit_mha(ctx, xn0, [(ve, v_xn0), rb_tok, sm_tok], w_out,
                 causal=False, evac_out=evac_outer, name="att0o",
                 spill_o=True)
    v_res2 = r["ve"]
    ctx.tap("res_preMoE", RES[:, :, :], [(ve, v_res2)], [D, T],
            rearr=(("(b p) t -> p b t",), {"p": P}))

    w1bufs = [ctx.SCR_A[:, :, 0:1024], ctx.SCR_A[:, :, 1024:2048]]
    emit_moe(ctx, moein, [(ve, v_moein)], dr["exp_w1"], dr["exp_b1"],
             dr["exp_w2"], dr["exp_b2"], dr["gate_w0"], dr["gate_b0"],
             H=4 * D, softmax_then_topk=False, w1bufs=w1bufs,
             res_dep=v_res2, name="moe0")
    ar.pop()


def emit_layer1(ctx, ln_scr):
    te, ve, se, gc, qs, qg = ctx.te, ctx.ve, ctx.se, ctx.gc, ctx.qs, ctx.qg
    c = ctx.consts
    dr = ctx.dr
    PS = ctx.PS
    ar = ctx.arena
    xn1, RES = ctx.SCR_A, ctx.RES
    rows = ln_scr["rows"]
    sq = ln_scr["sq"]
    mub = ln_scr["mub"]
    rsb = ln_scr["rsb"]

    ar.push()
    mskA = ar.alloc(2 * P)
    mskL = ar.alloc(P)
    self_j = ar.alloc(4 * P)
    beta_r = ar.alloc(T)
    wsl_flat = ar.alloc(2048)
    wsl = wsl_flat.rearrange("p (b f) -> p b f", b=NB)
    FIRL = ar.alloc(NB * 64).rearrange("p (b k) -> p b k", b=NB)
    stm = ar.alloc(64)
    st16 = ar.alloc(P)
    idt4 = ar.alloc(32)
    W2S = ar.alloc(NB * 4).rearrange("p (b k) -> p b k", b=NB)
    ve.op("memset", mskA[:, 0:P], -1.0)
    ve.op("memset", mskA[:, P:2 * P], -1.0)
    ve.op("memset", mskL[:, :], 1.0)
    ve.op("memset", self_j[:, :], 1.0)
    mv = ve.cnt
    gc.op("affine_select", mskA[:, 0:P], mskA[:, 0:P], [[-1, P]], ALU.is_gt,
          0.0, base=0, channel_multiplier=1, deps=[(ve, mv)])
    gc.op("affine_select", mskA[:, P:2 * P], mskA[:, P:2 * P], [[1, P]],
          ALU.is_gt, 0.0, base=0, channel_multiplier=-1)
    gc.op("affine_select", mskL[:, :], mskL[:, :], [[1, P]], ALU.is_ge,
          0.0, base=0, channel_multiplier=-1)
    for j in range(4):
        gc.op("affine_select", self_j[0:4, P * j:P * (j + 1)],
              self_j[0:4, P * j:P * (j + 1)], [[0, P]], ALU.is_equal, 0.0,
              base=-j, channel_multiplier=1)
    g_msk = gc.cnt
    w2s_tok = qg.dma(W2S[:, :, :],
                     dr["g_w2"].rearrange("(b p) f -> p b f", p=P),
                     pool="persist")

    firl_tok = qg.dma(FIRL[:, :, :], dr["fir_long"], pool="persist")

    SM = ctx.sb("SM1", [P, 16], F32)
    sm_tok = qs.open_batch(pool="persist")
    qs.dma(SM[:, 0:8], dr["g_b1"])
    qs.dma(SM[:, 8:10], dr["rms_w"])
    CW = ctx.sb("l1_cw", [P, NB, 12], F32)
    qs.dma(CW[:, :, 0:4], dr["conv_q"])
    qs.dma(CW[:, :, 4:8], dr["conv_k"])
    qs.dma(CW[:, :, 8:12], dr["conv_v"])
    CWB = ctx.sb("l1_cwb", [P, NB, 12], BF16)
    qg.open_batch(pool="persist")
    qg.dma(CWB[:, :, 0:4], dr["conv_q"])
    qg.dma(CWB[:, :, 4:8], dr["conv_k"])
    cwb_tok = qg.dma(CWB[:, :, 8:12], dr["conv_v"])
    qg.close_batch()
    FIRS = ctx.sb("l1_firs", [P, NB, 5], F32)
    qs.dma(FIRS[:, :, :], dr["fir_short"])
    SR = ctx.sb("l1_sr", [4, 24], F32)
    qs.dma(SR[0:1, 0:4], dr["log_temp"])
    qs.dma(SR[0:1, 4:8], dr["crl"])
    BB = ctx.sb("l1_bb", [4, 8], F32)
    qs.dma(BB[0:4, 0:4], dr["base_bias"])
    qs.dma(BB[0:4, 4:5], dr["g_b2"])
    sm_tok = qs.close_batch()

    ar.push()
    ln_scr["cast3d"] = ar.alloc3(NB, T)
    v_res = ve.cnt
    v_xn1 = emit_ln(ctx, RES, True, xn1, ln_scr,
                    src_dep=[(ve, v_res), ctx.xld])
    ar.pop()
    ctx.tap("xn1", xn1[:, :, :], [(ve, v_xn1)], [D, T],
            rearr=(("(b p) t -> p b t",), {"p": P}))

    wbs = ar.alloc3(NB, 4)
    wb_ld = qg.dma(wbs, dr["wb1"].rearrange("(b p) f -> p b f", p=P),
                   pool="persist", deps=[(ctx.ve, v_xn1), (ctx.se, se.cnt)])

    emit_l1_heads(ctx, xn1, v_xn1, beta_r, (wbs, wb_ld), wsl,
                  dict(sm=sm_tok, SM=SM, CW=CW, FIRS=FIRS, FIRL=FIRL, SR=SR,
                       BB=BB, mskA=mskA, mskL=mskL, self_j=self_j,
                       g_msk=g_msk, wsl_flat=wsl_flat, firl=firl_tok,
                       idt4=idt4, W2S=W2S, w2s_tok=w2s_tok, stm=stm,
                       st16=st16, CWB=CWB, cwb_tok=cwb_tok))
    v_res2 = ctx.ve.cnt
    ar.pop()
    ctx.tap("res_preMoE1", RES[:, :, :], [(ve, v_res2)], [D, T],
            rearr=(("(b p) t -> p b t",), {"p": P}))

    ar.push()
    w1bufs = [ar.alloc3(NB, 1024), ar.alloc3(NB, 1024)]
    emit_moe(ctx, xn1, [(ve, v_xn1)], dr["m_w1"], dr["m_b1"],
             dr["m_w2"], dr["m_b2"], dr["gate_w1"], None,
             H=D, softmax_then_topk=True, w1bufs=w1bufs,
             res_dep=v_res2, name="moe1")
    ar.pop()


def emit_l1_heads(ctx, xn1, v_xn1, beta_r, wb, wsl, aux):
    wbs, wb_ld = wb
    te, ve, se, gc, qs, qg = ctx.te, ctx.ve, ctx.se, ctx.gc, ctx.qs, ctx.qg
    RES = ctx.RES
    c = ctx.consts
    dr = ctx.dr
    PS = ctx.PS
    ar = ctx.arena
    rows = ctx.ln_scr["rows"]
    sq = ctx.ln_scr["sq"]
    mub = ctx.ln_scr["mub"]
    rsb = ctx.ln_scr["rsb"]
    SM, CW, FIRS, FIRL = aux["SM"], aux["CW"], aux["FIRS"], aux["FIRL"]
    SR, BB = aux["SR"], aux["BB"]
    mskA, mskL, self_j = aux["mskA"], aux["mskL"], aux["self_j"]
    sm_tok = aux["sm"]
    NCH = 16

    S0 = ctx.S0

    idt4 = aux["idt4"][0:4, 0:4]
    W2S = aux["W2S"]
    w2s_tok = aux["w2s_tok"]
    FLOOR = ctx.sb("l1_floor", [P, 16], F32)
    stm = aux["stm"]
    st16 = aux["st16"][0:16, :]
    CWB = aux["CWB"]
    cwb_tok = aux["cwb_tok"]
    st16_war = []
    for jj in range(4):
        ve.op("memset", FLOOR[:, 4 * jj:4 * jj + 1], 0.05)
        ve.op("memset", FLOOR[:, 4 * jj + 1:4 * jj + 2], 0.05)
        ve.op("memset", FLOOR[:, 4 * jj + 2:4 * jj + 4], 0.0)
    v_floor = ve.cnt

    se.op("activation", SR[0:1, 8:12], SR[0:1, 0:4], AF.Abs, deps=[sm_tok])
    se.op("activation", SR[0:1, 8:12], SR[0:1, 8:12], AF.Exp,
          scale=ctx.consts["cpool"][0:1, 5:6])
    se.op("activation", SR[0:1, 8:12], SR[0:1, 8:12], AF.Ln,
          bias=c["cpool"][0:1, 3:4])
    s_sp = se.cnt
    ve.op("tensor_scalar_max", SR[0:1, 16:20], SR[0:1, 0:4], 0.0,
          deps=[sm_tok, (se, s_sp)])
    ve.op("tensor_add", SR[0:1, 16:20], SR[0:1, 16:20], SR[0:1, 8:12])
    ve.op("tensor_scalar_add", SR[0:1, 16:20], SR[0:1, 16:20], 1e-4)
    ve.op("reciprocal", SR[0:1, 8:12], SR[0:1, 16:20])
    v_itemp = ve.cnt
    se.op("activation", SR[0:1, 12:16], SR[0:1, 4:8], AF.Sigmoid,
          deps=[sm_tok, (ve, v_itemp)])
    s_crl = se.cnt
    ve.op("tensor_scalar_mul", SR[0:1, 12:16], SR[0:1, 12:16], 0.5,
          deps=[(se, s_crl)])
    v_crl = ve.cnt

    bank_ve = {b: 0 for b in range(8)}
    bank_se = {b: 0 for b in range(8)}

    for h in range(4):
        ar.push()
        Bv = ar.alloc(2 * (64 + T)).rearrange("p (b t) -> p b t", b=2)
        delta = ar.alloc3(2, T)
        Bq = ar.alloc3(2, T)
        Bk = ar.alloc3(2, T)
        o_h = ar.alloc3(2, T)

        ar.push()
        A_pad = ar.alloc(2 * (64 + T)).rearrange("p (b t) -> p b t", b=2)
        ve.op("memset", A_pad[:, :, 0:64], 0.0)
        ve.op("memset", Bv[:, :, 0:64], 0.0)
        v_pad = ve.cnt
        conv_done = {}
        for ti, (wnm, cw0, Bdst, off) in enumerate(
                (("wq1", 0, Bq, 0), ("wk1", 4, Bk, 0), ("wv1", 8, Bv, 64))):
            w_ld = wload3(ctx, wsl[:, :, 0:256],
                          dr[wnm][:, 256 * h:256 * (h + 1)],
                          deps=[(te, te.cnt)])
            for mb in range(2):
                for kb in range(NB):
                    for tc in range(4):
                        te.mm(PS[:, 4 + tc, :],
                              wsl[:, kb, 128 * mb:128 * (mb + 1)],
                              xn1[:, kb, 512 * tc:512 * (tc + 1)],
                              start=(kb == 0), stop=(kb == NB - 1),
                              deps=[w_ld, (ve, v_xn1), (ve, bank_ve[4 + tc]),
                                    (se, bank_se[4 + tc])],
                              inc=(kb == NB - 1 and tc == 3))
                tcnt = te.cnt
                for tc in range(4):
                    ve.op("tensor_copy",
                          A_pad[:, mb, 64 + 512 * tc:64 + 512 * (tc + 1)],
                          PS[:, 4 + tc, :], deps=[(te, tcnt), (ve, v_pad)])
                    bank_ve[4 + tc] = ve.cnt
            dt4 = [sq[:, 128 * jj:128 * (jj + 1)] for jj in range(8)]
            gdeps = [(se, se.cnt), (ve, ve.cnt), cwb_tok, (te, te.cnt)]
            for mb in range(2):
                gb = 2 * h + mb
                for j in range(4):
                    gc.op("affine_select", dt4[4 * mb + j],
                          CWB[:, gb, cw0 + j:cw0 + j + 1].broadcast_to([P, P]),
                          [[1, P]], ALU.is_equal, 0.0, base=0,
                          channel_multiplier=-1, deps=gdeps)
            gcv = gc.cnt
            for mb in range(2):
                for j in range(4):
                    for tc in range(4):
                        te.mm(PS[:, 4 + tc, :], dt4[4 * mb + j],
                              A_pad[:, mb, 61 + j + 512 * tc:
                                    61 + j + 512 * (tc + 1)],
                              start=(j == 0), stop=(j == 3),
                              deps=[(gc, gcv), (ve, bank_ve[4 + tc]),
                                    (se, bank_se[4 + tc])],
                              inc=(j == 3 and tc == 3))
                tcnt = te.cnt
                for tc in range(4):
                    se.op("activation",
                          Bdst[:, mb, off + 512 * tc:off + 512 * (tc + 1)],
                          PS[:, 4 + tc, :], AF.Silu, deps=[(te, tcnt)])
                    bank_se[4 + tc] = se.cnt
            conv_done[ti] = se.cnt
            conv_done[ti] = se.cnt
        ar.pop()

        for Bsrc, sdep in ((Bq, conv_done[0]), (Bk, conv_done[1])):
            for mb in range(2):
                se.op("activation", sq[:, :], Bsrc[:, mb, :], AF.Square,
                      deps=[(se, sdep), (te, te.cnt)])
                scnt = se.cnt
                for tc in range(4):
                    te.mm(PS[0:1, 4 + tc, :], c["onescolb"][:, :],
                          sq[:, 512 * tc:512 * (tc + 1)],
                          start=(mb == 0), stop=(mb == 1),
                          deps=[(se, scnt), (ve, bank_ve[4 + tc]),
                                (se, bank_se[4 + tc])],
                          inc=(tc == 3))
            tsum = te.cnt
            se.op("activation", rows[0:1, 0:T],
                  PS[0:1, 4:8, :].rearrange("p a b -> p (a b)"), AF.Sqrt,
                  bias=c["cpool"][0:1, 1:2], deps=[(te, tsum)])
            ssq_ = se.cnt
            for tc in range(4):
                bank_se[4 + tc] = ssq_
            ve.op("reciprocal", rows[0:1, 0:T], rows[0:1, 0:T],
                  deps=[(se, ssq_)])
            vr = ve.cnt
            for tc in range(4):
                te.mm(PS[:, tc, :], c["ones1b"][:, :],
                      rows[0:1, 512 * tc:512 * (tc + 1)],
                      start=True, stop=True,
                      deps=[(ve, vr), (ve, bank_ve[tc])], inc=True)
            tbc = te.cnt
            for tc in range(4):
                ve.op("tensor_copy", mub[:, 512 * tc:512 * (tc + 1)],
                      PS[:, tc, :], deps=[(te, tbc)])
                bank_ve[tc] = ve.cnt
            for mb in range(2):
                ve.op("tensor_mul", Bsrc[:, mb, :], Bsrc[:, mb, :], mub[:, :])
            for tc in range(4):
                bank_ve[4 + tc] = ve.cnt
        v_norm = ve.cnt

        ar.push()
        kb_fm = ar.alloc3(2, T)
        vb_fm = ar.alloc3(2, T)
        Xb = ar.alloc(16 * P).rearrange("p (i q) -> p i q", i=16)
        XTb = ar.alloc(16 * P).rearrange("p (i q) -> p i q", i=16)
        for tc in range(4):
            for kb in range(NB):
                te.mm(PS[0:1, 4 + tc, :], wbs[:, kb, h:h + 1],
                      xn1[:, kb, 512 * tc:512 * (tc + 1)],
                      start=(kb == 0), stop=(kb == NB - 1),
                      deps=[wb_ld, (ve, v_xn1), (se, bank_se[4 + tc]),
                            (ve, bank_ve[4 + tc])],
                      inc=(kb == NB - 1))
        tcnt = te.cnt
        se.op("activation", beta_r[0:1, 0:T],
              PS[0:1, 4:8, :].rearrange("p a b -> p (a b)"), AF.Sigmoid,
              deps=[(te, tcnt)])
        s_beta = se.cnt
        for tc in range(4):
            bank_se[4 + tc] = s_beta
        for tc in range(4):
            te.mm(PS[:, tc, :], c["ones1b"][:, :],
                  beta_r[0:1, 512 * tc:512 * (tc + 1)],
                  start=True, stop=True,
                  deps=[(se, s_beta), (ve, v_norm), (ve, bank_ve[tc])],
                  inc=True)
        tbc = te.cnt
        for tc in range(4):
            ve.op("tensor_copy", mub[:, 512 * tc:512 * (tc + 1)],
                  PS[:, tc, :], deps=[(te, tbc)])
            bank_ve[tc] = ve.cnt
        for mb in range(2):
            ve.op("tensor_mul", kb_fm[:, mb, :], Bk[:, mb, :], mub[:, :],
                  deps=[(se, conv_done[1])])
            ve.op("tensor_mul", vb_fm[:, mb, :], Bv[:, mb, 64:64 + T],
                  mub[:, :], deps=[(se, conv_done[2])])
        v_kbvb = ve.cnt

        ve.op("memset", S0[:, :, :], 0.0, deps=[(te, te.cnt)])
        ve.op("memset", rsb[:, 1536:2048], 0.0)
        v_s = ve.cnt
        bank_ve = {b: max(bank_ve[b], v_s) for b in range(8)}

        Nc = sq[:, 0:512].rearrange("p (i q) -> p i q", i=4)
        NTc = sq[:, 512:1024].rearrange("p (i q) -> p i q", i=4)
        for grp in range(4):
            g0 = 4 * grp
            for ii in range(4):
                i = g0 + ii
                csl = slice(P * i, P * (i + 1))
                for mb in range(2):
                    te.mm(PS[:, 0, P * ii:P * (ii + 1)],
                          kb_fm[:, mb, csl], Bk[:, mb, csl],
                          start=(mb == 0), stop=(mb == 1),
                          deps=[(ve, v_kbvb), (ve, bank_ve[0])],
                          inc=(mb == 1))
                    te.mm(PS[:, 1, P * ii:P * (ii + 1)],
                          Bk[:, mb, csl], kb_fm[:, mb, csl],
                          start=(mb == 0), stop=(mb == 1),
                          deps=[(ve, bank_ve[1])], inc=(mb == 1))
            tA = te.cnt
            ve.op("tensor_mul", Nc[:, :, :], PS[:, 0, :].rearrange(
                "p (i q) -> p i q", i=4),
                mskA[:, 0:P].rearrange("p (o q) -> p o q", o=1
                                       ).broadcast_to([P, 4, P]),
                deps=[(te, tA), (gc, aux["g_msk"])])
            bank_ve[0] = ve.cnt
            ve.op("tensor_mul", NTc[:, :, :], PS[:, 1, :].rearrange(
                "p (i q) -> p i q", i=4),
                mskA[:, P:2 * P].rearrange("p (o q) -> p o q", o=1
                                           ).broadcast_to([P, 4, P]))
            bank_ve[1] = ve.cnt
            ve.op("tensor_add", Xb[:, g0:g0 + 4, :], Nc[:, :, :],
                  c["identb"][:, :].rearrange("p (o q) -> p o q", o=1
                                              ).broadcast_to([P, 4, P]),
                  deps=[(te, te.cnt)])
            ve.op("tensor_add", XTb[:, g0:g0 + 4, :], NTc[:, :, :],
                  c["identb"][:, :].rearrange("p (o q) -> p o q", o=1
                                              ).broadcast_to([P, 4, P]))
            vX = ve.cnt
            for lv in range(6):
                for ii in range(4):
                    te.mm(PS[:, 2, P * ii:P * (ii + 1)], NTc[:, ii, :],
                          Nc[:, ii, :], start=True, stop=True,
                          deps=[(ve, ve.cnt), (se, se.cnt), (ve, bank_ve[2]),
                                (se, bank_se[2])],
                          inc=(ii == 3))
                    te.mm(PS[:, 3, P * ii:P * (ii + 1)], Nc[:, ii, :],
                          NTc[:, ii, :], start=True, stop=True,
                          deps=[(ve, bank_ve[3]), (se, bank_se[3])],
                          inc=(ii == 3))
                tSq = te.cnt
                ve.op("tensor_copy", Nc[:, :, :], PS[:, 2, :].rearrange(
                    "p (i q) -> p i q", i=4), deps=[(te, tSq)])
                bank_ve[2] = ve.cnt
                ve.op("tensor_copy", NTc[:, :, :], PS[:, 3, :].rearrange(
                    "p (i q) -> p i q", i=4), deps=[(te, tSq)])
                bank_ve[3] = ve.cnt
                vsq = ve.cnt
                for ii in range(4):
                    te.mm(PS[:, 4, P * ii:P * (ii + 1)], NTc[:, ii, :],
                          Xb[:, g0 + ii, :], start=True, stop=False,
                          deps=[(ve, vsq), (se, se.cnt), (ve, bank_ve[4]),
                                (se, bank_se[4])])
                    te.mm(PS[:, 4, P * ii:P * (ii + 1)], c["identb"][:, :],
                          Xb[:, g0 + ii, :], start=False, stop=True,
                          inc=(ii == 3))
                    te.mm(PS[:, 5, P * ii:P * (ii + 1)], Xb[:, g0 + ii, :],
                          NTc[:, ii, :], start=True, stop=False,
                          deps=[(ve, vsq), (se, se.cnt), (ve, bank_ve[5]),
                                (se, bank_se[5])])
                    te.mm(PS[:, 5, P * ii:P * (ii + 1)], c["identb"][:, :],
                          XTb[:, g0 + ii, :], start=False, stop=True,
                          inc=(ii == 3))
                tL = te.cnt
                ve.op("tensor_copy", Xb[:, g0:g0 + 4, :],
                      PS[:, 4, :].rearrange("p (i q) -> p i q", i=4),
                      deps=[(te, tL)])
                bank_ve[4] = ve.cnt
                ve.op("tensor_copy", XTb[:, g0:g0 + 4, :],
                      PS[:, 5, :].rearrange("p (i q) -> p i q", i=4))
                bank_ve[5] = ve.cnt
        vXdone = ve.cnt

        kn2 = [rsb[:, 0:256], rsb[:, 256:512]]
        u2 = [rsb[:, 512:768], rsb[:, 768:1024]]
        wneg2 = [rsb[:, 1024:1280], rsb[:, 1280:1536]]
        S_bf2 = [rsb[:, 1536:2048].rearrange("p (b d) -> p b d", b=2),
                 mub[:, 0:512].rearrange("p (b d) -> p b d", b=2)]
        kb_cm2 = mub[:, 512:768]
        vb_cm2 = mub[:, 768:1024]
        ALT2 = [mub[:, 1024:1152], mub[:, 1152:1280]]
        ve.op("memset", S_bf2[1][:, :, :], 0.0, deps=[(te, te.cnt)])
        v_s2 = ve.cnt
        v_sbf = [v_s, v_s2]
        v_upre = [0, 0]
        v_wneg = [0, 0]
        v_alt = [0, 0]
        v_u = [0, 0]

        def scan_pre(jj):
            p = jj % 2
            csl = slice(P * jj, P * (jj + 1))
            for mb in range(2):
                te.mm(PS[:, 3, 256:384], Bk[:, mb, csl], Bq[:, mb, csl],
                      start=(mb == 0), stop=(mb == 1),
                      deps=[(ve, bank_ve[3]), (ve, v_kbvb)])
            for mb in range(2):
                te.mm(PS[:, 2, P * mb:P * (mb + 1)],
                      Bk[:, mb, csl], c["identb"][:, :],
                      start=True, stop=True,
                      deps=[(ve, bank_ve[2])])
                te.mm(PS[:, 2, 256 + P * mb:256 + P * (mb + 1)],
                      kb_fm[:, mb, csl], c["identb"][:, :],
                      start=True, stop=True, inc=(mb == 1))
                te.mm(PS[:, 3, P * mb:P * (mb + 1)],
                      vb_fm[:, mb, csl], c["identb"][:, :],
                      start=True, stop=True, inc=(mb == 1))
            t_t = te.cnt
            ve.op("tensor_copy", kn2[p][:, :], PS[:, 2, 0:256],
                  deps=[(te, t_t)])
            ve.op("tensor_copy", kb_cm2[:, :], PS[:, 2, 256:512])
            bank_ve[2] = ve.cnt
            ve.op("tensor_copy", vb_cm2[:, :], PS[:, 3, 0:256])
            ve.op("tensor_mul", ALT2[p][:, :], PS[:, 3, 256:384],
                  mskL[:, :], deps=[(gc, aux["g_msk"])])
            bank_ve[3] = ve.cnt
            v_alt[p] = ve.cnt
            v_evac1 = ve.cnt
            for mb in range(2):
                te.mm(PS[:, 4, P * mb:P * (mb + 1)],
                      kb_cm2[:, P * mb:P * (mb + 1)], XTb[:, jj, :],
                      start=True, stop=True,
                      deps=[(ve, v_evac1), (ve, bank_ve[4]), (ve, vXdone)],
                      inc=(mb == 1))
            t_w = te.cnt
            ve.op("tensor_scalar_mul", wneg2[p][:, :], PS[:, 4, 0:256],
                  -1.0, deps=[(te, t_w)])
            bank_ve[4] = ve.cnt
            v_wneg[p] = ve.cnt
            te.mm(PS[:, 6 + p, 0:256], XTb[:, jj, :], vb_cm2[:, :],
                  start=True, stop=False,
                  deps=[(ve, v_evac1), (ve, bank_ve[6 + p]), (ve, vXdone),
                        (se, bank_se[6 + p])])

        def scan_chain(jj):
            p = jj % 2
            for mb in range(2):
                te.mm(PS[:, 6 + p, 0:256],
                      wneg2[p][:, P * mb:P * (mb + 1)],
                      S_bf2[1 - p][:, mb, :], start=False, stop=(mb == 1),
                      deps=[(ve, v_wneg[p]), (ve, v_sbf[1 - p])],
                      inc=(mb == 1))
            t_u = te.cnt
            ve.op("tensor_copy", u2[p][:, :], PS[:, 6 + p, 0:256],
                  deps=[(te, t_u)])
            bank_ve[6 + p] = ve.cnt
            v_u[p] = ve.cnt
            for mb in range(2):
                te.mm(PS[:, 1, 256 * mb:256 * (mb + 1)],
                      kn2[p][:, P * mb:P * (mb + 1)], u2[p][:, :],
                      start=True, stop=True,
                      deps=[(ve, v_u[p]), (ve, bank_ve[1])],
                      inc=(mb == 1))
            t_s = te.cnt
            ve.op("tensor_add", S_bf2[p][:, :, :], S0[:, :, :],
                  PS[:, 1, :].rearrange("p (b d) -> p b d", b=2),
                  deps=[(te, t_s)])
            v_sbf[p] = ve.cnt
            ve.op("tensor_add", S0[:, :, :], S0[:, :, :],
                  PS[:, 1, :].rearrange("p (b d) -> p b d", b=2))
            bank_ve[1] = ve.cnt

        def scan_o(jj):
            p = jj % 2
            csl = slice(P * jj, P * (jj + 1))
            for dvb in range(2):
                for mb in range(2):
                    te.mm(PS[:, 0, P * dvb:P * (dvb + 1)],
                          S_bf2[1 - p][:, mb, P * dvb:P * (dvb + 1)],
                          Bq[:, mb, csl], start=(mb == 0), stop=False,
                          deps=[(ve, bank_ve[0]), (ve, v_sbf[1 - p]),
                                (ve, v_u[p]), (ve, v_alt[p])])
                te.mm(PS[:, 0, P * dvb:P * (dvb + 1)],
                      u2[p][:, P * dvb:P * (dvb + 1)], ALT2[p][:, :],
                      start=False, stop=True, inc=(dvb == 1))
            t_o = te.cnt
            ve.op("tensor_copy", delta[:, :, csl],
                  PS[:, 0, 0:256].rearrange("p (b q) -> p b q", b=2),
                  deps=[(te, t_o)])
            bank_ve[0] = ve.cnt

        scan_pre(0)
        scan_pre(1)
        for jj in range(16):
            scan_chain(jj)
            scan_o(jj)
            if jj + 2 < 16:
                scan_pre(jj + 2)
        v_delta = ve.cnt
        ar.pop()

        ar.push()
        W1q = ar.alloc3(NB, 256)
        stats16 = ar.alloc(T)
        p_tm = ar.alloc(64).rearrange("p (tt j) -> p tt j", tt=16)
        h1t = ar.alloc(512)
        W1B = aux["wsl_flat"][0:16, 0:1024]
        w1b_tok = ctx.qg.dma(W1B, dr["g_w1"][D:D + 16, :],
                             deps=[(te, te.cnt)])

        for mb in range(2):
            gb = 2 * h + mb
            ve.op("tensor_scalar_mul", Bq[:, mb, :], Bv[:, mb, 60:60 + T],
                  FIRS[:, gb, 0:1], deps=[(ve, v_delta), sm_tok])
            for j in range(1, 5):
                ve.op("scalar_tensor_tensor", Bq[:, mb, :],
                      Bv[:, mb, 60 + j:60 + j + T], FIRS[:, gb, j:j + 1],
                      Bq[:, mb, :], ALU.mult, ALU.add)
        v_fs = ve.cnt
        half = [[sq[:, 128 * jj:128 * (jj + 1)] for jj in range(8)],
                [sq[:, 1024 + 128 * jj:1024 + 128 * (jj + 1)]
                 for jj in range(8)]]
        for mb in range(2):
            gb = 2 * h + mb
            half_war = [te.cnt, te.cnt]
            for jg in range(8):
                par = jg % 2
                gdeps = [sm_tok, aux["firl"], (te, half_war[par]),
                         (ve, v_delta)]
                for jj in range(8):
                    j = 8 * jg + jj
                    gc.op("affine_select", half[par][jj],
                          FIRL[:, gb, j:j + 1].broadcast_to([P, P]),
                          [[1, P]], ALU.is_equal, 0.0, base=0,
                          channel_multiplier=-1, deps=gdeps)
                gcd = gc.cnt
                for jj in range(8):
                    j = 8 * jg + jj
                    for tc in range(4):
                        bank = 4 + tc
                        te.mm(PS[:, bank, :], half[par][jj],
                              Bv[:, mb, 1 + j + 512 * tc:
                                 1 + j + 512 * (tc + 1)],
                              start=(j == 0), stop=(j == 63),
                              deps=[(gc, gcd), (ve, bank_ve[bank]),
                                    (se, bank_se[bank])],
                              inc=(jj == 7 and tc == 3))
                half_war[par] = te.cnt
            tfl = te.cnt
            for tc in range(4):
                bank = 4 + tc
                ve.op("tensor_copy", Bk[:, mb, 512 * tc:512 * (tc + 1)],
                      PS[:, bank, :], deps=[(te, tfl)])
                bank_ve[bank] = ve.cnt
        v_fl = ve.cnt

        st_toks = []
        r_mean = rows[0:1, 0:T]
        r_var = rows[0:1, T:2 * T]
        r_abs = rows[32:33, 0:T]
        r_l2 = rows[32:33, T:2 * T]
        sqb = [sq, mub]
        st_row_war = []
        t3_prev = te.cnt
        for si, (src2, off) in enumerate(
                ((Bq, 0), (Bk, 0), (delta, 0), (Bv, 64))):
            sdep = [(ve, v_fl), (ve, v_fs)]
            for tc in range(4):
                for mb in range(2):
                    te.mm(PS[0:1, 4 + tc, :], c["onescolb"][:, :],
                          src2[:, mb, off + 512 * tc:off + 512 * (tc + 1)],
                          start=(mb == 0), stop=(mb == 1),
                          deps=sdep + [(ve, bank_ve[4 + tc]),
                                       (se, bank_se[4 + tc])],
                          inc=(mb == 1 and tc == 3))
            t1 = te.cnt
            for mb in range(2):
                se.op("activation", sqb[mb][:, :],
                      src2[:, mb, off:off + T], AF.Square,
                      deps=sdep + [(te, t3_prev)])
            s_sq = se.cnt
            for tc in range(4):
                for mb in range(2):
                    te.mm(PS[32:33, 4 + tc, :], c["onescolb"][:, :],
                          sqb[mb][:, 512 * tc:512 * (tc + 1)],
                          start=(mb == 0), stop=(mb == 1),
                          deps=[(se, s_sq)], inc=(mb == 1 and tc == 3))
            t2 = te.cnt
            for mb in range(2):
                se.op("activation", sqb[mb][:, :],
                      src2[:, mb, off:off + T], AF.Abs, deps=[(te, t2)])
            s_ab = se.cnt
            for tc in range(4):
                for mb in range(2):
                    te.mm(PS[64:65, 4 + tc, :], c["onescolb"][:, :],
                          sqb[mb][:, 512 * tc:512 * (tc + 1)],
                          start=(mb == 0), stop=(mb == 1),
                          deps=[(se, s_ab)], inc=(mb == 1 and tc == 3))
            t3 = te.cnt
            t3_prev = t3
            s1 = PS[0:1, 4:8, :].rearrange("p a b -> p (a b)")
            s2 = PS[32:33, 4:8, :].rearrange("p a b -> p (a b)")
            s3 = PS[64:65, 4:8, :].rearrange("p a b -> p (a b)")
            ve.op("tensor_scalar_mul", r_mean, s1, 1.0 / 256,
                  deps=[(te, t1)] + st_row_war)
            ve.op("tensor_mul", r_var, r_mean, r_mean, deps=[(te, t2)])
            ve.op("scalar_tensor_tensor", r_var, s2, 1.0 / 256, r_var,
                  ALU.mult, ALU.subtract)
            ve.op("tensor_scalar_mul", r_abs, s3, 1.0 / 256,
                  deps=[(te, t3)])
            v_rows = ve.cnt
            se.op("activation", r_l2, s2, AF.Sqrt,
                  deps=[(te, t2), (ve, v_rows)] + st_row_war)
            s_rows = se.cnt
            for tc in range(4):
                bank_ve[4 + tc] = v_rows
                bank_se[4 + tc] = s_rows
            for j, (rsrc, dep) in enumerate(((r_mean, (ve, v_rows)),
                                             (r_var, (ve, v_rows)),
                                             (r_abs, (ve, v_rows)),
                                             (r_l2, (se, s_rows)))):
                tok = qg.dma(stats16[4 * si + j:4 * si + j + 1, :],
                             rsrc, deps=[dep], pool="aux")
                st_toks.append(tok)
            st_row_war = st_toks[-4:]

        ve.op("tensor_add", BB[0:4, 5:6], BB[0:4, 4:5], BB[0:4, h:h + 1],
              deps=[sm_tok])
        v_bb = ve.cnt
        lg_fm = rows[0:4, T:2 * T]
        lg_war = [se.cnt, se.cnt, se.cnt, se.cnt]
        for qd in range(4):
            w1_ld = wload3(ctx, W1q, dr["g_w1"][0:D, 256 * qd:256 * (qd + 1)],
                           deps=[(te, te.cnt)])
            for fb2 in range(2):
                fblk = 2 * qd + fb2
                for kb in range(NB):
                    for tc in range(4):
                        te.mm(PS[:, tc, :],
                              W1q[:, kb, 128 * fb2:128 * (fb2 + 1)],
                              xn1[:, kb, 512 * tc:512 * (tc + 1)],
                              start=(kb == 0), stop=False,
                              deps=[w1_ld, w1b_tok, (se, bank_se[tc]),
                                    (ve, bank_ve[tc]), (ve, v_xn1)] + st_toks,
                              inc=False)
                for tc in range(4):
                    te.mm(PS[:, tc, :],
                          W1B[0:16, 128 * fblk:128 * (fblk + 1)],
                          stats16[0:16, 512 * tc:512 * (tc + 1)],
                          start=False, stop=True, inc=True)
                tcnt = te.cnt
                for tc in range(4):
                    se.op("activation", h1t[:, :], PS[:, tc, :], AF.Gelu,
                          bias=SM[:, fblk:fblk + 1],
                          deps=[(te, tcnt), (te, te.cnt)])
                    scnt = se.cnt
                    bank_se[tc] = scnt
                    te.mm(PS[0:4, 4 + tc, :], W2S[:, fblk, :], h1t[:, :],
                          start=(fblk == 0), stop=(fblk == 7),
                          deps=[(se, scnt), w2s_tok, (se, lg_war[tc]),
                                (ve, bank_ve[4 + tc])],
                          inc=True)
                    if fblk == 7:
                        se.op("activation", lg_fm[:, 512 * tc:512 * (tc + 1)],
                              PS[0:4, 4 + tc, :], AF.Copy,
                              deps=[(te, te.cnt)])
                        lg_war[tc] = se.cnt
                        bank_se[4 + tc] = se.cnt
        v_lg_se = se.cnt
        ve.op("tensor_scalar_add", lg_fm[:, :], lg_fm[:, :], BB[0:4, 5:6],
              deps=[(se, v_lg_se), (ve, v_bb)])
        v_lg = ve.cnt

        te.mm(PS[0:4, 2, 0:1], c["ones1f"][0:1, 0:4], SR[0:1, 8 + h:9 + h],
              start=True, stop=True,
              deps=[(ve, v_itemp), (ve, bank_ve[2])], inc=True)
        tid = te.cnt
        ve.op("tensor_scalar_mul", BB[0:4, 6:7], PS[0:4, 2, 0:1], 1.0,
              deps=[(te, tid)])
        ve.op("tensor_scalar_mul", idt4[:, :], c["identb"][0:4, 0:4],
              BB[0:4, 6:7])
        v_idt = ve.cnt
        bank_ve[2] = v_idt

        for tg in range(4):
            for k in range(4):
                tt = 4 * tg + k
                te.mm(PS[:, 3, 4 * k:4 * k + 4],
                      lg_fm[:, P * tt:P * (tt + 1)], idt4[:, :],
                      start=True, stop=True,
                      deps=[(ve, v_lg), (ve, v_idt), (ve, bank_ve[3])],
                      inc=(k == 3))
            tcnt = te.cnt
            g16 = PS[:, 3, 0:16].rearrange("p (a b) -> p a b", b=4)
            ve.op("tensor_reduce", stm[:, 0:4], g16, AX.X, ALU.max,
                  deps=[(te, tcnt)])
            ve.op("tensor_sub", stm[:, 16:32].rearrange(
                "p (a b) -> p a b", b=4), g16,
                stm[:, 0:4].rearrange("p (a b) -> p a b", b=1
                                      ).broadcast_to([P, 4, 4]))
            vsm = ve.cnt
            se.op("activation", stm[:, 16:32], stm[:, 16:32], AF.Exp,
                  deps=[(ve, vsm)])
            ssm = se.cnt
            bank_ve[3] = ve.cnt
            e16 = stm[:, 16:32].rearrange("p (a b) -> p a b", b=4)
            ve.op("tensor_reduce", stm[:, 0:4], e16, AX.X, ALU.add,
                  deps=[(se, ssm)])
            ve.op("reciprocal", stm[:, 4:8], stm[:, 0:4])
            ve.op("tensor_mul", e16, e16,
                  stm[:, 4:8].rearrange("p (a b) -> p a b", b=1
                                        ).broadcast_to([P, 4, 4]))
            ve.op("tensor_max", stm[:, 16:32], stm[:, 16:32], FLOOR[:, :],
                  deps=[(ve, v_floor)])
            ve.op("tensor_reduce", stm[:, 0:4], e16, AX.X, ALU.add)
            ve.op("reciprocal", stm[:, 4:8], stm[:, 0:4])
            ve.op("tensor_mul", p_tm[:, 4 * tg:4 * tg + 4, :], e16,
                  stm[:, 4:8].rearrange("p (a b) -> p a b", b=1
                                        ).broadcast_to([P, 4, 4]))
        v_ptm = ve.cnt

        prow4 = rows[0:4, 0:T]
        pr_se = [se.cnt, se.cnt]
        for tt in range(16):
            bk = 2 + (tt % 2)
            te.mm(PS[0:4, bk, 0:P],
                  p_tm[:, tt, :], c["identb"][:, :],
                  start=True, stop=True,
                  deps=[(ve, v_ptm), (se, pr_se[tt % 2]),
                        (ve, bank_ve[bk])],
                  inc=True)
            tcnt = te.cnt
            se.op("activation", prow4[:, P * tt:P * (tt + 1)],
                  PS[0:4, bk, 0:P], AF.Copy, deps=[(te, tcnt)])
            pr_se[tt % 2] = se.cnt
        s_prow = se.cnt
        bank_se[2] = se.cnt
        bank_se[3] = se.cnt
        te.mm(PS[:, 2, 0:1], c["ones1f"][:, :], SR[0:1, 12 + h:13 + h],
              start=True, stop=True, deps=[(ve, v_crl), (se, s_prow)],
              inc=True)
        tcnt = te.cnt
        ve.op("tensor_scalar_mul", ctx.fcol[:, 15:16], PS[:, 2, 0:1], 1.0,
              deps=[(te, tcnt)])
        v_crlc = ve.cnt
        bank_ve[2] = v_crlc

        tmp = h1t
        srcs = [(Bq, 0), (Bk, 0), (delta, 0), (Bv, 64)]
        comb_war = ve.cnt
        for j in range(4):
            for tc in range(4):
                bank = j % 2
                te.mm(PS[:, bank, :], self_j[0:4, P * j:P * (j + 1)],
                      prow4[:, 512 * tc:512 * (tc + 1)],
                      start=True, stop=True,
                      deps=[(se, s_prow), (ve, bank_ve[bank]),
                            (gc, aux["g_msk"])], inc=True)
                tcnt = te.cnt
                src2, off = srcs[j]
                for mb in range(2):
                    if j == 0:
                        ve.op("tensor_mul",
                              o_h[:, mb, 512 * tc:512 * (tc + 1)],
                              src2[:, mb, off + 512 * tc:off + 512 * (tc + 1)],
                              PS[:, bank, :], deps=[(te, tcnt), (ve, comb_war)])
                    else:
                        ve.op("tensor_mul", tmp[:, :],
                              src2[:, mb, off + 512 * tc:off + 512 * (tc + 1)],
                              PS[:, bank, :], deps=[(te, tcnt)])
                        ve.op("tensor_add",
                              o_h[:, mb, 512 * tc:512 * (tc + 1)],
                              o_h[:, mb, 512 * tc:512 * (tc + 1)], tmp[:, :])
                bank_ve[bank] = ve.cnt
        for mb in range(2):
            ve.op("scalar_tensor_tensor", o_h[:, mb, :], Bq[:, mb, :],
                  ctx.fcol[:, 15:16], o_h[:, mb, :], ALU.mult, ALU.add,
                  deps=[(ve, v_crlc)])
        v_comb = ve.cnt

        sqb2 = [sq, mub]
        for mb in range(2):
            se.op("activation", sqb2[mb][:, :], o_h[:, mb, :],
                  AF.Square, deps=[(ve, v_comb), (te, te.cnt)])
        s_rs = se.cnt
        for tt in range(16):
            for mb in range(2):
                te.mm(PS[:, 6, tt:tt + 1], sqb2[mb][:, P * tt:P * (tt + 1)],
                      c["onescolb"][:, :], start=(mb == 0), stop=(mb == 1),
                      deps=[(se, s_rs), (ve, bank_ve[6])],
                      inc=(tt == 15 and mb == 1))
        trs = te.cnt
        ve.op("tensor_scalar_mul", stm[:, 0:16], PS[:, 6, 0:16], 1.0 / 256,
              deps=[(te, trs)])
        bank_ve[6] = ve.cnt
        vv = ve.cnt
        se.op("activation", stm[:, 16:32], stm[:, 0:16], AF.Sqrt,
              bias=c["cpool"][:, 0:1], deps=[(ve, vv)])
        srs = se.cnt
        ve.op("reciprocal", stm[:, 0:16], stm[:, 16:32], deps=[(se, srs)])
        v_rstd = ve.cnt
        te.mm(PS[0:16, 2, 0:P], stm[:, 0:16], c["identb"][:, :],
              start=True, stop=True,
              deps=[(ve, v_rstd), (ve, bank_ve[2])], inc=True)
        tcnt = te.cnt
        ve.op("tensor_copy", st16[:, :], PS[0:16, 2, 0:P],
              deps=[(te, tcnt)] + st16_war)
        bank_ve[2] = ve.cnt
        v16 = ve.cnt
        rrow = rows[0:1, 0:T]
        rs_tok = qg.dma(rrow.rearrange("o (a b) -> o a b", a=16), st16[:, :],
                        deps=[(ve, v16)], pool="aux")
        st16_war = [rs_tok]
        for tc in range(4):
            te.mm(PS[:, tc, :], c["ones1b"][:, :],
                  rrow[0:1, 512 * tc:512 * (tc + 1)],
                  start=True, stop=True,
                  deps=[rs_tok, (ve, bank_ve[tc])], inc=True)
        tbc = te.cnt
        for tc in range(4):
            ve.op("tensor_copy", mub[:, 512 * tc:512 * (tc + 1)],
                  PS[:, tc, :], deps=[(te, tbc)])
            bank_ve[tc] = ve.cnt
        for mb in range(2):
            ve.op("scalar_tensor_tensor", o_h[:, mb, :], o_h[:, mb, :],
                  SM[:, 8 + mb:9 + mb], mub[:, :], ALU.mult, ALU.mult)
        v_on = ve.cnt
        s_on = se.cnt

        wo_war = te.cnt
        for quarter in range(8):
            wo_ld = wload3(ctx, wsl[:, 0:2, 0:P],
                           dr["wo1"][256 * h:256 * (h + 1),
                                     P * quarter:P * (quarter + 1)],
                           deps=[(te, wo_war)])
            mbk = quarter
            for tc in range(4):
                bank = tc % 2
                for kb in range(2):
                    te.mm(PS[:, bank, :], wsl[:, kb, 0:P],
                          o_h[:, kb, 512 * tc:512 * (tc + 1)],
                          start=(kb == 0), stop=(kb == 1),
                          deps=[wo_ld, (ve, v_on), (se, s_on),
                                (ve, bank_ve[bank]), (se, bank_se[bank])],
                          inc=(kb == 1))
                tcnt = te.cnt
                ve.op("tensor_add", RES[:, mbk, 512 * tc:512 * (tc + 1)],
                      RES[:, mbk, 512 * tc:512 * (tc + 1)], PS[:, bank, :],
                      deps=[(te, tcnt)])
                bank_ve[bank] = ve.cnt
            wo_war = te.cnt
        ar.pop()
        ar.pop()



def host_layout(inputs, core):
    g = lambda k: np.asarray(inputs[k])
    fmv = lambda v: v.reshape(NB, P).T
    m = {}
    m["xT"] = g("L0_attn_x")[core].T
    wqkvT = g("L0_attn_W_qkv").T.reshape(D, 16, 3, 64)
    m["w_qT"] = wqkvT[:, :, 0, :].reshape(D, D)
    m["w_kT"] = wqkvT[:, :, 1, :].reshape(D, D)
    m["w_vT"] = wqkvT[:, :, 2, :].reshape(D, D)
    bqkv = g("L0_attn_b_qkv").reshape(16, 3, 64)
    m["bq_a"] = fmv(bqkv[:, 0, :].reshape(-1))
    m["bk_a"] = fmv(bqkv[:, 1, :].reshape(-1))
    m["bv_a"] = bqkv[:, 2, :].reshape(-1, 1)
    m["w_oT"] = g("L0_attn_W_o").T
    m["b_o"] = fmv(g("L0_attn_b_o"))
    m["ln1_g"] = fmv(g("L0_ffn_ln1_g"))
    m["ln1_b"] = fmv(g("L0_ffn_ln1_b"))
    m["ln2_g"] = fmv(g("L0_ffn_ln2_g"))
    m["ln2_b"] = fmv(g("L0_ffn_ln2_b"))
    m["wq0"] = g("L0_ffn_Wq"); m["bq0"] = fmv(g("L0_ffn_bq"))
    m["wk0"] = g("L0_ffn_Wk"); m["bk0"] = fmv(g("L0_ffn_bk"))
    m["wv0"] = g("L0_ffn_Wv"); m["bv0"] = g("L0_ffn_bv").reshape(-1, 1)
    m["wp0"] = g("L0_ffn_Wp"); m["bp0"] = fmv(g("L0_ffn_bp"))
    m["gate_w0"] = g("L0_ffn_gate_W")
    m["gate_b0"] = g("L0_ffn_gate_b").reshape(1, -1)
    m["exp_w1"] = g("L0_ffn_exp_W1")
    m["exp_b1"] = g("L0_ffn_exp_b1").reshape(8, 32, P).transpose(0, 2, 1)
    m["exp_w2"] = g("L0_ffn_exp_W2"); m["exp_b2"] = g("L0_ffn_exp_b2")
    m["wq1"] = g("L1_attn_Wq"); m["wk1"] = g("L1_attn_Wk")
    m["wv1"] = g("L1_attn_Wv"); m["wb1"] = g("L1_attn_Wb")
    sw = lambda v, k: v.reshape(NB, P, k).transpose(1, 0, 2)
    m["conv_q"] = sw(g("L1_attn_conv_q"), 4)
    m["conv_k"] = sw(g("L1_attn_conv_k"), 4)
    m["conv_v"] = sw(g("L1_attn_conv_v"), 4)
    m["fir_long"] = sw(g("L1_attn_fir_long").reshape(D, 64), 64)
    m["fir_short"] = sw(g("L1_attn_fir_short").reshape(D, 5), 5)
    m["g_w1"] = g("L1_attn_W1"); m["g_b1"] = fmv(g("L1_attn_b1"))
    m["g_w2"] = g("L1_attn_W2"); m["g_b2"] = g("L1_attn_b2").reshape(-1, 1)
    m["log_temp"] = g("L1_attn_log_temp").reshape(1, -1)
    m["base_bias"] = g("L1_attn_base_bias").T
    m["crl"] = g("L1_attn_conv_residual_logit").reshape(1, -1)
    m["rms_w"] = g("L1_attn_rms_w").reshape(2, P).T
    m["wo1"] = g("L1_attn_Wo")
    m["gate_w1"] = g("L1_ffn_gate_w").T
    m["m_w1"] = g("L1_ffn_w1")
    m["m_b1"] = g("L1_ffn_b1").reshape(8, 8, P).transpose(0, 2, 1)
    m["m_w2"] = g("L1_ffn_w2"); m["m_b2"] = g("L1_ffn_b2")
    return {k: np.ascontiguousarray(v, dtype=np.float32) for k, v in m.items()}


def kernel(**inputs):
    nc, dr = build_program(layers=(0, 1))
    in_names = set(dr.keys())
    maps = []
    for core in range(8):
        full = host_layout(inputs, core)
        maps.append({k: v for k, v in full.items() if k in in_names})
    res = run_bass_kernel_spmd(nc, maps, core_ids=list(range(8)))
    out = np.stack([np.ascontiguousarray(r["outT"].T) for r in res.results])
    return out.astype(np.float32)



def wload3(ctx, dst3, dram_ap, deps=()):
    return ctx.qg.dma(dst3, dram_ap.rearrange("(b p) f -> p b f", p=P), deps=deps)


def emit_mha(ctx, x3d, x_dep, w, causal, evac_out, name, NG=2,
             spill_o=False):
    te, ve, se, gc, qg = ctx.te, ctx.ve, ctx.se, ctx.gc, ctx.qg
    c = ctx.consts
    PS = ctx.PS
    rows = ctx.ln_scr["rows"]
    fcol = ctx.fcol

    ar = ctx.arena
    ar.push()
    x_dep = list(x_dep) + [(ctx.se, ctx.se.cnt), (ctx.ve, ctx.ve.cnt),
                           (ctx.gc, ctx.gc.cnt)]
    nbg = 1
    if spill_o:
        o_scr = ctx.nc.dram_tensor(name + "_oscr", [D, T], BF16,
                                   kind="Internal").ap()
        ostg = [ar.alloc(512) for _ in range(2)]
        ostg_tok = [None, None]
        ostg_all = {}
        O3 = None
    else:
        O3 = ar.alloc3(NB, T)
    qf = ar.alloc3(nbg, T)
    kf = ar.alloc3(nbg, T)
    vtm = ar.alloc(16 * NG * 65).rearrange("p (tt h e) -> p tt h e", tt=16, h=NG)
    sq1 = ctx.ln_scr["sq"]
    sq2 = ctx.ln_scr["mub"]
    rb = ctx.ln_scr["rsb"][:, 0:512]
    cmcol = ctx.cmcolb
    wqs = ar.alloc3(NB, NG * 64)
    wks = ar.alloc3(NB, NG * 64)
    wvs = ar.alloc3(NB, NG * 64)
    wos = wvs

    ve.op("memset", vtm[:, :, :, 64:65], 1.0)

    assert NG == 2
    n_groups = 16 // NG
    pt3 = [ctx.ln_scr["rsb"][:, 512:1536], ctx.ln_scr["mub"][:, 0:1024],
           sq1[:, 0:1024]]
    pt_war = [0, 0, 0]
    st_war = [0, 0]
    se0 = se.cnt
    bank_war = {b: 0 for b in range(8)}
    bank_wars = {b: 0 for b in range(8)}
    wbuf_war = {"q": 0, "k": 0, "v": 0}

    for g in range(n_groups):
        cs = slice(g * NG * 64, (g + 1) * NG * 64)
        wq_ld = wload3(ctx, wqs, w["wq"][:, cs], deps=[(te, wbuf_war["q"])])
        wk_ld = wload3(ctx, wks, w["wk"][:, cs], deps=[(te, wbuf_war["k"])])
        wv_ld = wload3(ctx, wvs, w["wv"][:, cs], deps=[(te, wbuf_war["v"])])

        for which, wt, dstf, ld, bfm in (("q", wqs, qf, wq_ld, w["bqf"]),
                                         ("k", wks, kf, wk_ld, w["bkf"])):
            for kb in range(NB):
                for tc in range(4):
                    te.mm(PS[:, 4 + tc, :], wt[:, kb, 0:P],
                          x3d[:, kb, 512 * tc:512 * (tc + 1)],
                          start=(kb == 0), stop=(kb == NB - 1),
                          deps=[ld, (ve, bank_war[4 + tc]),
                                (se, bank_wars[4 + tc])] + x_dep,
                          inc=(kb == NB - 1 and tc == 3))
            tcnt = te.cnt
            for tc in range(4):
                ve.op("tensor_scalar_add",
                      dstf[:, 0, 512 * tc:512 * (tc + 1)],
                      PS[:, 4 + tc, :], bfm[:, g:g + 1],
                      deps=[(te, tcnt)])
                bank_war[4 + tc] = ve.cnt
            wbuf_war[which] = te.cnt
        vqk = ve.cnt

        for tt in range(16):
            bank = 2 + (tt % 2)
            te.mm(PS[:, bank, 0:NG * 64], c["ones1b"][0:1, :],
                  w["bvr"][0:1, cs], start=True, stop=False,
                  deps=[wv_ld, (ve, bank_war[bank]), (se, st_war[1]),
                        (ve, c["ready_ve"])] + x_dep)
            for kb in range(NB):
                te.mm(PS[:, bank, 0:NG * 64], x3d[:, kb, P * tt:P * (tt + 1)],
                      wvs[:, kb, :], start=False, stop=(kb == NB - 1),
                      inc=(kb == NB - 1))
            tcnt = te.cnt
            for hh in range(NG):
                ve.op("tensor_copy", vtm[:, tt, hh, 0:64],
                      PS[:, bank, hh * 64:hh * 64 + 64], deps=[(te, tcnt)])
            bank_war[bank] = ve.cnt
        wbuf_war["v"] = te.cnt

        sqs = {}
        for j, (srcf, dsts) in enumerate(((qf, sq1), (kf, sq2))):
            se.op("activation", dsts[:, :], srcf[:, 0, :], AF.Square,
                  deps=[(ve, vqk), (te, pt_war[1]), (te, pt_war[2])])
            sqs[j] = se.cnt
        for j, dsts in ((0, sq1), (1, sq2)):
            for tmb in range(16):
                te.mm(PS[:, 6 + j, tmb:tmb + 1],
                      dsts[:, P * tmb:P * (tmb + 1)],
                      c["onescolb"][:, :], start=True, stop=True,
                      deps=[(se, sqs[j]), (ve, bank_war[6 + j])],
                      inc=(tmb == 15))
        tq = te.cnt
        for j in range(2):
            ve.op("reduce_max", cmcol[:, j:j + 1], PS[:, 6 + j, 0:16], AX.X,
                  deps=[(te, tq)])
            bank_war[6 + j] = ve.cnt
        vc = ve.cnt
        for j in range(2):
            te.mm(PS[0:1, 6 + j, 0:128], cmcol[:, j:j + 1], c["identb"][:],
                  start=True, stop=True,
                  deps=[(ve, vc), (ve, bank_war[6 + j]), (gc, c["ready_gc"])],
                  inc=True)
        tcnt = te.cnt
        for j in range(2):
            ve.op("reduce_max", rows[0:1, 1500 + j:1501 + j],
                  PS[0:1, 6 + j, 0:128], AX.X, deps=[(te, tcnt)])
            bank_war[6 + j] = ve.cnt
        vmx = ve.cnt
        se.op("activation", rows[0:1, 1500:1502], rows[0:1, 1500:1502],
              AF.Sqrt, deps=[(ve, vmx)])
        ssq = se.cnt
        ve.op("tensor_mul", rows[0:1, 1502:1503], rows[0:1, 1500:1501],
              rows[0:1, 1501:1502], deps=[(se, ssq)])
        vprod = ve.cnt
        te.mm(PS[:, 7, 0:1], c["ones1b"][:], rows[0:1, 1502:1503],
              start=True, stop=True,
              deps=[(ve, vprod), (ve, bank_war[7])], inc=True)
        tcnt = te.cnt
        ve.op("tensor_scalar_mul", fcol[:, 2 * g:2 * g + 1], PS[:, 7, 0:1],
              -0.125, deps=[(te, tcnt), (se, se0)])
        bank_war[7] = ve.cnt
        vcb = ve.cnt

        for qc in range(4):
            kt_hi = 16 if not causal else 4 * (qc + 1)
            pv_depq = {}
            for kt in range(kt_hi + 2):
                if kt < kt_hi:
                    par = kt % 3
                    sb = kt % 2
                    for hh in range(NG):
                        base = 64 * hh
                        bank = 2 * sb + hh
                        te.mm(PS[:, bank, :],
                              kf[base:base + 64, 0, P * kt:P * (kt + 1)],
                              qf[base:base + 64, 0, 512 * qc:512 * (qc + 1)],
                              start=True, stop=True,
                              deps=[(ve, vcb), (se, st_war[sb]),
                                    (ve, bank_war[bank])],
                              inc=True)
                    tcnt = te.cnt
                    se.op("activation", pt3[par][:, :],
                          PS[:, 2 * sb:2 * sb + 2, :].rearrange(
                              "p a b -> p (a b)"),
                          AF.Exp, bias=fcol[:, 2 * g:2 * g + 1],
                          scale=c["cpool"][:, 2:3],
                          deps=[(te, tcnt), (te, pt_war[par])])
                    scnt = se.cnt
                    st_war[sb] = scnt
                    for hh in range(NG):
                        dep_pv = (se, scnt)
                        if causal and kt >= 4 * qc:
                            gc.op("affine_select",
                                  pt3[par][:, 512 * hh:512 * (hh + 1)],
                                  pt3[par][:, 512 * hh:512 * (hh + 1)],
                                  [[1, 512]], ALU.is_ge,
                                  0.0, base=qc * 512 - kt * P,
                                  channel_multiplier=-1, deps=[(se, scnt)])
                            dep_pv = (gc, gc.cnt)
                        pv_depq.setdefault(kt, {})[hh] = dep_pv
                if kt >= 2:
                    kp = kt - 2
                    parp = kp % 3
                    for hh in range(NG):
                        te.mm(PS[0:65, 6 + hh, 0:512], vtm[:, kp, hh, :],
                              pt3[parp][:, 512 * hh:512 * (hh + 1)],
                              start=(kp == 0), stop=(kp == kt_hi - 1),
                              deps=[pv_depq[kp][hh],
                                    (ve, bank_war[6 + hh])],
                              inc=(kp == kt_hi - 1))
                        pt_war[parp] = te.cnt
            for hh in range(NG):
                h = g * NG + hh
                oblk, obase = h // 2, 64 * (h % 2)
                tpv = te.cnt
                ve.op("tensor_copy", rows[0:1, 512 * hh:512 * (hh + 1)],
                      PS[64:65, 6 + hh, :], deps=[(te, tpv)])
                vr_ = ve.cnt
                te.mm(PS[0:64, 4 + hh, :], c["ones1b"][0:1, 0:64],
                      rows[0:1, 512 * hh:512 * (hh + 1)],
                      start=True, stop=True,
                      deps=[(ve, vr_), (ve, bank_war[4 + hh])], inc=True)
                tcnt = te.cnt
                ve.op("reciprocal", rb[0:64, :], PS[0:64, 4 + hh, :],
                      deps=[(te, tcnt)])
                bank_war[4 + hh] = ve.cnt
                if spill_o:
                    par = (4 * h + qc) % 2
                    dep_st = [ostg_tok[par]] if ostg_tok[par] else []
                    ve.op("tensor_mul", ostg[par][0:64, :],
                          PS[0:64, 6 + hh, 0:512], rb[0:64, :], deps=dep_st)
                    ostg_tok[par] = ctx.qs.dma(
                        o_scr[64 * h:64 * h + 64, 512 * qc:512 * (qc + 1)],
                        ostg[par][0:64, :], deps=[(ve, ve.cnt)])
                    ostg_all[ostg_tok[par].sem.name] = ostg_tok[par]
                else:
                    ve.op("tensor_mul",
                          O3[obase:obase + 64, oblk, 512 * qc:512 * (qc + 1)],
                          PS[0:64, 6 + hh, 0:512], rb[0:64, :])
                bank_war[6 + hh] = ve.cnt
    vO = ve.cnt

    assert spill_o
    ord_toks = list(ostg_all.values())
    wo_buf = ar.alloc3(NB, 512)
    Ost = [ar.alloc3(NB, 512) for _ in range(2)]
    ost_tok = [None, None]
    ost_war = [0, 0]
    wo_war = wbuf_war["v"]
    for half in range(2):
        wo_ld = wload3(ctx, wo_buf,
                       w["wo"][:, 512 * half:512 * (half + 1)],
                       deps=[(te, wo_war)])
        for tc in range(4):
            par = tc % 2
            ost_tok[par] = ctx.qs.dma(
                Ost[par][:, :, :],
                o_scr[:, 512 * tc:512 * (tc + 1)].rearrange(
                    "(b p) t -> p b t", p=P),
                deps=[(te, ost_war[par])] + ord_toks)
            for q2 in range(4):
                quarter = 4 * half + q2
                bank = 4 + q2
                for kb in range(NB):
                    te.mm(PS[:, bank, :], wo_buf[:, kb, P * q2:P * (q2 + 1)],
                          Ost[par][:, kb, :],
                          start=(kb == 0), stop=(kb == NB - 1),
                          deps=[wo_ld, ost_tok[par], (ve, bank_war[bank]),
                                (se, bank_wars[bank])],
                          inc=(kb == NB - 1))
                evac_out(PS[:, bank, :], quarter, tc, te.cnt)
                bank_war[bank] = ve.cnt
            ost_war[par] = te.cnt
        wo_war = te.cnt
    ar.pop()
    return {"ve": ve.cnt, "te": te.cnt}



def emit_moe(ctx, x3d, x_dep, w1_d, b1_d, w2_d, b2_d, gw_d, gb_d, H,
             softmax_then_topk, w1bufs, res_dep, name):
    te, ve, se, gc, qs, qg = ctx.te, ctx.ve, ctx.se, ctx.gc, ctx.qs, ctx.qg
    c = ctx.consts
    PS = ctx.PS
    RES = ctx.RES
    x_dep = list(x_dep) + [(ctx.se, ctx.se.cnt), (ctx.ve, ctx.ve.cnt),
                           (ctx.gc, ctx.gc.cnt), (ctx.ve, res_dep)]
    NFC = H // 1024
    ar = ctx.arena
    ar.push()
    macc = ar.alloc(16 * 1024).rearrange("p (tt d) -> p tt d", tt=16)
    sq3 = ctx.ln_scr["sq"].rearrange("p (b t) -> p b t", b=4)
    mub3 = ctx.ln_scr["mub"].rearrange("p (b t) -> p b t", b=4)
    midv = lambda fb: (sq3 if fb % 2 == 0 else mub3)[:, fb // 2, :]
    w2buf = ar.alloc3(NB, 1024)
    gws = ar.alloc3(NB, 8)
    gbr = ar.alloc(32)
    b2r = ctx.rowbuf
    if not hasattr(ctx, "moe_sb"):
        ctx.moe_sb = dict(
            iota8=ctx.sb("moe_iota", [P, 8], F32),
            wf=ctx.sb("moe_wf", [P, 16, 8], F32),
            gsc=ctx.sb("moe_gsc", [P, 20], F32),
            i8=ctx.sb("moe_i8", [P, 8], U32),
            b1f=ctx.sb("moe_b1f", [P, 32], F32))
    iota8 = ctx.moe_sb["iota8"]
    wf = ctx.moe_sb["wf"]
    gsc = ctx.moe_sb["gsc"]
    i8 = ctx.moe_sb["i8"]
    b1f = ctx.moe_sb["b1f"]

    gc.op("iota", iota8[:, :], [[1, 8]], channel_multiplier=0,
          allow_small_or_imprecise_dtypes=True)
    g_iota = gc.cnt

    gw_ld = ctx.qg.dma(gws, gw_d.rearrange("(b p) f -> p b f", p=P),
                       pool="persist")
    gb_tok = gw_ld
    if gb_d is not None:
        gb_tok = ctx.qg.dma(gbr[0:1, 0:8], gb_d, pool="persist")

    gate_war = []
    for tt in range(16):
        if gb_d is not None:
            te.mm(PS[:, 6, 0:8], c["ones1b"][0:1, :], gbr[0:1, 0:8],
                  start=True, stop=False,
                  deps=[gw_ld, gb_tok, (ve, c["ready_ve"])] + x_dep + gate_war)
            first = False
        else:
            first = True
        for kb in range(NB):
            te.mm(PS[:, 6, 0:8], x3d[:, kb, P * tt:P * (tt + 1)],
                  gws[:, kb, :], start=(first and kb == 0),
                  stop=(kb == NB - 1),
                  deps=[gw_ld, gb_tok, (ve, c["ready_ve"])] + x_dep + gate_war,
                  inc=(kb == NB - 1))
        tcnt = te.cnt
        lg = gsc[:, 0:8]
        if softmax_then_topk:
            ve.op("reduce_max", gsc[:, 8:9], PS[:, 6, 0:8], AX.X,
                  negate=True, deps=[(te, tcnt)])
            se.op("activation", lg, PS[:, 6, 0:8], AF.Exp,
                  bias=gsc[:, 8:9], accum_out=gsc[:, 9:10],
                  deps=[(ve, ve.cnt), (te, tcnt)])
            ve.op("reciprocal", gsc[:, 10:11], gsc[:, 9:10],
                  deps=[(se, se.cnt)])
            ve.op("tensor_scalar_mul", lg, lg, gsc[:, 10:11])
        else:
            ve.op("tensor_copy", lg, PS[:, 6, 0:8], deps=[(te, tcnt)])
        vlg = ve.cnt
        gate_war = [(ve, ve.cnt), (se, se.cnt)]
        ve.op("max", gsc[:, 12:20], lg)
        ve.op("max_index", i8[:, :], gsc[:, 12:20], lg)
        ve.op("tensor_copy", gsc[:, 8:10], i8[:, 0:2])
        if softmax_then_topk:
            ve.op("tensor_copy", gsc[:, 10:11], gsc[:, 12:13])
            ve.op("tensor_copy", gsc[:, 11:12], gsc[:, 13:14])
            vw = ve.cnt
        else:
            ve.op("tensor_sub", gsc[:, 10:11], gsc[:, 12:13], gsc[:, 13:14])
            se.op("activation", gsc[:, 10:11], gsc[:, 10:11], AF.Sigmoid,
                  deps=[(ve, ve.cnt)])
            ve.op("tensor_scalar", gsc[:, 11:12], gsc[:, 10:11], -1.0, 1.0,
                  ALU.mult, ALU.add, deps=[(se, se.cnt)])
            vw = ve.cnt
        ve.op("scalar_tensor_tensor", wf[:, tt, :], iota8[:, :],
              gsc[:, 8:9], gsc[:, 10:11].broadcast_to([P, 8]),
              ALU.is_equal, ALU.mult, deps=[(gc, g_iota), (ve, vw)])
        ve.op("scalar_tensor_tensor", gsc[:, 12:20], iota8[:, :],
              gsc[:, 9:10], gsc[:, 11:12].broadcast_to([P, 8]),
              ALU.is_equal, ALU.mult)
        ve.op("tensor_add", wf[:, tt, :], wf[:, tt, :], gsc[:, 12:20])
    v_wf = ve.cnt
    bank_war = {b: 0 for b in range(8)}
    bank_war_se = {b: 0 for b in range(8)}
    bank_war[6] = v_wf

    w1_war = [te.cnt, te.cnt]
    w2_war = te.cnt
    b_war = se.cnt
    mid_se = {fb: 0 for fb in range(NB)}
    for e in range(8):
        b1_ld = qs.dma(b1f[:, 0:H // P], b1_d[e], deps=[(se, b_war)])
        b2_ld = qg.dma(b2r[0:1, 0:D], b2_d[e:e + 1, :], deps=[(te, w2_war)])
        for fc in range(NFC):
            wbi = (e * NFC + fc) % 2
            wb = w1bufs[wbi]
            w1_ld = wload3(ctx, wb, w1_d[e, :, 1024 * fc:1024 * (fc + 1)],
                           deps=[(te, w1_war[wbi])])
            w2_ld = wload3(ctx, w2buf,
                           w2_d[e, 1024 * fc:1024 * (fc + 1), :],
                           deps=[(te, w2_war)])
            for ch in range(4):
                t0 = 512 * ch
                for fb in range(NB):
                    bank = fb % 4
                    for kb in range(NB):
                        te.mm(PS[:, bank, :],
                              wb[:, kb, P * fb:P * (fb + 1)],
                              x3d[:, kb, t0:t0 + 512],
                              start=(kb == 0), stop=(kb == NB - 1),
                              deps=[w1_ld, (se, bank_war_se[bank])] + x_dep,
                              inc=(kb == NB - 1))
                    tcnt = te.cnt
                    se.op("activation", midv(fb)[:, :],
                          PS[:, bank, :], AF.Gelu,
                          bias=b1f[:, 8 * fc + fb:8 * fc + fb + 1],
                          deps=[(te, tcnt), b1_ld])
                    bank_war_se[bank] = se.cnt
                    mid_se[fb] = se.cnt
                for ts in range(4):
                    tt = 4 * ch + ts
                    for n in range(2):
                        ybank = 4 + 2 * (ts % 2) + n
                        if fc == 0:
                            te.mm(PS[:, ybank, :], c["ones1b"][0:1, :],
                                  b2r[0:1, 512 * n:512 * (n + 1)],
                                  start=True, stop=False,
                                  deps=[b2_ld, (ve, bank_war[ybank])])
                    for fb in range(NB):
                        for n in range(2):
                            ybank = 4 + 2 * (ts % 2) + n
                            te.mm(PS[:, ybank, :],
                                  midv(fb)[:, P * ts:P * (ts + 1)],
                                  w2buf[:, fb, 512 * n:512 * (n + 1)],
                                  start=(fc != 0 and fb == 0),
                                  stop=(fb == NB - 1),
                                  deps=[w2_ld, (se, mid_se[fb]),
                                        (ve, bank_war[ybank])],
                                  inc=(fb == NB - 1 and n == 1))
                    tcnt = te.cnt
                    for n in range(2):
                        ybank = 4 + 2 * (ts % 2) + n
                        dst = macc[:, tt, 512 * n:512 * (n + 1)]
                        if e == 0 and fc == 0:
                            ve.op("tensor_scalar_mul", dst,
                                  PS[:, ybank, :], wf[:, tt, e:e + 1],
                                  deps=[(te, tcnt), (ve, v_wf)])
                        else:
                            ve.op("scalar_tensor_tensor", dst,
                                  PS[:, ybank, :], wf[:, tt, e:e + 1],
                                  dst, ALU.mult, ALU.add,
                                  deps=[(te, tcnt)])
                        bank_war[ybank] = ve.cnt
            w1_war[wbi] = te.cnt
            w2_war = te.cnt
        b_war = se.cnt
    v_macc = ve.cnt

    for tt in range(16):
        for db in range(NB):
            bank = (tt * NB + db) % 2
            te.mm(PS[:, bank, 0:128], macc[:, tt, P * db:P * (db + 1)],
                  c["identb"][:], start=True, stop=True,
                  deps=[(ve, v_macc), (se, bank_war_se[bank]),
                        (ve, bank_war[bank]), (gc, c["ready_gc"])],
                  inc=True)
            tcnt = te.cnt
            ve.op("tensor_add", RES[:, db, P * tt:P * (tt + 1)],
                  RES[:, db, P * tt:P * (tt + 1)], PS[:, bank, 0:128],
                  deps=[(te, tcnt), (ve, res_dep)])
            bank_war[bank] = ve.cnt
    ar.pop()
    return ve.cnt


def run_layer1_build(taps=()):
    return build_program(layers=(1,), taps=taps)
